# version 04
# baseline (speedup 1.0000x reference)
import math
from contextlib import ExitStack

import numpy as np
import ml_dtypes

import concourse.bass as bass
import concourse.mybir as mybir
from concourse.bass_utils import run_bass_kernel_spmd

F32 = mybir.dt.float32
I32 = mybir.dt.int32
BF16 = mybir.dt.bfloat16
AF = mybir.ActivationFunctionType
ALU = mybir.AluOpType
AX = mybir.AxisListType

P = 128
D = 1024
FF = 4096
NSTATE = 16
B = 2
L = 8192
NCORES = 8
TC = B * L // NCORES
DBLK = D // P
FBLK = FF // P
TF = 512
NT = TC // TF
TP2 = 2048
EPS = 1e-5

SERIALIZE = True


def _gelu64(x):
    x = np.asarray(x, np.float64)
    return 0.5 * x * (1.0 + np.vectorize(math.erf)(x / math.sqrt(2.0)))

def _build_pwl():
    X = np.linspace(-4.0, 4.0, 33)
    Y = _gelu64(X)
    j = np.arange(17)
    qv = 256.0 * Y[16 - j]
    g = (qv[1:] - qv[:-1]) / 64.0
    g0 = g[0]
    dj = g[1:] - g[:-1]
    t = np.arange(-40000, 40001, dtype=np.float64)
    uc = np.minimum(np.abs(t), 1024.0)
    q = g0 * uc
    for k in range(1, 16):
        q = q + dj[k - 1] * np.maximum(uc - 64.0 * k, 0.0)
    mine = np.maximum(t, 0.0) + np.round(q)
    xq = t / 256.0
    y = np.interp(xq, X, Y)
    y = np.where(xq > 4.0, xq, np.where(xq < -4.0, 0.0, y))
    ref = np.round(256.0 * y)
    assert np.array_equal(mine, ref), "PWL hinge table does not match reference"
    return float(g0), [float(v) for v in dj]

_PWL_G0, _PWL_DJ = _build_pwl()


class Buf:
    __slots__ = ("t", "last_write", "readers")
    def __init__(self, t):
        self.t = t
        self.last_write = None
        self.readers = []
    def __getitem__(self, sl):
        return self.t[sl]
    @property
    def ap(self):
        return self.t[:]


class Sch:
    ENGS = ("sync", "vector", "scalar", "gpsimd", "tensor")

    def __init__(self, nc):
        self.nc = nc
        self.ops = {e: [] for e in self.ENGS}
        self.val = {e: 0 for e in self.ENGS}
        self.es = ExitStack()
        self.sem = {e: self.es.enter_context(nc.semaphore(f"s_{e}")) for e in self.ENGS}
        self.prev = None

    def emit(self, eng, fn, reads=(), writes=(), deps=(), dma=False, psum_src=False):
        dl = list(deps)
        if SERIALIZE and self.prev is not None and self.prev[0] != eng:
            dl.append(self.prev)
        for b in reads:
            if b.last_write is not None:
                dl.append(b.last_write)
        for b in writes:
            if b.last_write is not None:
                dl.append(b.last_write)
            dl.extend(b.readers)
        inc = 16 if dma else 1
        self.val[eng] += inc
        tok = (eng, self.val[eng])
        self.ops[eng].append((fn, dl, inc, psum_src))
        self.prev = tok
        for b in reads:
            b.readers.append(tok)
        for b in writes:
            b.last_write = tok
            b.readers = []
        return tok

    def mm(self, ps, lhsT_buf, lhsT_ap, rhs_buf, rhs_ap, start, stop, out_ap=None):
        dl = []
        if SERIALIZE and self.prev is not None and self.prev[0] != "tensor":
            dl.append(self.prev)
        if lhsT_buf is not None and lhsT_buf.last_write is not None:
            dl.append(lhsT_buf.last_write)
        if rhs_buf is not None and rhs_buf.last_write is not None:
            dl.append(rhs_buf.last_write)
        if start:
            if ps.last_write is not None:
                dl.append(ps.last_write)
            dl.extend(ps.readers)
        self.val["tensor"] += 1
        tok = ("tensor", self.val["tensor"])
        self.prev = tok
        oap = ps.ap if out_ap is None else out_ap
        self.ops["tensor"].append(
            (lambda e, oap=oap, la=lhsT_ap, ra=rhs_ap, st=start, sp=stop:
             e.matmul(oap, la, ra, start=st, stop=sp), dl, 1, False))
        if lhsT_buf is not None:
            lhsT_buf.readers.append(tok)
        if rhs_buf is not None:
            rhs_buf.readers.append(tok)
        if stop:
            ps.last_write = tok
            ps.readers = []
        return tok

    def dma(self, dst_ap, src_ap, reads=(), writes=(), deps=(), eng="sync"):
        import numpy as _np
        assert _np.prod(dst_ap.shape) == _np.prod(src_ap.shape), (
            f"dma size mismatch {dst_ap.shape} <- {src_ap.shape}")
        return self.emit(
            eng, lambda e: e.dma_start(out=dst_ap, in_=src_ap),
            reads=reads, writes=writes, deps=deps, dma=True)

    def finalize(self):
        nc = self.nc
        with nc.Block() as block:
            for eng in self.ENGS:
                if not self.ops[eng]:
                    continue
                def make(eng):
                    def body(e):
                        water = {s2: 0 for s2 in self.ENGS}
                        for fn, deps, inc, psrc in self.ops[eng]:
                            need = {}
                            for (de, dv) in deps:
                                if dv > water[de]:
                                    need[de] = max(need.get(de, 0), dv)
                            waited_pe = False
                            for de, dv in need.items():
                                e.wait_ge(self.sem[de], dv)
                                water[de] = dv
                                if de == "tensor":
                                    waited_pe = True
                            if psrc and waited_pe:
                                e.nop(cycle_cnt=192, nofuse=True)
                            ins = fn(e)
                            ins.then_inc(self.sem[eng], inc)
                    return body
                getattr(block, eng)(make(eng))
        self.es.close()


def sbuf(nc, name, shape, dtype):
    return Buf(nc.alloc_sbuf_tensor(name, shape, dtype))

def psum(nc, name, shape, dtype=F32):
    return Buf(nc.alloc_psum_tensor(name, shape, dtype))



def emit_ln(s, C, xblk, xn_out):
    v, a = "vector", "scalar"
    ones_col, ones_row = C["ones_col"], C["ones_row"]
    ps_s1, ps_s2, ps_r, ps_m = C["ps_s1"], C["ps_s2"], C["ps_rstd"], C["ps_mmr"]
    st_mean, st_rstd, st_mmr = C["st_mean"], C["st_rstd"], C["st_mmr"]
    tmp_sq, tmp_sq2, tmp32 = C["tmp_sq"], C["tmp_sq2"], C["tmp32"]
    for kb in range(DBLK):
        s.emit(v, lambda e, kb=kb: e.tensor_copy(tmp_sq.ap, xblk[kb].ap),
               reads=[xblk[kb]], writes=[tmp_sq])
        s.mm(ps_s1, ones_col, ones_col.ap, tmp_sq, tmp_sq.ap,
             start=(kb == 0), stop=(kb == DBLK - 1))
        s.emit(a, lambda e, kb=kb: e.activation(tmp_sq2.ap, xblk[kb].ap, AF.Square),
               reads=[xblk[kb]], writes=[tmp_sq2])
        s.mm(ps_s2, ones_col, ones_col.ap, tmp_sq2, tmp_sq2.ap,
             start=(kb == 0), stop=(kb == DBLK - 1))
    s.emit(v, lambda e: e.tensor_scalar(st_mean.ap, ps_s1.ap, 1.0 / D, None, ALU.mult),
           reads=[ps_s1], writes=[st_mean], psum_src=True)
    s.emit(v, lambda e: e.tensor_mul(st_rstd.ap, st_mean.ap, st_mean.ap),
           reads=[st_mean], writes=[st_rstd])
    s.emit(v, lambda e: e.scalar_tensor_tensor(st_rstd.ap, ps_s2.ap, 1.0 / D,
                                               st_rstd.ap, ALU.mult, ALU.subtract),
           reads=[ps_s2, st_rstd], writes=[st_rstd], psum_src=True)
    s.emit(a, lambda e: e.activation(st_rstd.ap, st_rstd.ap, AF.Ln,
                                     bias=C["eps1"][0:1, 0:1]),
           reads=[st_rstd, C["eps1"]], writes=[st_rstd])
    s.emit(a, lambda e: e.activation(st_rstd.ap, st_rstd.ap, AF.Exp, scale=-0.5),
           reads=[st_rstd], writes=[st_rstd])
    s.emit(v, lambda e: e.tensor_mul(st_mmr.ap, st_mean.ap, st_rstd.ap),
           reads=[st_mean, st_rstd], writes=[st_mmr])
    s.mm(ps_r, ones_row, ones_row.ap, st_rstd, st_rstd.ap, start=True, stop=True)
    s.mm(ps_m, ones_row, ones_row.ap, st_mmr, st_mmr.ap, start=True, stop=True)
    for kb in range(DBLK):
        s.emit(v, lambda e, kb=kb: e.tensor_mul(tmp32.ap, xblk[kb].ap, ps_r.ap),
               reads=[xblk[kb], ps_r], writes=[tmp32], psum_src=True)
        s.emit(v, lambda e, kb=kb: e.tensor_sub(xn_out[kb].ap, tmp32.ap, ps_m.ap),
               reads=[tmp32, ps_m], writes=[xn_out[kb]], psum_src=True)


def emit_pwl(s, C, ps_h, b1c_ap, b1c_buf, tiles, hb):
    v, a, g = "vector", "scalar", "gpsimd"
    ti, uc, accp, accn, hin = (tiles[k] for k in ("ti", "uc", "accp", "accn", "hin"))
    s.emit(v, lambda e: e.tensor_scalar(ti.ap, ps_h.ap, 256.0, b1c_ap, ALU.mult, ALU.add),
           reads=[ps_h, b1c_buf], writes=[ti], psum_src=True)
    s.emit(v, lambda e: e.scalar_tensor_tensor(uc.ap, ti.ap, -1.0, ti.ap, ALU.mult, ALU.max),
           reads=[ti], writes=[uc])
    s.emit(v, lambda e: e.tensor_scalar(uc.ap, uc.ap, 1024.0, None, ALU.min),
           reads=[uc], writes=[uc])
    s.emit(v, lambda e: e.tensor_scalar(accp.ap, uc.ap, _PWL_G0, None, ALU.mult),
           reads=[uc], writes=[accp])
    s.emit(v, lambda e: e.scalar_tensor_tensor(accp.ap, ti.ap, 0.0, accp.ap, ALU.max, ALU.add),
           reads=[ti, accp], writes=[accp])
    first_neg = True
    for k in range(1, 16):
        dk = _PWL_DJ[k - 1]
        ad = abs(dk)
        s.emit(a, lambda e, ad=ad, k=k: e.activation(
                   hin.ap, uc.ap, AF.Relu, bias=C["hingeb"][:, k - 1:k], scale=ad),
               reads=[uc, C["hingeb"]], writes=[hin])
        if dk >= 0:
            s.emit(v, lambda e: e.tensor_add(accp.ap, accp.ap, hin.ap),
                   reads=[hin, accp], writes=[accp])
        elif first_neg:
            s.emit(v, lambda e: e.tensor_copy(accn.ap, hin.ap),
                   reads=[hin], writes=[accn])
            first_neg = False
        else:
            s.emit(g, lambda e: e.tensor_tensor(accn.ap, accn.ap, hin.ap, ALU.add),
                   reads=[hin, accn], writes=[accn])
    s.emit(v, lambda e: e.tensor_tensor(ti.ap, accp.ap, accn.ap, ALU.subtract),
           reads=[accp, accn], writes=[ti])
    s.emit(v, lambda e: e.tensor_copy(hb.ap, ti.ap), reads=[ti], writes=[hb])
    return hb


def emit_ffn(s, C, xblk, w2_h):
    v = "vector"
    xn = C["xn"]
    emit_ln(s, C, xblk, xn)
    h1 = C["h1"]
    ps_h = C["ps_h"]
    for m in range(FBLK):
        ph = ps_h[m % 2]
        for kb in range(DBLK):
            s.mm(ph, C["w1b"][kb], C["w1b"][kb][:, m * P:(m + 1) * P],
                 xn[kb], xn[kb].ap, start=(kb == 0), stop=(kb == DBLK - 1))
        emit_pwl(s, C, ph, C["b1c"][:, m:m + 1], C["b1c_buf"], C["pwl"][m % 2], h1[m])
    w2r = w2_h.rearrange("(m p) o -> p m o", p=P)
    for mo in range(DBLK):
        w2b = C["w2b"][mo % 2]
        for h in range(2):
            w2s = C["wst"][h]
            hm = FBLK // 2
            s.dma(w2s.ap, w2r[:, h * hm:(h + 1) * hm, mo * P:(mo + 1) * P], writes=[w2s])
            s.emit(v, lambda e, w2b=w2b, w2s=w2s, h=h, hm=hm: e.tensor_copy(
                       w2b[:, h * hm * P:(h + 1) * hm * P], w2s.ap),
                   reads=[w2s], writes=[w2b])
        po = C["ps_o"][mo % 2]
        for m in range(FBLK):
            s.mm(po, w2b, w2b[:, m * P:(m + 1) * P], h1[m], h1[m].ap,
                 start=(m == 0), stop=(m == FBLK - 1))
        t3 = C["t3"][mo % 2]
        s.emit(v, lambda e, t3=t3, po=po, mo=mo: e.tensor_scalar(
                   t3.ap, po.ap, C["b2c"][:, mo:mo + 1], None, ALU.add),
               reads=[po, C["b2c_buf"]], writes=[t3], psum_src=True)
        s.emit(v, lambda e, t3=t3, mo=mo: e.scalar_tensor_tensor(
                   xblk[mo].ap, t3.ap, 1.0 / 512.0, xblk[mo].ap, ALU.mult, ALU.add),
               reads=[t3, xblk[mo]], writes=[xblk[mo]])


def emit_w1_load(s, C, w1_h):
    v = "vector"
    for kb in range(DBLK):
        for h in range(2):
            st = C["wst"][(kb * 2 + h) % 2]
            sl = slice(h * (FF // 2), (h + 1) * (FF // 2))
            s.dma(st.ap, w1_h[kb * P:(kb + 1) * P, sl], writes=[st])
            s.emit(v, lambda e, kb=kb, st=st, sl=sl: e.tensor_copy(
                       C["w1b"][kb][:, sl], st.ap),
                   reads=[st], writes=[C["w1b"][kb]])



def build_fused(core_ids):
    nc = bass.Bass("TRN2", target_bir_lowering=False, debug=False)
    x_h = nc.dram_tensor("x", [D, TC], F32, kind="ExternalInput")
    w1_h = nc.dram_tensor("w1", [D, FF], F32, kind="ExternalInput")
    b1_h = nc.dram_tensor("b1", [P, FBLK], F32, kind="ExternalInput")
    w2_h = nc.dram_tensor("w2", [FF, D], F32, kind="ExternalInput")
    b2_h = nc.dram_tensor("b2", [P, DBLK], F32, kind="ExternalInput")
    wz_h = nc.dram_tensor("wz", [D, D], F32, kind="ExternalInput")
    wx_h = nc.dram_tensor("wx", [D, D], F32, kind="ExternalInput")
    wdt_h = nc.dram_tensor("wdt", [D, D], F32, kind="ExternalInput")
    bz_h = nc.dram_tensor("bz", [P, DBLK], F32, kind="ExternalInput")
    bx_h = nc.dram_tensor("bx", [P, DBLK], F32, kind="ExternalInput")
    bdt_h = nc.dram_tensor("bdt", [P, DBLK], F32, kind="ExternalInput")
    wb_h = nc.dram_tensor("wb", [D, NSTATE], F32, kind="ExternalInput")
    wc_h = nc.dram_tensor("wc", [D, NSTATE], F32, kind="ExternalInput")
    bb_h = nc.dram_tensor("bb", [NSTATE, 1], F32, kind="ExternalInput")
    bc_h = nc.dram_tensor("bc", [NSTATE, 1], F32, kind="ExternalInput")
    al_h = nc.dram_tensor("alog", [D, NSTATE], F32, kind="ExternalInput")
    wo_h = nc.dram_tensor("wo", [D, D], F32, kind="ExternalInput")
    bo_h = nc.dram_tensor("bo", [P, DBLK], F32, kind="ExternalInput")
    xo_h = nc.dram_tensor("xo", [D, TC], F32, kind="ExternalOutput")
    dtl_h = nc.dram_tensor("dtl", [D, TC], F32, kind="Internal")
    vvl_h = nc.dram_tensor("vvl", [D, TC], F32, kind="Internal")
    bml_h = nc.dram_tensor("bml", [NSTATE, TC], F32, kind="Internal")
    ccl_h = nc.dram_tensor("ccl", [NSTATE, TC], F32, kind="Internal")
    gl_h = nc.dram_tensor("gl", [D, TC], BF16, kind="Internal")
    agin_h = nc.dram_tensor("agin", [2 * D, NSTATE], F32, kind="Internal")
    agout_h = nc.dram_tensor("agout", [NCORES * 2 * D, NSTATE], F32, kind="Internal")
    x2_h = nc.dram_tensor("x2i", [D, TC], F32, kind="Internal")
    z_h = nc.dram_tensor("zi", [D, TC], BF16, kind="Internal")
    selm_h = nc.dram_tensor("selm", [P, NCORES], F32, kind="ExternalInput")

    s = Sch(nc)
    v, a, g, t = "vector", "scalar", "gpsimd", "tensor"

    C = {}
    C["ones_col"] = sbuf(nc, "ones_col", [P, 1], BF16)
    C["ones_row"] = sbuf(nc, "ones_row", [1, P], F32)
    s.emit(v, lambda e: e.memset(C["ones_col"].ap, 1.0), writes=[C["ones_col"]])
    s.emit(v, lambda e: e.memset(C["ones_row"].ap, 1.0), writes=[C["ones_row"]])
    C["eps1"] = sbuf(nc, "eps1", [1, 1], F32)
    s.emit(v, lambda e: e.memset(C["eps1"].ap, EPS), writes=[C["eps1"]])
    C["ps_s1"] = psum(nc, "ps_s1", [1, TF])
    C["ps_s2"] = psum(nc, "ps_s2", [1, TF])
    C["ps_rstd"] = psum(nc, "ps_rstd", [P, TF])
    C["ps_mmr"] = psum(nc, "ps_mmr", [P, TF])
    C["ps_h"] = [psum(nc, f"ps_h{i}", [P, TF]) for i in range(2)]
    C["ps_o"] = [psum(nc, f"ps_o{i}", [P, TF]) for i in range(2)]
    C["st_mean"] = sbuf(nc, "st_mean", [1, TF], F32)
    C["st_rstd"] = sbuf(nc, "st_rstd", [1, TF], F32)
    C["st_mmr"] = sbuf(nc, "st_mmr", [1, TF], F32)
    C["tmp_sq"] = sbuf(nc, "tmp_sq", [P, TF], BF16)
    C["tmp_sq2"] = sbuf(nc, "tmp_sq2", [P, TF], BF16)
    C["tmp32"] = sbuf(nc, "tmp32", [P, TF], F32)
    C["xn"] = [sbuf(nc, f"xn{kb}", [P, TF], BF16) for kb in range(DBLK)]
    w1big = nc.alloc_sbuf_tensor("w1big", [P, DBLK * FF], BF16)
    h1big = nc.alloc_sbuf_tensor("h1big", [P, FBLK * TF], BF16)
    C["w1b"] = [Buf(w1big[:, kb * FF:(kb + 1) * FF]) for kb in range(DBLK)]
    C["h1"] = [Buf(h1big[:, m * TF:(m + 1) * TF]) for m in range(FBLK)]
    C["wst"] = [sbuf(nc, f"wst{i}", [P, FF // 2], F32) for i in range(2)]
    C["w2b"] = [sbuf(nc, f"w2b{i}", [P, FF], BF16) for i in range(2)]
    C["t3"] = [sbuf(nc, f"t3_{i}", [P, TF], I32) for i in range(2)]
    C["pwl"] = []
    for i in range(2):
        C["pwl"].append(dict(
            ti=sbuf(nc, f"pw_ti{i}", [P, TF], I32),
            uc=sbuf(nc, f"pw_uc{i}", [P, TF], F32),
            accp=sbuf(nc, f"pw_ap{i}", [P, TF], F32),
            accn=sbuf(nc, f"pw_an{i}", [P, TF], F32),
            hin=sbuf(nc, f"pw_hi{i}", [P, TF], F32),
        ))
    hb = sbuf(nc, "hingeb", [P, 15], F32)
    for k in range(1, 16):
        ad = abs(_PWL_DJ[k - 1])
        s.emit(v, lambda e, k=k, ad=ad: e.memset(hb[:, k - 1:k], -ad * 64.0 * k),
               writes=[hb])
    C["hingeb"] = hb
    def bias_col(name, h, n, scale=None):
        raw = sbuf(nc, name + "r", [P, n], F32)
        s.dma(raw.ap, h[:], writes=[raw])
        if scale is None:
            return raw
        sc = sbuf(nc, name + "c", [P, n], F32)
        s.emit(v, lambda e: e.tensor_scalar(sc.ap, raw.ap, scale, None, ALU.mult),
               reads=[raw], writes=[sc])
        return sc
    b1c = bias_col("b1", b1_h, FBLK, 256.0)
    b2c = bias_col("b2", b2_h, DBLK, 256.0)
    C["b1c"], C["b1c_buf"] = b1c.t, b1c
    C["b2c"], C["b2c_buf"] = b2c.t, b2c
    bzc = bias_col("bz", bz_h, DBLK)
    bxc = bias_col("bx", bx_h, DBLK)
    bdtc = bias_col("bdt", bdt_h, DBLK)
    boc = bias_col("bo", bo_h, DBLK)
    bb16 = sbuf(nc, "bbc", [NSTATE, 1], F32)
    s.dma(bb16.ap, bb_h[:], writes=[bb16])
    bc16 = sbuf(nc, "bcc", [NSTATE, 1], F32)
    s.dma(bc16.ap, bc_h[:], writes=[bc16])

    xs = [sbuf(nc, f"xs{kb}", [P, TF], F32) for kb in range(DBLK)]
    x2_d, ain_d, aout_d, yin_d, yout_d, z_d = (Buf(None) for _ in range(6))

    emit_w1_load(s, C, w1_h)
    for f in range(NT):
        sl = slice(f * TF, (f + 1) * TF)
        for kb in range(DBLK):
            s.dma(xs[kb].ap, x_h[kb * P:(kb + 1) * P, sl], writes=[xs[kb]])
        emit_ffn(s, C, xs, w2_h)
        for kb in range(DBLK):
            s.dma(x2_h[kb * P:(kb + 1) * P, sl], xs[kb].ap,
                  reads=[xs[kb]], writes=[x2_d])

    for name, h, off in (("wz", wz_h, 0), ("wx", wx_h, D), ("wdt", wdt_h, 2 * D)):
        for kb in range(DBLK):
            st = C["wst"][kb % 2]
            s.dma(st[:, 0:D], h[kb * P:(kb + 1) * P, :], writes=[st])
            s.emit(v, lambda e, kb=kb, st=st, off=off: e.tensor_copy(
                       C["w1b"][kb][:, off:off + D], st[:, 0:D]),
                   reads=[st], writes=[C["w1b"][kb]])
    wsmall = {}
    for name, h in (("wb", wb_h), ("wc", wc_h)):
        st = C["wst"][0]
        s.dma(st[:, 0:DBLK * NSTATE], h.rearrange("(k p) n -> p k n", p=P), writes=[st])
        wbv = sbuf(nc, f"{name}b", [P, DBLK * NSTATE], BF16)
        s.emit(v, lambda e, wbv=wbv, st=st: e.tensor_copy(wbv.ap, st[:, 0:DBLK * NSTATE]),
               reads=[st], writes=[wbv])
        wsmall[name] = wbv

    xssm = [Buf(h1big[:, i * 1024:(i + 1) * 1024].bitcast(F32)) for i in range(2)]
    dtt = [Buf(h1big[:, (2 + i) * 1024:(3 + i) * 1024].bitcast(F32)) for i in range(2)]
    pbt = [Buf(h1big[:, (4 + i) * 1024:(5 + i) * 1024].bitcast(F32)) for i in range(2)]
    sc1 = [Buf(h1big[:, (6 + i) * 1024:(7 + i) * 1024].bitcast(F32)) for i in range(2)]
    zt = sbuf(nc, "zt", [P, TF], BF16)
    sm16 = sbuf(nc, "sm16", [NSTATE, TF], F32)
    ps_bc = C["ps_rstd"]

    dtl_d, vvl_d, bml_d, ccl_d, gl_d, agin_d, agout_d = (Buf(None) for _ in range(7))
    for f in range(NT):
        sl = slice(f * TF, (f + 1) * TF)
        for kb in range(DBLK):
            s.dma(xs[kb].ap, x2_h[kb * P:(kb + 1) * P, sl], reads=[x2_d], writes=[xs[kb]])
        xn = C["xn"]
        emit_ln(s, C, xs, xn)
        for mo in range(DBLK):
            po = C["ps_o"][mo % 2]
            for kb in range(DBLK):
                s.mm(po, C["w1b"][kb], C["w1b"][kb][:, mo * P:(mo + 1) * P],
                     xn[kb], xn[kb].ap, start=(kb == 0), stop=(kb == DBLK - 1))
            s.emit(a, lambda e, po=po, mo=mo: e.activation(
                       zt.ap, po.ap, AF.Sigmoid, bias=bzc[:, mo:mo + 1]),
                   reads=[po, bzc], writes=[zt], psum_src=True)
            s.dma(z_h[mo * P:(mo + 1) * P, sl], zt.ap, reads=[zt], writes=[z_d])
        for mo in range(DBLK):
            i2 = mo % 2
            po = C["ps_o"][i2]
            for kb in range(DBLK):
                s.mm(po, C["w1b"][kb], C["w1b"][kb][:, 2 * D + mo * P:2 * D + (mo + 1) * P],
                     xn[kb], xn[kb].ap, start=(kb == 0), stop=(kb == DBLK - 1))
            s.emit(a, lambda e, po=po, mo=mo, i2=i2: e.activation(
                       pbt[i2].ap, po.ap, AF.Identity, bias=bdtc[:, mo:mo + 1]),
                   reads=[po, bdtc], writes=[pbt[i2]], psum_src=True)
            s.emit(v, lambda e, i2=i2: e.scalar_tensor_tensor(
                       sc1[i2].ap, pbt[i2].ap, -1.0, pbt[i2].ap, ALU.mult, ALU.max),
                   reads=[pbt[i2]], writes=[sc1[i2]])
            s.emit(a, lambda e, i2=i2: e.activation(sc1[i2].ap, sc1[i2].ap, AF.Exp,
                                                    scale=-1.0),
                   reads=[sc1[i2]], writes=[sc1[i2]])
            s.emit(a, lambda e, i2=i2: e.activation(sc1[i2].ap, sc1[i2].ap, AF.Ln,
                                                    bias=1.0),
                   reads=[sc1[i2]], writes=[sc1[i2]])
            s.emit(v, lambda e, i2=i2: e.scalar_tensor_tensor(
                       dtt[i2].ap, pbt[i2].ap, 0.0, sc1[i2].ap, ALU.max, ALU.add),
                   reads=[pbt[i2], sc1[i2]], writes=[dtt[i2]])
            s.dma(dtl_h[mo * P:(mo + 1) * P, sl], dtt[i2].ap, reads=[dtt[i2]], writes=[dtl_d])
            po2 = C["ps_h"][i2]
            for kb in range(DBLK):
                s.mm(po2, C["w1b"][kb], C["w1b"][kb][:, D + mo * P:D + (mo + 1) * P],
                     xn[kb], xn[kb].ap, start=(kb == 0), stop=(kb == DBLK - 1))
            s.emit(a, lambda e, po2=po2, mo=mo, i2=i2: e.activation(
                       xssm[i2].ap, po2.ap, AF.Identity, bias=bxc[:, mo:mo + 1]),
                   reads=[po2, bxc], writes=[xssm[i2]], psum_src=True)
            s.emit(v, lambda e, i2=i2: e.tensor_mul(xssm[i2].ap, xssm[i2].ap, dtt[i2].ap),
                   reads=[dtt[i2], xssm[i2]], writes=[xssm[i2]])
            s.dma(vvl_h[mo * P:(mo + 1) * P, sl], xssm[i2].ap, reads=[xssm[i2]], writes=[vvl_d])
        for name, out_h2, out_d2, bias in (("wb", bml_h, bml_d, bb16),
                                           ("wc", ccl_h, ccl_d, bc16)):
            wbv = wsmall[name]
            for kb in range(DBLK):
                s.mm(ps_bc, wbv, wbv[:, kb * NSTATE:(kb + 1) * NSTATE],
                     xn[kb], xn[kb].ap, start=(kb == 0), stop=(kb == DBLK - 1),
                     out_ap=ps_bc.t[0:NSTATE, :])
            s.emit(a, lambda e, bias=bias: e.activation(
                       sm16.ap, ps_bc.t[0:NSTATE, :], AF.Identity, bias=bias[:, 0:1]),
                   reads=[ps_bc, bias], writes=[sm16], psum_src=True)
            s.dma(out_h2[:, sl], sm16.ap, reads=[sm16], writes=[out_d2])

    rg = [list(core_ids)]
    acol = sbuf(nc, "acol", [P, DBLK * NSTATE], F32)
    s.dma(acol.ap, al_h.rearrange("(k p) n -> p k n", p=P), writes=[acol])
    s.emit(a, lambda e: e.activation(acol.ap, acol.ap, AF.Exp), reads=[acol], writes=[acol])
    s.emit(v, lambda e: e.tensor_scalar(acol.ap, acol.ap, -1.0, None, ALU.mult),
           reads=[acol], writes=[acol])
    yloc = [Buf(w1big[:, dj * 4096:(dj + 1) * 4096].bitcast(F32)) for dj in range(DBLK)]
    at = Buf(h1big[:, 0:4096].bitcast(F32))
    ut = Buf(h1big[:, 4096:8192].bitcast(F32))
    ht = Buf(h1big[:, 8192:12288].bitcast(F32))
    qt = Buf(h1big[:, 12288:16384].bitcast(F32))
    dtb = C["wst"][0]
    vvb = C["wst"][1]
    zerot = Buf(C["w2b"][0][:, 0:4096].bitcast(F32))
    rowB = [Buf(C["w2b"][0][0:1, 0:4096].bitcast(F32)),
            Buf(C["w2b"][0][32:33, 0:4096].bitcast(F32))]
    rowC = [Buf(C["w2b"][1][0:1, 0:4096].bitcast(F32)),
            Buf(C["w2b"][1][32:33, 0:4096].bitcast(F32))]
    rowbase = {id(rowB[0]): 0, id(rowB[1]): 32, id(rowC[0]): 0, id(rowC[1]): 32}
    ones128 = sbuf(nc, "ones128", [P, P], F32)
    s.emit(v, lambda e: e.memset(ones128.ap, 1.0), writes=[ones128])
    stot = sbuf(nc, "stot", [P, 1], F32)
    hendL = sbuf(nc, "hendL", [P, NSTATE], F32)
    wtotL = sbuf(nc, "wtotL", [P, NSTATE], F32)
    hinT = sbuf(nc, "hinT", [P, NSTATE], F32)
    e1T = sbuf(nc, "e1T", [P, NSTATE], F32)
    wjT = sbuf(nc, "wjT", [P, NSTATE], F32)
    hjT = sbuf(nc, "hjT", [P, NSTATE], F32)
    selm = sbuf(nc, "selmT", [P, NCORES], F32)
    s.dma(selm.ap, selm_h[:], writes=[selm])
    NPB = TC // TF

    def scan_block(dj, init_col, data1, y_first):
        a0 = dj * NSTATE
        for n in range(NSTATE):
            i2 = n % 2
            rC = rowC[i2]
            s.dma(rC.ap, ccl_h[n:n + 1, :], reads=[ccl_d], writes=[rC])
            if data1 is ut:
                rB = rowB[i2]
                s.dma(rB.ap, bml_h[n:n + 1, :], reads=[bml_d], writes=[rB])
            s.emit(a, lambda e, n=n, a0=a0: e.activation(
                       at.ap, dtb.ap, AF.Exp, scale=acol[:, a0 + n:a0 + n + 1]),
                   reads=[dtb, acol], writes=[at])
            if data1 is ut:
                rbb = rowbase[id(rB)]
                for p in range(NPB):
                    pb_ = C["ps_h"][p % 2]
                    s.mm(pb_, ones128, ones128[rbb:rbb + 1, :], rB,
                         rB[0:1, p * TF:(p + 1) * TF], start=True, stop=True)
                    s.emit(v, lambda e, p=p, pb_=pb_: e.tensor_tensor(
                               ut[:, p * TF:(p + 1) * TF],
                               vvb[:, p * TF:(p + 1) * TF], pb_.ap, ALU.mult),
                           reads=[vvb, pb_], writes=[ut], psum_src=True)
            s.emit(v, lambda e, n=n, ic=init_col(n): e.tensor_tensor_scan(
                       ht.ap, at.ap, data1.ap, ic, ALU.mult, ALU.add),
                   reads=[at, data1] + ([hinT] if init_col(n) is not None and data1 is zerot else []),
                   writes=[ht])
            if data1 is ut:
                s.emit(v, lambda e, n=n: e.tensor_copy(
                           hendL[:, n:n + 1], ht[:, TC - 1:TC]),
                       reads=[ht], writes=[hendL])
            rcb = rowbase[id(rC)]
            for p in range(NPB):
                pc_ = C["ps_o"][p % 2]
                s.mm(pc_, ones128, ones128[rcb:rcb + 1, :], rC,
                     rC[0:1, p * TF:(p + 1) * TF], start=True, stop=True)
                s.emit(v, lambda e, p=p, pc_=pc_: e.tensor_tensor(
                           qt[:, p * TF:(p + 1) * TF],
                           ht[:, p * TF:(p + 1) * TF], pc_.ap, ALU.mult),
                       reads=[ht, pc_], writes=[qt], psum_src=True)
            if y_first and n == 0:
                s.emit(v, lambda e, dj=dj: e.tensor_copy(yloc[dj].ap, qt.ap),
                       reads=[qt], writes=[yloc[dj]])
            else:
                s.emit(v, lambda e, dj=dj: e.tensor_add(yloc[dj].ap, yloc[dj].ap, qt.ap),
                       reads=[qt, yloc[dj]], writes=[yloc[dj]])

    for dj in range(DBLK):
        s.dma(dtb.ap, dtl_h[dj * P:(dj + 1) * P, :], reads=[dtl_d], writes=[dtb])
        s.dma(vvb.ap, vvl_h[dj * P:(dj + 1) * P, :], reads=[vvl_d], writes=[vvb])
        scan_block(dj, lambda n: 0.0, ut, y_first=True)
        s.emit(v, lambda e: e.tensor_reduce(stot.ap, dtb.ap, AX.X, ALU.add),
               reads=[dtb], writes=[stot])
        for n in range(NSTATE):
            s.emit(a, lambda e, n=n, dj=dj: e.activation(
                       wtotL[:, n:n + 1], stot.ap, AF.Exp,
                       scale=acol[:, dj * NSTATE + n:dj * NSTATE + n + 1]),
                   reads=[stot, acol], writes=[wtotL])
        s.dma(agin_h[dj * P:(dj + 1) * P, :], hendL.ap, reads=[hendL], writes=[agin_d])
        s.dma(agin_h[D + dj * P:D + (dj + 1) * P, :], wtotL.ap,
              reads=[wtotL], writes=[agin_d])

    s.emit(g, lambda e: e.collective_compute(
               "AllGather", ALU.bypass, replica_groups=rg,
               ins=[agin_h[:]], outs=[agout_h[:]]),
           reads=[agin_d], writes=[agout_d], dma=False)

    s.emit(v, lambda e: e.memset(zerot.ap, 0.0), writes=[zerot])
    for dj in range(DBLK):
        s.dma(dtb.ap, dtl_h[dj * P:(dj + 1) * P, :], reads=[dtl_d], writes=[dtb])
        s.emit(v, lambda e: e.memset(hinT.ap, 0.0), writes=[hinT])
        for j in range(NCORES):
            base = j * 2 * D
            s.dma(hjT.ap, agout_h[base + dj * P:base + (dj + 1) * P, :],
                  reads=[agout_d], writes=[hjT])
            s.dma(wjT.ap, agout_h[base + D + dj * P:base + D + (dj + 1) * P, :],
                  reads=[agout_d], writes=[wjT])
            s.emit(v, lambda e: e.tensor_mul(e1T.ap, wjT.ap, hinT.ap),
                   reads=[wjT, hinT], writes=[e1T])
            s.emit(v, lambda e: e.tensor_add(e1T.ap, e1T.ap, hjT.ap),
                   reads=[e1T, hjT], writes=[e1T])
            s.emit(v, lambda e: e.tensor_sub(e1T.ap, e1T.ap, hinT.ap),
                   reads=[e1T, hinT], writes=[e1T])
            s.emit(v, lambda e, j=j: e.scalar_tensor_tensor(
                       hinT.ap, e1T.ap, selm[:, j:j + 1], hinT.ap, ALU.mult, ALU.add),
                   reads=[e1T, selm, hinT], writes=[hinT])
        scan_block(dj, lambda n: hinT[:, n:n + 1], zerot, y_first=False)
        zview = Buf(C["w2b"][1][0:P, 0:2048])
        gview = Buf(C["w2b"][1][0:P, 2048:4096])
        s.dma(zview.ap, z_h[dj * P:(dj + 1) * P, :], reads=[z_d], writes=[zview])
        s.emit(v, lambda e, dj=dj: e.tensor_tensor(
                   gview.ap, yloc[dj].ap, zview.ap, ALU.mult),
               reads=[yloc[dj], zview], writes=[gview])
        s.dma(gl_h[dj * P:(dj + 1) * P, :], gview.ap, reads=[gview], writes=[gl_d])

    emit_w1_load(s, C, w1_h)
    gt = C["h1"][:DBLK]
    xo_d = Buf(None)
    for f in range(NT):
        sl = slice(f * TF, (f + 1) * TF)
        for kb in range(DBLK):
            s.dma(xs[kb].ap, x2_h[kb * P:(kb + 1) * P, sl], reads=[x2_d], writes=[xs[kb]])
        for kb in range(DBLK):
            st = C["wst"][kb % 2]
            s.dma(st[:, 0:D], wo_h[kb * P:(kb + 1) * P, :], writes=[st])
            s.emit(v, lambda e, kb=kb, st=st: e.tensor_copy(
                       C["w2b"][0][:, kb * 512:kb * 512 + 512], st[:, 0:512]),
                   reads=[st], writes=[C["w2b"][0]])
            s.emit(v, lambda e, kb=kb, st=st: e.tensor_copy(
                       C["w2b"][1][:, kb * 512:kb * 512 + 512], st[:, 512:D]),
                   reads=[st], writes=[C["w2b"][1]])
        for kb in range(DBLK):
            s.dma(gt[kb].ap, gl_h[kb * P:(kb + 1) * P, sl], reads=[gl_d], writes=[gt[kb]])
        for mo in range(DBLK):
            po = C["ps_o"][mo % 2]
            half, off = divmod(mo * P, 512)
            wob = C["w2b"][half]
            for kb in range(DBLK):
                s.mm(po, wob, wob[:, kb * 512 + off:kb * 512 + off + P],
                     gt[kb], gt[kb].ap, start=(kb == 0), stop=(kb == DBLK - 1))
            s.emit(v, lambda e, po=po, mo=mo: e.scalar_tensor_tensor(
                       xs[mo].ap, po.ap, boc[:, mo:mo + 1], xs[mo].ap, ALU.add, ALU.add),
                   reads=[po, boc, xs[mo]], writes=[xs[mo]], psum_src=True)
        emit_ffn(s, C, xs, w2_h)
        for kb in range(DBLK):
            s.dma(xo_h[kb * P:(kb + 1) * P, sl], xs[kb].ap, reads=[xs[kb]], writes=[xo_d])

    s.emit("sync", lambda e: e.nop(), deps=[("sync", s.val["sync"])])
    s.finalize()
    return nc


def _col(b):
    n = b.shape[0] // P
    return np.ascontiguousarray(b.reshape(n, P).T)


def kernel(**inputs):
    xf = np.asarray(inputs["L0_attn_x"], np.float32).reshape(B * L, D)
    shared = dict(
        w1=np.ascontiguousarray(np.asarray(inputs["L0_ffn_w1"], np.float32)),
        b1=_col(np.asarray(inputs["L0_ffn_b1"], np.float32)),
        w2=np.ascontiguousarray(np.asarray(inputs["L0_ffn_w2"], np.float32)),
        b2=_col(np.asarray(inputs["L0_ffn_b2"], np.float32)),
        wz=np.ascontiguousarray(np.asarray(inputs["L0_attn_Wz"], np.float32)),
        wx=np.ascontiguousarray(np.asarray(inputs["L0_attn_Wx"], np.float32)),
        wdt=np.ascontiguousarray(np.asarray(inputs["L0_attn_Wdt"], np.float32)),
        bz=_col(np.asarray(inputs["L0_attn_bz"], np.float32)),
        bx=_col(np.asarray(inputs["L0_attn_bx"], np.float32)),
        bdt=_col(np.asarray(inputs["L0_attn_bdt"], np.float32)),
        wb=np.ascontiguousarray(np.asarray(inputs["L0_attn_WB"], np.float32)),
        wc=np.ascontiguousarray(np.asarray(inputs["L0_attn_WC"], np.float32)),
        bb=np.ascontiguousarray(np.asarray(inputs["L0_attn_bB"], np.float32).reshape(NSTATE, 1)),
        bc=np.ascontiguousarray(np.asarray(inputs["L0_attn_bC"], np.float32).reshape(NSTATE, 1)),
        wo=np.ascontiguousarray(np.asarray(inputs["L0_attn_Wout"], np.float32)),
        bo=_col(np.asarray(inputs["L0_attn_bout"], np.float32)),
    )
    alog = np.asarray(inputs["L0_attn_A_log"], np.float32)
    core_ids = list(range(NCORES))
    nc = build_fused(core_ids)
    maps = []
    for i in range(NCORES):
        m = dict(shared)
        m["x"] = np.ascontiguousarray(xf[i * TC:(i + 1) * TC, :].T)
        m["alog"] = np.ascontiguousarray(alog)
        sel = np.zeros((P, NCORES), np.float32)
        for j in range(NCORES):
            if j // 4 == i // 4 and j % 4 < i % 4:
                sel[:, j] = 1.0
        m["selm"] = sel
        maps.append(m)
    res = run_bass_kernel_spmd(nc, maps, core_ids=core_ids)
    out = np.empty((B * L, D), np.float32)
    for i in range(NCORES):
        out[i * TC:(i + 1) * TC, :] = np.asarray(res.results[i]["xo"], np.float32).T
    return out.reshape(B, L, D)


# version 05
# speedup vs baseline: 1.0267x; 1.0267x over previous
import math
from contextlib import ExitStack

import numpy as np
import ml_dtypes

import concourse.bass as bass
import concourse.mybir as mybir
from concourse.bass_utils import run_bass_kernel_spmd

F32 = mybir.dt.float32
I32 = mybir.dt.int32
BF16 = mybir.dt.bfloat16
AF = mybir.ActivationFunctionType
ALU = mybir.AluOpType
AX = mybir.AxisListType

P = 128
D = 1024
FF = 4096
NSTATE = 16
B = 2
L = 8192
NCORES = 8
TC = B * L // NCORES
DBLK = D // P
FBLK = FF // P
TF = 512
NT = TC // TF
TP2 = 2048
EPS = 1e-5

SERIALIZE = True
CHAIN_EXEMPT = set()
ISSUE_CHAIN = True
DEBUG_TAPS = False


def _gelu64(x):
    x = np.asarray(x, np.float64)
    return 0.5 * x * (1.0 + np.vectorize(math.erf)(x / math.sqrt(2.0)))

def _build_pwl():
    X = np.linspace(-4.0, 4.0, 33)
    Y = _gelu64(X)
    j = np.arange(17)
    qv = 256.0 * Y[16 - j]
    g = (qv[1:] - qv[:-1]) / 64.0
    g0 = g[0]
    dj = g[1:] - g[:-1]
    t = np.arange(-40000, 40001, dtype=np.float64)
    uc = np.minimum(np.abs(t), 1024.0)
    q = g0 * uc
    for k in range(1, 16):
        q = q + dj[k - 1] * np.maximum(uc - 64.0 * k, 0.0)
    mine = np.maximum(t, 0.0) + np.round(q)
    xq = t / 256.0
    y = np.interp(xq, X, Y)
    y = np.where(xq > 4.0, xq, np.where(xq < -4.0, 0.0, y))
    ref = np.round(256.0 * y)
    assert np.array_equal(mine, ref), "PWL hinge table does not match reference"
    return float(g0), [float(v) for v in dj]

_PWL_G0, _PWL_DJ = _build_pwl()


class Buf:
    __slots__ = ("t", "last_write", "readers")
    def __init__(self, t):
        self.t = t
        self.last_write = None
        self.readers = []
    def __getitem__(self, sl):
        return self.t[sl]
    @property
    def ap(self):
        return self.t[:]


class Sch:
    ENGS = ("sync", "vector", "scalar", "gpsimd", "tensor")
    SEMS = ENGS + ("syncI",)

    def __init__(self, nc):
        self.nc = nc
        self.ops = {e: [] for e in self.ENGS}
        self.val = {e: 0 for e in self.SEMS}
        self.es = ExitStack()
        self.sem = {e: self.es.enter_context(nc.semaphore(f"s_{e}")) for e in self.SEMS}
        self.prev = None

    def emit(self, eng, fn, reads=(), writes=(), deps=(), dma=False, psum_src=False,
             ccomp=False):
        dl = list(deps)
        if SERIALIZE and eng not in CHAIN_EXEMPT \
                and self.prev is not None and self.prev[0] != eng:
            dl.append(self.prev)
        for b in reads:
            if b.last_write is not None:
                dl.append(b.last_write)
        for b in writes:
            if b.last_write is not None:
                dl.append(b.last_write)
            dl.extend(b.readers)
        inc = 16 if dma else 1
        self.val[eng] += inc
        tok = (eng, self.val[eng])
        self.ops[eng].append((fn, dl, inc, psum_src, None))
        if dma and ISSUE_CHAIN and not ccomp:
            self.val["syncI"] += 1
            itok = ("syncI", self.val["syncI"])
            self.ops[eng].append(
                (lambda e: e.nop(cycle_cnt=0, nofuse=True), [], 1, False, "syncI"))
            if eng not in CHAIN_EXEMPT:
                self.prev = itok
        elif eng not in CHAIN_EXEMPT:
            self.prev = tok
        for b in reads:
            b.readers.append(tok)
        for b in writes:
            b.last_write = tok
            b.readers = []
        return tok

    def mm(self, ps, lhsT_buf, lhsT_ap, rhs_buf, rhs_ap, start, stop, out_ap=None):
        dl = []
        if SERIALIZE and "tensor" not in CHAIN_EXEMPT \
                and self.prev is not None and self.prev[0] != "tensor":
            dl.append(self.prev)
        if lhsT_buf is not None and lhsT_buf.last_write is not None:
            dl.append(lhsT_buf.last_write)
        if rhs_buf is not None and rhs_buf.last_write is not None:
            dl.append(rhs_buf.last_write)
        if start:
            if ps.last_write is not None:
                dl.append(ps.last_write)
            dl.extend(ps.readers)
        self.val["tensor"] += 1
        tok = ("tensor", self.val["tensor"])
        if "tensor" not in CHAIN_EXEMPT:
            self.prev = tok
        oap = ps.ap if out_ap is None else out_ap
        self.ops["tensor"].append(
            (lambda e, oap=oap, la=lhsT_ap, ra=rhs_ap, st=start, sp=stop:
             e.matmul(oap, la, ra, start=st, stop=sp), dl, 1, False, None))
        if lhsT_buf is not None:
            lhsT_buf.readers.append(tok)
        if rhs_buf is not None:
            rhs_buf.readers.append(tok)
        if stop:
            ps.last_write = tok
            ps.readers = []
        return tok

    def dma(self, dst_ap, src_ap, reads=(), writes=(), deps=(), eng="sync",
            ccomp=False):
        import numpy as _np
        assert _np.prod(dst_ap.shape) == _np.prod(src_ap.shape), (
            f"dma size mismatch {dst_ap.shape} <- {src_ap.shape}")
        return self.emit(
            eng, lambda e: e.dma_start(out=dst_ap, in_=src_ap),
            reads=reads, writes=writes, deps=deps, dma=True, ccomp=ccomp)

    def finalize(self):
        nc = self.nc
        with nc.Block() as block:
            for eng in self.ENGS:
                if not self.ops[eng]:
                    continue
                def make(eng):
                    def body(e):
                        water = {s2: 0 for s2 in self.SEMS}
                        for fn, deps, inc, psrc, semov in self.ops[eng]:
                            need = {}
                            for (de, dv) in deps:
                                if dv > water[de]:
                                    need[de] = max(need.get(de, 0), dv)
                            waited_pe = False
                            for de, dv in need.items():
                                e.wait_ge(self.sem[de], dv)
                                water[de] = dv
                                if de == "tensor":
                                    waited_pe = True
                            if psrc and waited_pe:
                                e.nop(cycle_cnt=192, nofuse=True)
                            ins = fn(e)
                            ins.then_inc(self.sem[semov or eng], inc)
                    return body
                getattr(block, eng)(make(eng))
        self.es.close()


def sbuf(nc, name, shape, dtype):
    return Buf(nc.alloc_sbuf_tensor(name, shape, dtype))

def psum(nc, name, shape, dtype=F32):
    return Buf(nc.alloc_psum_tensor(name, shape, dtype))



def emit_ln(s, C, xblk, xn_out):
    v, a = "vector", "scalar"
    ones_col, ones_row = C["ones_col"], C["ones_row"]
    ps_s1, ps_s2, ps_r, ps_m = C["ps_s1"], C["ps_s2"], C["ps_rstd"], C["ps_mmr"]
    st_mean, st_rstd, st_mmr = C["st_mean"], C["st_rstd"], C["st_mmr"]
    tmp_sq, tmp_sq2, tmp32 = C["tmp_sq"], C["tmp_sq2"], C["tmp32"]
    for kb in range(DBLK):
        s.emit(v, lambda e, kb=kb: e.tensor_copy(tmp_sq.ap, xblk[kb].ap),
               reads=[xblk[kb]], writes=[tmp_sq])
        s.mm(ps_s1, ones_col, ones_col.ap, tmp_sq, tmp_sq.ap,
             start=(kb == 0), stop=(kb == DBLK - 1))
        s.emit(a, lambda e, kb=kb: e.activation(tmp_sq2.ap, xblk[kb].ap, AF.Square),
               reads=[xblk[kb]], writes=[tmp_sq2])
        s.mm(ps_s2, ones_col, ones_col.ap, tmp_sq2, tmp_sq2.ap,
             start=(kb == 0), stop=(kb == DBLK - 1))
    s.emit(v, lambda e: e.tensor_scalar(st_mean.ap, ps_s1.ap, 1.0 / D, None, ALU.mult),
           reads=[ps_s1], writes=[st_mean], psum_src=True)
    s.emit(v, lambda e: e.tensor_mul(st_rstd.ap, st_mean.ap, st_mean.ap),
           reads=[st_mean], writes=[st_rstd])
    s.emit(v, lambda e: e.scalar_tensor_tensor(st_rstd.ap, ps_s2.ap, 1.0 / D,
                                               st_rstd.ap, ALU.mult, ALU.subtract),
           reads=[ps_s2, st_rstd], writes=[st_rstd], psum_src=True)
    s.emit(a, lambda e: e.activation(st_rstd.ap, st_rstd.ap, AF.Ln,
                                     bias=C["eps1"][0:1, 0:1]),
           reads=[st_rstd, C["eps1"]], writes=[st_rstd])
    s.emit(a, lambda e: e.activation(st_rstd.ap, st_rstd.ap, AF.Exp, scale=-0.5),
           reads=[st_rstd], writes=[st_rstd])
    s.emit(v, lambda e: e.tensor_mul(st_mmr.ap, st_mean.ap, st_rstd.ap),
           reads=[st_mean, st_rstd], writes=[st_mmr])
    s.mm(ps_r, ones_row, ones_row.ap, st_rstd, st_rstd.ap, start=True, stop=True)
    s.mm(ps_m, ones_row, ones_row.ap, st_mmr, st_mmr.ap, start=True, stop=True)
    for kb in range(DBLK):
        s.emit(v, lambda e, kb=kb: e.tensor_mul(tmp32.ap, xblk[kb].ap, ps_r.ap),
               reads=[xblk[kb], ps_r], writes=[tmp32], psum_src=True)
        s.emit(v, lambda e, kb=kb: e.tensor_sub(xn_out[kb].ap, tmp32.ap, ps_m.ap),
               reads=[tmp32, ps_m], writes=[xn_out[kb]], psum_src=True)


def emit_pwl(s, C, ps_h, b1c_ap, b1c_buf, tiles, hb):
    v, a, g = "vector", "scalar", "gpsimd"
    ti, uc, accp, accn, hin = (tiles[k] for k in ("ti", "uc", "accp", "accn", "hin"))
    s.emit(v, lambda e: e.tensor_scalar(ti.ap, ps_h.ap, 256.0, b1c_ap, ALU.mult, ALU.add),
           reads=[ps_h, b1c_buf], writes=[ti], psum_src=True)
    s.emit(v, lambda e: e.scalar_tensor_tensor(uc.ap, ti.ap, -1.0, ti.ap, ALU.mult, ALU.max),
           reads=[ti], writes=[uc])
    s.emit(v, lambda e: e.tensor_scalar(uc.ap, uc.ap, 1024.0, None, ALU.min),
           reads=[uc], writes=[uc])
    s.emit(v, lambda e: e.tensor_scalar(accp.ap, uc.ap, _PWL_G0, None, ALU.mult),
           reads=[uc], writes=[accp])
    s.emit(v, lambda e: e.scalar_tensor_tensor(accp.ap, ti.ap, 0.0, accp.ap, ALU.max, ALU.add),
           reads=[ti, accp], writes=[accp])
    first_neg = True
    for k in range(1, 16):
        dk = _PWL_DJ[k - 1]
        ad = abs(dk)
        s.emit(a, lambda e, ad=ad, k=k: e.activation(
                   hin.ap, uc.ap, AF.Relu, bias=C["hingeb"][:, k - 1:k], scale=ad),
               reads=[uc, C["hingeb"]], writes=[hin])
        if dk >= 0:
            s.emit(v, lambda e: e.tensor_add(accp.ap, accp.ap, hin.ap),
                   reads=[hin, accp], writes=[accp])
        elif first_neg:
            s.emit(v, lambda e: e.tensor_copy(accn.ap, hin.ap),
                   reads=[hin], writes=[accn])
            first_neg = False
        else:
            s.emit(g, lambda e: e.tensor_tensor(accn.ap, accn.ap, hin.ap, ALU.add),
                   reads=[hin, accn], writes=[accn])
    s.emit(v, lambda e: e.tensor_tensor(ti.ap, accp.ap, accn.ap, ALU.subtract),
           reads=[accp, accn], writes=[ti])
    s.emit(v, lambda e: e.tensor_copy(hb.ap, ti.ap), reads=[ti], writes=[hb])
    return hb


def emit_ffn(s, C, xblk, w2_h):
    v = "vector"
    xn = C["xn"]
    emit_ln(s, C, xblk, xn)
    h1 = C["h1"]
    ps_h = C["ps_h"]
    for m in range(FBLK):
        ph = ps_h[m % 2]
        for kb in range(DBLK):
            s.mm(ph, C["w1b"][kb], C["w1b"][kb][:, m * P:(m + 1) * P],
                 xn[kb], xn[kb].ap, start=(kb == 0), stop=(kb == DBLK - 1))
        emit_pwl(s, C, ph, C["b1c"][:, m:m + 1], C["b1c_buf"], C["pwl"][m % 2], h1[m])
    w2r = w2_h.rearrange("(m p) o -> p m o", p=P)
    for mo in range(DBLK):
        w2b = C["w2b"][mo % 2]
        for h in range(2):
            w2s = C["wst"][h]
            hm = FBLK // 2
            s.dma(w2s.ap, w2r[:, h * hm:(h + 1) * hm, mo * P:(mo + 1) * P], writes=[w2s])
            s.emit(v, lambda e, w2b=w2b, w2s=w2s, h=h, hm=hm: e.tensor_copy(
                       w2b[:, h * hm * P:(h + 1) * hm * P], w2s.ap),
                   reads=[w2s], writes=[w2b])
        po = C["ps_o"][mo % 2]
        for m in range(FBLK):
            s.mm(po, w2b, w2b[:, m * P:(m + 1) * P], h1[m], h1[m].ap,
                 start=(m == 0), stop=(m == FBLK - 1))
        t3 = C["t3"][mo % 2]
        s.emit(v, lambda e, t3=t3, po=po, mo=mo: e.tensor_scalar(
                   t3.ap, po.ap, C["b2c"][:, mo:mo + 1], None, ALU.add),
               reads=[po, C["b2c_buf"]], writes=[t3], psum_src=True)
        s.emit(v, lambda e, t3=t3, mo=mo: e.scalar_tensor_tensor(
                   xblk[mo].ap, t3.ap, 1.0 / 512.0, xblk[mo].ap, ALU.mult, ALU.add),
               reads=[t3, xblk[mo]], writes=[xblk[mo]])


def emit_w1_load(s, C, w1_h):
    v = "vector"
    for kb in range(DBLK):
        for h in range(2):
            st = C["wst"][(kb * 2 + h) % 2]
            sl = slice(h * (FF // 2), (h + 1) * (FF // 2))
            s.dma(st.ap, w1_h[kb * P:(kb + 1) * P, sl], writes=[st])
            s.emit(v, lambda e, kb=kb, st=st, sl=sl: e.tensor_copy(
                       C["w1b"][kb][:, sl], st.ap),
                   reads=[st], writes=[C["w1b"][kb]])



def build_fused(core_ids):
    nc = bass.Bass("TRN2", target_bir_lowering=False, debug=False)
    x_h = nc.dram_tensor("x", [D, TC], F32, kind="ExternalInput")
    w1_h = nc.dram_tensor("w1", [D, FF], F32, kind="ExternalInput")
    b1_h = nc.dram_tensor("b1", [P, FBLK], F32, kind="ExternalInput")
    w2_h = nc.dram_tensor("w2", [FF, D], F32, kind="ExternalInput")
    b2_h = nc.dram_tensor("b2", [P, DBLK], F32, kind="ExternalInput")
    wz_h = nc.dram_tensor("wz", [D, D], F32, kind="ExternalInput")
    wx_h = nc.dram_tensor("wx", [D, D], F32, kind="ExternalInput")
    wdt_h = nc.dram_tensor("wdt", [D, D], F32, kind="ExternalInput")
    bz_h = nc.dram_tensor("bz", [P, DBLK], F32, kind="ExternalInput")
    bx_h = nc.dram_tensor("bx", [P, DBLK], F32, kind="ExternalInput")
    bdt_h = nc.dram_tensor("bdt", [P, DBLK], F32, kind="ExternalInput")
    wb_h = nc.dram_tensor("wb", [D, NSTATE], F32, kind="ExternalInput")
    wc_h = nc.dram_tensor("wc", [D, NSTATE], F32, kind="ExternalInput")
    bb_h = nc.dram_tensor("bb", [NSTATE, 1], F32, kind="ExternalInput")
    bc_h = nc.dram_tensor("bc", [NSTATE, 1], F32, kind="ExternalInput")
    al_h = nc.dram_tensor("alog", [D, NSTATE], F32, kind="ExternalInput")
    wo_h = nc.dram_tensor("wo", [D, D], F32, kind="ExternalInput")
    bo_h = nc.dram_tensor("bo", [P, DBLK], F32, kind="ExternalInput")
    xo_h = nc.dram_tensor("xo", [D, TC], F32, kind="ExternalOutput")
    KI = "ExternalOutput" if DEBUG_TAPS else "Internal"
    dtl_h = nc.dram_tensor("dtl", [D, TC], F32, kind=KI)
    vvl_h = nc.dram_tensor("vvl", [D, TC], F32, kind=KI)
    bml_h = nc.dram_tensor("bml", [NSTATE, TC], F32, kind=KI)
    ccl_h = nc.dram_tensor("ccl", [NSTATE, TC], F32, kind=KI)
    gl_h = nc.dram_tensor("gl", [D, TC], BF16, kind=KI)
    agin_h = nc.dram_tensor("agin", [2 * D, NSTATE], F32, kind="Internal")
    agout_h = nc.dram_tensor("agout", [NCORES * 2 * D, NSTATE], F32, kind="Internal")
    x2_h = nc.dram_tensor("x2i", [D, TC], F32, kind=KI)
    z_h = nc.dram_tensor("zi", [D, TC], BF16, kind=KI)
    selm_h = nc.dram_tensor("selm", [P, NCORES], F32, kind="ExternalInput")

    s = Sch(nc)
    v, a, g, t = "vector", "scalar", "gpsimd", "tensor"

    C = {}
    C["ones_col"] = sbuf(nc, "ones_col", [P, 1], BF16)
    C["ones_row"] = sbuf(nc, "ones_row", [1, P], F32)
    s.emit(v, lambda e: e.memset(C["ones_col"].ap, 1.0), writes=[C["ones_col"]])
    s.emit(v, lambda e: e.memset(C["ones_row"].ap, 1.0), writes=[C["ones_row"]])
    C["eps1"] = sbuf(nc, "eps1", [1, 1], F32)
    s.emit(v, lambda e: e.memset(C["eps1"].ap, EPS), writes=[C["eps1"]])
    C["ps_s1"] = psum(nc, "ps_s1", [1, TF])
    C["ps_s2"] = psum(nc, "ps_s2", [1, TF])
    C["ps_rstd"] = psum(nc, "ps_rstd", [P, TF])
    C["ps_mmr"] = psum(nc, "ps_mmr", [P, TF])
    C["ps_h"] = [psum(nc, f"ps_h{i}", [P, TF]) for i in range(2)]
    C["ps_o"] = [psum(nc, f"ps_o{i}", [P, TF]) for i in range(2)]
    C["st_mean"] = sbuf(nc, "st_mean", [1, TF], F32)
    C["st_rstd"] = sbuf(nc, "st_rstd", [1, TF], F32)
    C["st_mmr"] = sbuf(nc, "st_mmr", [1, TF], F32)
    C["tmp_sq"] = sbuf(nc, "tmp_sq", [P, TF], BF16)
    C["tmp_sq2"] = sbuf(nc, "tmp_sq2", [P, TF], BF16)
    C["tmp32"] = sbuf(nc, "tmp32", [P, TF], F32)
    C["xn"] = [sbuf(nc, f"xn{kb}", [P, TF], BF16) for kb in range(DBLK)]
    w1big = nc.alloc_sbuf_tensor("w1big", [P, DBLK * FF], BF16)
    h1big = nc.alloc_sbuf_tensor("h1big", [P, FBLK * TF], BF16)
    C["w1b"] = [Buf(w1big[:, kb * FF:(kb + 1) * FF]) for kb in range(DBLK)]
    C["h1"] = [Buf(h1big[:, m * TF:(m + 1) * TF]) for m in range(FBLK)]
    C["wst"] = [sbuf(nc, f"wst{i}", [P, FF // 2], F32) for i in range(2)]
    C["w2b"] = [sbuf(nc, f"w2b{i}", [P, FF], BF16) for i in range(2)]
    C["t3"] = [sbuf(nc, f"t3_{i}", [P, TF], I32) for i in range(2)]
    C["pwl"] = []
    for i in range(2):
        C["pwl"].append(dict(
            ti=sbuf(nc, f"pw_ti{i}", [P, TF], I32),
            uc=sbuf(nc, f"pw_uc{i}", [P, TF], F32),
            accp=sbuf(nc, f"pw_ap{i}", [P, TF], F32),
            accn=sbuf(nc, f"pw_an{i}", [P, TF], F32),
            hin=sbuf(nc, f"pw_hi{i}", [P, TF], F32),
        ))
    hb = sbuf(nc, "hingeb", [P, 15], F32)
    for k in range(1, 16):
        ad = abs(_PWL_DJ[k - 1])
        s.emit(v, lambda e, k=k, ad=ad: e.memset(hb[:, k - 1:k], -ad * 64.0 * k),
               writes=[hb])
    C["hingeb"] = hb
    def bias_col(name, h, n, scale=None):
        raw = sbuf(nc, name + "r", [P, n], F32)
        s.dma(raw.ap, h[:], writes=[raw])
        if scale is None:
            return raw
        sc = sbuf(nc, name + "c", [P, n], F32)
        s.emit(v, lambda e: e.tensor_scalar(sc.ap, raw.ap, scale, None, ALU.mult),
               reads=[raw], writes=[sc])
        return sc
    b1c = bias_col("b1", b1_h, FBLK, 256.0)
    b2c = bias_col("b2", b2_h, DBLK, 256.0)
    C["b1c"], C["b1c_buf"] = b1c.t, b1c
    C["b2c"], C["b2c_buf"] = b2c.t, b2c
    bzc = bias_col("bz", bz_h, DBLK)
    bxc = bias_col("bx", bx_h, DBLK)
    bdtc = bias_col("bdt", bdt_h, DBLK)
    boc = bias_col("bo", bo_h, DBLK)
    bb16 = sbuf(nc, "bbc", [NSTATE, 1], F32)
    s.dma(bb16.ap, bb_h[:], writes=[bb16])
    bc16 = sbuf(nc, "bcc", [NSTATE, 1], F32)
    s.dma(bc16.ap, bc_h[:], writes=[bc16])

    xs = [sbuf(nc, f"xs{kb}", [P, TF], F32) for kb in range(DBLK)]
    x2_d, ain_d, aout_d, yin_d, yout_d, z_d = (Buf(None) for _ in range(6))

    emit_w1_load(s, C, w1_h)
    for f in range(NT):
        sl = slice(f * TF, (f + 1) * TF)
        for kb in range(DBLK):
            s.dma(xs[kb].ap, x_h[kb * P:(kb + 1) * P, sl], writes=[xs[kb]])
        emit_ffn(s, C, xs, w2_h)
        for kb in range(DBLK):
            s.dma(x2_h[kb * P:(kb + 1) * P, sl], xs[kb].ap,
                  reads=[xs[kb]], writes=[x2_d])

    for name, h, off in (("wz", wz_h, 0), ("wx", wx_h, D), ("wdt", wdt_h, 2 * D)):
        for kb in range(DBLK):
            st = C["wst"][kb % 2]
            s.dma(st[:, 0:D], h[kb * P:(kb + 1) * P, :], writes=[st])
            s.emit(v, lambda e, kb=kb, st=st, off=off: e.tensor_copy(
                       C["w1b"][kb][:, off:off + D], st[:, 0:D]),
                   reads=[st], writes=[C["w1b"][kb]])
    wsmall = {}
    for name, h in (("wb", wb_h), ("wc", wc_h)):
        st = C["wst"][0]
        s.dma(st[:, 0:DBLK * NSTATE], h.rearrange("(k p) n -> p k n", p=P), writes=[st])
        wbv = sbuf(nc, f"{name}b", [P, DBLK * NSTATE], BF16)
        s.emit(v, lambda e, wbv=wbv, st=st: e.tensor_copy(wbv.ap, st[:, 0:DBLK * NSTATE]),
               reads=[st], writes=[wbv])
        wsmall[name] = wbv

    xssm = [Buf(h1big[:, i * 1024:(i + 1) * 1024].bitcast(F32)) for i in range(2)]
    dtt = [Buf(h1big[:, (2 + i) * 1024:(3 + i) * 1024].bitcast(F32)) for i in range(2)]
    pbt = [Buf(h1big[:, (4 + i) * 1024:(5 + i) * 1024].bitcast(F32)) for i in range(2)]
    sc1 = [Buf(h1big[:, (6 + i) * 1024:(7 + i) * 1024].bitcast(F32)) for i in range(2)]
    zt = sbuf(nc, "zt", [P, TF], BF16)
    sm16 = sbuf(nc, "sm16", [NSTATE, TF], F32)
    ps_bc = C["ps_rstd"]

    dtl_d, vvl_d, bml_d, ccl_d, gl_d, agin_d, agout_d = (Buf(None) for _ in range(7))
    for f in range(NT):
        sl = slice(f * TF, (f + 1) * TF)
        for kb in range(DBLK):
            s.dma(xs[kb].ap, x2_h[kb * P:(kb + 1) * P, sl], reads=[x2_d], writes=[xs[kb]])
        xn = C["xn"]
        emit_ln(s, C, xs, xn)
        for mo in range(DBLK):
            po = C["ps_o"][mo % 2]
            for kb in range(DBLK):
                s.mm(po, C["w1b"][kb], C["w1b"][kb][:, mo * P:(mo + 1) * P],
                     xn[kb], xn[kb].ap, start=(kb == 0), stop=(kb == DBLK - 1))
            s.emit(a, lambda e, po=po, mo=mo: e.activation(
                       zt.ap, po.ap, AF.Sigmoid, bias=bzc[:, mo:mo + 1]),
                   reads=[po, bzc], writes=[zt], psum_src=True)
            s.dma(z_h[mo * P:(mo + 1) * P, sl], zt.ap, reads=[zt], writes=[z_d])
        for mo in range(DBLK):
            i2 = mo % 2
            po = C["ps_o"][i2]
            for kb in range(DBLK):
                s.mm(po, C["w1b"][kb], C["w1b"][kb][:, 2 * D + mo * P:2 * D + (mo + 1) * P],
                     xn[kb], xn[kb].ap, start=(kb == 0), stop=(kb == DBLK - 1))
            s.emit(a, lambda e, po=po, mo=mo, i2=i2: e.activation(
                       pbt[i2].ap, po.ap, AF.Identity, bias=bdtc[:, mo:mo + 1]),
                   reads=[po, bdtc], writes=[pbt[i2]], psum_src=True)
            s.emit(v, lambda e, i2=i2: e.scalar_tensor_tensor(
                       sc1[i2].ap, pbt[i2].ap, -1.0, pbt[i2].ap, ALU.mult, ALU.max),
                   reads=[pbt[i2]], writes=[sc1[i2]])
            s.emit(a, lambda e, i2=i2: e.activation(sc1[i2].ap, sc1[i2].ap, AF.Exp,
                                                    scale=-1.0),
                   reads=[sc1[i2]], writes=[sc1[i2]])
            s.emit(a, lambda e, i2=i2: e.activation(sc1[i2].ap, sc1[i2].ap, AF.Ln,
                                                    bias=1.0),
                   reads=[sc1[i2]], writes=[sc1[i2]])
            s.emit(v, lambda e, i2=i2: e.scalar_tensor_tensor(
                       dtt[i2].ap, pbt[i2].ap, 0.0, sc1[i2].ap, ALU.max, ALU.add),
                   reads=[pbt[i2], sc1[i2]], writes=[dtt[i2]])
            s.dma(dtl_h[mo * P:(mo + 1) * P, sl], dtt[i2].ap, reads=[dtt[i2]], writes=[dtl_d])
            po2 = C["ps_h"][i2]
            for kb in range(DBLK):
                s.mm(po2, C["w1b"][kb], C["w1b"][kb][:, D + mo * P:D + (mo + 1) * P],
                     xn[kb], xn[kb].ap, start=(kb == 0), stop=(kb == DBLK - 1))
            s.emit(a, lambda e, po2=po2, mo=mo, i2=i2: e.activation(
                       xssm[i2].ap, po2.ap, AF.Identity, bias=bxc[:, mo:mo + 1]),
                   reads=[po2, bxc], writes=[xssm[i2]], psum_src=True)
            s.emit(v, lambda e, i2=i2: e.tensor_mul(xssm[i2].ap, xssm[i2].ap, dtt[i2].ap),
                   reads=[dtt[i2], xssm[i2]], writes=[xssm[i2]])
            s.dma(vvl_h[mo * P:(mo + 1) * P, sl], xssm[i2].ap, reads=[xssm[i2]], writes=[vvl_d])
        for name, out_h2, out_d2, bias in (("wb", bml_h, bml_d, bb16),
                                           ("wc", ccl_h, ccl_d, bc16)):
            wbv = wsmall[name]
            for kb in range(DBLK):
                s.mm(ps_bc, wbv, wbv[:, kb * NSTATE:(kb + 1) * NSTATE],
                     xn[kb], xn[kb].ap, start=(kb == 0), stop=(kb == DBLK - 1),
                     out_ap=ps_bc.t[0:NSTATE, :])
            s.emit(a, lambda e, bias=bias: e.activation(
                       sm16.ap, ps_bc.t[0:NSTATE, :], AF.Identity, bias=bias[:, 0:1]),
                   reads=[ps_bc, bias], writes=[sm16], psum_src=True)
            s.dma(out_h2[:, sl], sm16.ap, reads=[sm16], writes=[out_d2])

    rg = [list(core_ids)]
    acol = sbuf(nc, "acol", [P, DBLK * NSTATE], F32)
    s.dma(acol.ap, al_h.rearrange("(k p) n -> p k n", p=P), writes=[acol])
    s.emit(a, lambda e: e.activation(acol.ap, acol.ap, AF.Exp), reads=[acol], writes=[acol])
    s.emit(v, lambda e: e.tensor_scalar(acol.ap, acol.ap, -1.0, None, ALU.mult),
           reads=[acol], writes=[acol])
    yloc = [Buf(w1big[:, dj * 4096:(dj + 1) * 4096].bitcast(F32)) for dj in range(DBLK)]
    at = Buf(h1big[:, 0:4096].bitcast(F32))
    ut = Buf(h1big[:, 4096:8192].bitcast(F32))
    ht = Buf(h1big[:, 8192:12288].bitcast(F32))
    qt = Buf(h1big[:, 12288:16384].bitcast(F32))
    dtb = C["wst"][0]
    vvb = C["wst"][1]
    zerot = Buf(C["w2b"][0][:, 0:4096].bitcast(F32))
    rowB = [Buf(C["w2b"][0][0:1, 0:4096].bitcast(F32)),
            Buf(C["w2b"][0][32:33, 0:4096].bitcast(F32))]
    rowC = [Buf(C["w2b"][1][0:1, 0:4096].bitcast(F32)),
            Buf(C["w2b"][1][32:33, 0:4096].bitcast(F32))]
    rowbase = {id(rowB[0]): 0, id(rowB[1]): 32, id(rowC[0]): 0, id(rowC[1]): 32}
    ones128 = sbuf(nc, "ones128", [P, P], F32)
    s.emit(v, lambda e: e.memset(ones128.ap, 1.0), writes=[ones128])
    stot = sbuf(nc, "stot", [P, 1], F32)
    hendL = sbuf(nc, "hendL", [P, NSTATE], F32)
    wtotL = sbuf(nc, "wtotL", [P, NSTATE], F32)
    hinT = sbuf(nc, "hinT", [P, NSTATE], F32)
    e1T = sbuf(nc, "e1T", [P, NSTATE], F32)
    wjT = sbuf(nc, "wjT", [P, NSTATE], F32)
    hjT = sbuf(nc, "hjT", [P, NSTATE], F32)
    selm = sbuf(nc, "selmT", [P, NCORES], F32)
    s.dma(selm.ap, selm_h[:], writes=[selm])
    NPB = TC // TF

    def scan_block(dj, init_col, data1, y_first):
        a0 = dj * NSTATE
        for n in range(NSTATE):
            i2 = n % 2
            rC = rowC[i2]
            s.dma(rC.ap, ccl_h[n:n + 1, :], reads=[ccl_d], writes=[rC])
            if data1 is ut:
                rB = rowB[i2]
                s.dma(rB.ap, bml_h[n:n + 1, :], reads=[bml_d], writes=[rB])
            s.emit(a, lambda e, n=n, a0=a0: e.activation(
                       at.ap, dtb.ap, AF.Exp, scale=acol[:, a0 + n:a0 + n + 1]),
                   reads=[dtb, acol], writes=[at])
            if data1 is ut:
                rbb = rowbase[id(rB)]
                for p in range(NPB):
                    pb_ = C["ps_h"][p % 2]
                    s.mm(pb_, ones128, ones128[rbb:rbb + 1, :], rB,
                         rB[0:1, p * TF:(p + 1) * TF], start=True, stop=True)
                    s.emit(v, lambda e, p=p, pb_=pb_: e.tensor_tensor(
                               ut[:, p * TF:(p + 1) * TF],
                               vvb[:, p * TF:(p + 1) * TF], pb_.ap, ALU.mult),
                           reads=[vvb, pb_], writes=[ut], psum_src=True)
            s.emit(v, lambda e, n=n, ic=init_col(n): e.tensor_tensor_scan(
                       ht.ap, at.ap, data1.ap, ic, ALU.mult, ALU.add),
                   reads=[at, data1] + ([hinT] if init_col(n) is not None and data1 is zerot else []),
                   writes=[ht])
            if data1 is ut:
                s.emit(v, lambda e, n=n: e.tensor_copy(
                           hendL[:, n:n + 1], ht[:, TC - 1:TC]),
                       reads=[ht], writes=[hendL])
            rcb = rowbase[id(rC)]
            for p in range(NPB):
                pc_ = C["ps_o"][p % 2]
                s.mm(pc_, ones128, ones128[rcb:rcb + 1, :], rC,
                     rC[0:1, p * TF:(p + 1) * TF], start=True, stop=True)
                s.emit(v, lambda e, p=p, pc_=pc_: e.tensor_tensor(
                           qt[:, p * TF:(p + 1) * TF],
                           ht[:, p * TF:(p + 1) * TF], pc_.ap, ALU.mult),
                       reads=[ht, pc_], writes=[qt], psum_src=True)
            if y_first and n == 0:
                s.emit(v, lambda e, dj=dj: e.tensor_copy(yloc[dj].ap, qt.ap),
                       reads=[qt], writes=[yloc[dj]])
            else:
                s.emit(v, lambda e, dj=dj: e.tensor_add(yloc[dj].ap, yloc[dj].ap, qt.ap),
                       reads=[qt, yloc[dj]], writes=[yloc[dj]])

    for dj in range(DBLK):
        s.dma(dtb.ap, dtl_h[dj * P:(dj + 1) * P, :], reads=[dtl_d], writes=[dtb])
        s.dma(vvb.ap, vvl_h[dj * P:(dj + 1) * P, :], reads=[vvl_d], writes=[vvb])
        scan_block(dj, lambda n: 0.0, ut, y_first=True)
        s.emit(v, lambda e: e.tensor_reduce(stot.ap, dtb.ap, AX.X, ALU.add),
               reads=[dtb], writes=[stot])
        for n in range(NSTATE):
            s.emit(a, lambda e, n=n, dj=dj: e.activation(
                       wtotL[:, n:n + 1], stot.ap, AF.Exp,
                       scale=acol[:, dj * NSTATE + n:dj * NSTATE + n + 1]),
                   reads=[stot, acol], writes=[wtotL])
        s.dma(agin_h[dj * P:(dj + 1) * P, :], hendL.ap, reads=[hendL], writes=[agin_d])
        s.dma(agin_h[D + dj * P:D + (dj + 1) * P, :], wtotL.ap,
              reads=[wtotL], writes=[agin_d])

    s.emit(g, lambda e: e.collective_compute(
               "AllGather", ALU.bypass, replica_groups=rg,
               ins=[agin_h[:]], outs=[agout_h[:]]),
           reads=[agin_d], writes=[agout_d], dma=False)

    s.emit(v, lambda e: e.memset(zerot.ap, 0.0), writes=[zerot])
    for dj in range(DBLK):
        s.dma(dtb.ap, dtl_h[dj * P:(dj + 1) * P, :], reads=[dtl_d], writes=[dtb])
        s.emit(v, lambda e: e.memset(hinT.ap, 0.0), writes=[hinT])
        for j in range(NCORES):
            base = j * 2 * D
            s.dma(hjT.ap, agout_h[base + dj * P:base + (dj + 1) * P, :],
                  reads=[agout_d], writes=[hjT])
            s.dma(wjT.ap, agout_h[base + D + dj * P:base + D + (dj + 1) * P, :],
                  reads=[agout_d], writes=[wjT])
            s.emit(v, lambda e: e.tensor_mul(e1T.ap, wjT.ap, hinT.ap),
                   reads=[wjT, hinT], writes=[e1T])
            s.emit(v, lambda e: e.tensor_add(e1T.ap, e1T.ap, hjT.ap),
                   reads=[e1T, hjT], writes=[e1T])
            s.emit(v, lambda e: e.tensor_sub(e1T.ap, e1T.ap, hinT.ap),
                   reads=[e1T, hinT], writes=[e1T])
            s.emit(v, lambda e, j=j: e.scalar_tensor_tensor(
                       hinT.ap, e1T.ap, selm[:, j:j + 1], hinT.ap, ALU.mult, ALU.add),
                   reads=[e1T, selm, hinT], writes=[hinT])
        scan_block(dj, lambda n: hinT[:, n:n + 1], zerot, y_first=False)
        zview = Buf(C["w2b"][1][0:P, 0:2048])
        gview = Buf(C["w2b"][1][0:P, 2048:4096])
        s.dma(zview.ap, z_h[dj * P:(dj + 1) * P, :], reads=[z_d], writes=[zview])
        s.emit(v, lambda e, dj=dj: e.tensor_tensor(
                   gview.ap, yloc[dj].ap, zview.ap, ALU.mult),
               reads=[yloc[dj], zview], writes=[gview])
        s.dma(gl_h[dj * P:(dj + 1) * P, :], gview.ap, reads=[gview], writes=[gl_d])

    emit_w1_load(s, C, w1_h)
    gt = C["h1"][:DBLK]
    xo_d = Buf(None)
    for f in range(NT):
        sl = slice(f * TF, (f + 1) * TF)
        for kb in range(DBLK):
            s.dma(xs[kb].ap, x2_h[kb * P:(kb + 1) * P, sl], reads=[x2_d], writes=[xs[kb]])
        for kb in range(DBLK):
            st = C["wst"][kb % 2]
            s.dma(st[:, 0:D], wo_h[kb * P:(kb + 1) * P, :], writes=[st])
            s.emit(v, lambda e, kb=kb, st=st: e.tensor_copy(
                       C["w2b"][0][:, kb * 512:kb * 512 + 512], st[:, 0:512]),
                   reads=[st], writes=[C["w2b"][0]])
            s.emit(v, lambda e, kb=kb, st=st: e.tensor_copy(
                       C["w2b"][1][:, kb * 512:kb * 512 + 512], st[:, 512:D]),
                   reads=[st], writes=[C["w2b"][1]])
        for kb in range(DBLK):
            s.dma(gt[kb].ap, gl_h[kb * P:(kb + 1) * P, sl], reads=[gl_d], writes=[gt[kb]])
        for mo in range(DBLK):
            po = C["ps_o"][mo % 2]
            half, off = divmod(mo * P, 512)
            wob = C["w2b"][half]
            for kb in range(DBLK):
                s.mm(po, wob, wob[:, kb * 512 + off:kb * 512 + off + P],
                     gt[kb], gt[kb].ap, start=(kb == 0), stop=(kb == DBLK - 1))
            s.emit(v, lambda e, po=po, mo=mo: e.scalar_tensor_tensor(
                       xs[mo].ap, po.ap, boc[:, mo:mo + 1], xs[mo].ap, ALU.add, ALU.add),
                   reads=[po, boc, xs[mo]], writes=[xs[mo]], psum_src=True)
        emit_ffn(s, C, xs, w2_h)
        for kb in range(DBLK):
            s.dma(xo_h[kb * P:(kb + 1) * P, sl], xs[kb].ap, reads=[xs[kb]], writes=[xo_d])

    s.emit("sync", lambda e: e.nop(), deps=[("sync", s.val["sync"])])
    s.finalize()
    return nc


def _col(b):
    n = b.shape[0] // P
    return np.ascontiguousarray(b.reshape(n, P).T)


def kernel(**inputs):
    xf = np.asarray(inputs["L0_attn_x"], np.float32).reshape(B * L, D)
    shared = dict(
        w1=np.ascontiguousarray(np.asarray(inputs["L0_ffn_w1"], np.float32)),
        b1=_col(np.asarray(inputs["L0_ffn_b1"], np.float32)),
        w2=np.ascontiguousarray(np.asarray(inputs["L0_ffn_w2"], np.float32)),
        b2=_col(np.asarray(inputs["L0_ffn_b2"], np.float32)),
        wz=np.ascontiguousarray(np.asarray(inputs["L0_attn_Wz"], np.float32)),
        wx=np.ascontiguousarray(np.asarray(inputs["L0_attn_Wx"], np.float32)),
        wdt=np.ascontiguousarray(np.asarray(inputs["L0_attn_Wdt"], np.float32)),
        bz=_col(np.asarray(inputs["L0_attn_bz"], np.float32)),
        bx=_col(np.asarray(inputs["L0_attn_bx"], np.float32)),
        bdt=_col(np.asarray(inputs["L0_attn_bdt"], np.float32)),
        wb=np.ascontiguousarray(np.asarray(inputs["L0_attn_WB"], np.float32)),
        wc=np.ascontiguousarray(np.asarray(inputs["L0_attn_WC"], np.float32)),
        bb=np.ascontiguousarray(np.asarray(inputs["L0_attn_bB"], np.float32).reshape(NSTATE, 1)),
        bc=np.ascontiguousarray(np.asarray(inputs["L0_attn_bC"], np.float32).reshape(NSTATE, 1)),
        wo=np.ascontiguousarray(np.asarray(inputs["L0_attn_Wout"], np.float32)),
        bo=_col(np.asarray(inputs["L0_attn_bout"], np.float32)),
    )
    alog = np.asarray(inputs["L0_attn_A_log"], np.float32)
    core_ids = list(range(NCORES))
    nc = build_fused(core_ids)
    maps = []
    for i in range(NCORES):
        m = dict(shared)
        m["x"] = np.ascontiguousarray(xf[i * TC:(i + 1) * TC, :].T)
        m["alog"] = np.ascontiguousarray(alog)
        sel = np.zeros((P, NCORES), np.float32)
        for j in range(NCORES):
            if j // 4 == i // 4 and j % 4 < i % 4:
                sel[:, j] = 1.0
        m["selm"] = sel
        maps.append(m)
    res = run_bass_kernel_spmd(nc, maps, core_ids=core_ids)
    out = np.empty((B * L, D), np.float32)
    for i in range(NCORES):
        out[i * TC:(i + 1) * TC, :] = np.asarray(res.results[i]["xo"], np.float32).T
    return out.reshape(B, L, D)


# version 06
# speedup vs baseline: 1.2844x; 1.2510x over previous
import math
from contextlib import ExitStack

import numpy as np
import ml_dtypes

import concourse.bass as bass
import concourse.mybir as mybir
from concourse.bass_utils import run_bass_kernel_spmd

F32 = mybir.dt.float32
I32 = mybir.dt.int32
BF16 = mybir.dt.bfloat16
AF = mybir.ActivationFunctionType
ALU = mybir.AluOpType
AX = mybir.AxisListType

P = 128
D = 1024
FF = 4096
NSTATE = 16
B = 2
L = 8192
NCORES = 8
TC = B * L // NCORES
DBLK = D // P
FBLK = FF // P
TF = 512
NT = TC // TF
TP2 = 2048
EPS = 1e-5

SERIALIZE = True
CHAIN_EXEMPT = set()
ISSUE_CHAIN = True
DEBUG_TAPS = False


def _gelu64(x):
    x = np.asarray(x, np.float64)
    return 0.5 * x * (1.0 + np.vectorize(math.erf)(x / math.sqrt(2.0)))

def _build_pwl():
    X = np.linspace(-4.0, 4.0, 33)
    Y = _gelu64(X)
    j = np.arange(17)
    qv = 256.0 * Y[16 - j]
    g = (qv[1:] - qv[:-1]) / 64.0
    g0 = g[0]
    dj = g[1:] - g[:-1]
    t = np.arange(-40000, 40001, dtype=np.float64)
    uc = np.minimum(np.abs(t), 1024.0)
    q = g0 * uc
    for k in range(1, 16):
        q = q + dj[k - 1] * np.maximum(uc - 64.0 * k, 0.0)
    mine = np.maximum(t, 0.0) + np.round(q)
    xq = t / 256.0
    y = np.interp(xq, X, Y)
    y = np.where(xq > 4.0, xq, np.where(xq < -4.0, 0.0, y))
    ref = np.round(256.0 * y)
    assert np.array_equal(mine, ref), "PWL hinge table does not match reference"
    return float(g0), [float(v) for v in dj]

_PWL_G0, _PWL_DJ = _build_pwl()


class Buf:
    __slots__ = ("t", "last_write", "readers")
    def __init__(self, t):
        self.t = t
        self.last_write = None
        self.readers = []
    def __getitem__(self, sl):
        return self.t[sl]
    @property
    def ap(self):
        return self.t[:]


class Sch:
    ENGS = ("sync", "vector", "scalar", "gpsimd", "tensor")
    SEMS = ENGS + ("syncI",)

    def __init__(self, nc):
        self.nc = nc
        self.ops = {e: [] for e in self.ENGS}
        self.val = {e: 0 for e in self.SEMS}
        self.es = ExitStack()
        self.sem = {e: self.es.enter_context(nc.semaphore(f"s_{e}")) for e in self.SEMS}
        self.prev = None

    def emit(self, eng, fn, reads=(), writes=(), deps=(), dma=False, psum_src=False,
             ccomp=False):
        dl = list(deps)
        if SERIALIZE and eng not in CHAIN_EXEMPT \
                and self.prev is not None and self.prev[0] != eng:
            dl.append(self.prev)
        for b in reads:
            if b.last_write is not None:
                dl.append(b.last_write)
        for b in writes:
            if b.last_write is not None:
                dl.append(b.last_write)
            dl.extend(b.readers)
        inc = 16 if dma else 1
        self.val[eng] += inc
        tok = (eng, self.val[eng])
        self.ops[eng].append((fn, dl, inc, psum_src, None))
        if dma and ISSUE_CHAIN and not ccomp:
            self.val["syncI"] += 1
            itok = ("syncI", self.val["syncI"])
            self.ops[eng].append(
                (lambda e: e.nop(cycle_cnt=0, nofuse=True), [], 1, False, "syncI"))
            if eng not in CHAIN_EXEMPT:
                self.prev = itok
        elif eng not in CHAIN_EXEMPT:
            self.prev = tok
        for b in reads:
            b.readers.append(tok)
        for b in writes:
            b.last_write = tok
            b.readers = []
        return tok

    def mm(self, ps, lhsT_buf, lhsT_ap, rhs_buf, rhs_ap, start, stop, out_ap=None):
        dl = []
        if SERIALIZE and "tensor" not in CHAIN_EXEMPT \
                and self.prev is not None and self.prev[0] != "tensor":
            dl.append(self.prev)
        if lhsT_buf is not None and lhsT_buf.last_write is not None:
            dl.append(lhsT_buf.last_write)
        if rhs_buf is not None and rhs_buf.last_write is not None:
            dl.append(rhs_buf.last_write)
        if start:
            if ps.last_write is not None:
                dl.append(ps.last_write)
            dl.extend(ps.readers)
        self.val["tensor"] += 1
        tok = ("tensor", self.val["tensor"])
        if "tensor" not in CHAIN_EXEMPT:
            self.prev = tok
        oap = ps.ap if out_ap is None else out_ap
        self.ops["tensor"].append(
            (lambda e, oap=oap, la=lhsT_ap, ra=rhs_ap, st=start, sp=stop:
             e.matmul(oap, la, ra, start=st, stop=sp), dl, 1, False, None))
        if lhsT_buf is not None:
            lhsT_buf.readers.append(tok)
        if rhs_buf is not None:
            rhs_buf.readers.append(tok)
        if stop:
            ps.last_write = tok
            ps.readers = []
        return tok

    def dma(self, dst_ap, src_ap, reads=(), writes=(), deps=(), eng="sync",
            ccomp=False):
        import numpy as _np
        assert _np.prod(dst_ap.shape) == _np.prod(src_ap.shape), (
            f"dma size mismatch {dst_ap.shape} <- {src_ap.shape}")
        return self.emit(
            eng, lambda e: e.dma_start(out=dst_ap, in_=src_ap),
            reads=reads, writes=writes, deps=deps, dma=True, ccomp=ccomp)

    def finalize(self):
        nc = self.nc
        with nc.Block() as block:
            for eng in self.ENGS:
                if not self.ops[eng]:
                    continue
                def make(eng):
                    def body(e):
                        water = {s2: 0 for s2 in self.SEMS}
                        for fn, deps, inc, psrc, semov in self.ops[eng]:
                            need = {}
                            for (de, dv) in deps:
                                if dv > water[de]:
                                    need[de] = max(need.get(de, 0), dv)
                            waited_pe = False
                            for de, dv in need.items():
                                e.wait_ge(self.sem[de], dv)
                                water[de] = dv
                                if de == "tensor":
                                    waited_pe = True
                            if psrc and waited_pe:
                                e.nop(cycle_cnt=192, nofuse=True)
                            ins = fn(e)
                            ins.then_inc(self.sem[semov or eng], inc)
                    return body
                getattr(block, eng)(make(eng))
        self.es.close()


def sbuf(nc, name, shape, dtype):
    return Buf(nc.alloc_sbuf_tensor(name, shape, dtype))

def psum(nc, name, shape, dtype=F32):
    return Buf(nc.alloc_psum_tensor(name, shape, dtype))



def emit_ln(s, C, xblk, xn_out):
    v, a = "vector", "scalar"
    ones_col, ones_row = C["ones_col"], C["ones_row"]
    ps_s1, ps_s2, ps_r, ps_m = C["ps_s1"], C["ps_s2"], C["ps_rstd"], C["ps_mmr"]
    st_mean, st_rstd, st_mmr = C["st_mean"], C["st_rstd"], C["st_mmr"]
    tmp_sq, tmp_sq2, tmp32 = C["tmp_sq"], C["tmp_sq2"], C["tmp32"]
    for kb in range(DBLK):
        s.emit(v, lambda e, kb=kb: e.tensor_copy(tmp_sq.ap, xblk[kb].ap),
               reads=[xblk[kb]], writes=[tmp_sq])
        s.mm(ps_s1, ones_col, ones_col.ap, tmp_sq, tmp_sq.ap,
             start=(kb == 0), stop=(kb == DBLK - 1))
        s.emit(a, lambda e, kb=kb: e.activation(tmp_sq2.ap, xblk[kb].ap, AF.Square),
               reads=[xblk[kb]], writes=[tmp_sq2])
        s.mm(ps_s2, ones_col, ones_col.ap, tmp_sq2, tmp_sq2.ap,
             start=(kb == 0), stop=(kb == DBLK - 1))
    s.emit(v, lambda e: e.tensor_scalar(st_mean.ap, ps_s1.ap, 1.0 / D, None, ALU.mult),
           reads=[ps_s1], writes=[st_mean], psum_src=True)
    s.emit(v, lambda e: e.tensor_mul(st_rstd.ap, st_mean.ap, st_mean.ap),
           reads=[st_mean], writes=[st_rstd])
    s.emit(v, lambda e: e.scalar_tensor_tensor(st_rstd.ap, ps_s2.ap, 1.0 / D,
                                               st_rstd.ap, ALU.mult, ALU.subtract),
           reads=[ps_s2, st_rstd], writes=[st_rstd], psum_src=True)
    s.emit(a, lambda e: e.activation(st_rstd.ap, st_rstd.ap, AF.Ln,
                                     bias=C["eps1"][0:1, 0:1]),
           reads=[st_rstd, C["eps1"]], writes=[st_rstd])
    s.emit(a, lambda e: e.activation(st_rstd.ap, st_rstd.ap, AF.Exp, scale=-0.5),
           reads=[st_rstd], writes=[st_rstd])
    s.emit(v, lambda e: e.tensor_mul(st_mmr.ap, st_mean.ap, st_rstd.ap),
           reads=[st_mean, st_rstd], writes=[st_mmr])
    s.mm(ps_r, ones_row, ones_row.ap, st_rstd, st_rstd.ap, start=True, stop=True)
    s.mm(ps_m, ones_row, ones_row.ap, st_mmr, st_mmr.ap, start=True, stop=True)
    for kb in range(DBLK):
        s.emit(v, lambda e, kb=kb: e.tensor_mul(tmp32.ap, xblk[kb].ap, ps_r.ap),
               reads=[xblk[kb], ps_r], writes=[tmp32], psum_src=True)
        s.emit(v, lambda e, kb=kb: e.tensor_sub(xn_out[kb].ap, tmp32.ap, ps_m.ap),
               reads=[tmp32, ps_m], writes=[xn_out[kb]], psum_src=True)


def emit_pwl4(s, C, g0):
    v, a = "vector", "scalar"
    ti4, uc4, ac4, ht4 = C["ti4"], C["uc4"], C["ac4"], C["ht4"]
    for i in range(4):
        m = g0 + i
        ph = C["ps4"][i]
        s.emit(v, lambda e, ph=ph, m=m, i=i: e.tensor_scalar(
                   ti4[:, i * TF:(i + 1) * TF], ph.ap, 256.0, C["b1c"][:, m:m + 1],
                   ALU.mult, ALU.add),
               reads=[ph, C["b1c_buf"]], writes=[ti4], psum_src=True)
    s.emit(v, lambda e: e.scalar_tensor_tensor(uc4.ap, ti4.ap, -1.0, ti4.ap,
                                               ALU.mult, ALU.max),
           reads=[ti4], writes=[uc4])
    s.emit(v, lambda e: e.tensor_scalar(uc4.ap, uc4.ap, 1024.0, None, ALU.min),
           reads=[uc4], writes=[uc4])
    s.emit(v, lambda e: e.tensor_scalar(ac4.ap, uc4.ap, _PWL_G0, None, ALU.mult),
           reads=[uc4], writes=[ac4])
    s.emit(v, lambda e: e.scalar_tensor_tensor(ac4.ap, ti4.ap, 0.0, ac4.ap,
                                               ALU.max, ALU.add),
           reads=[ti4, ac4], writes=[ac4])
    for k0 in range(1, 16, 2):
        ks = [k for k in (k0, k0 + 1) if k <= 15]
        for j, k in enumerate(ks):
            ad = abs(_PWL_DJ[k - 1])
            s.emit(a, lambda e, ad=ad, k=k, j=j: e.activation(
                       ht4[j].ap, uc4.ap, AF.Relu, bias=C["hingeb"][:, k - 1:k],
                       scale=ad),
                   reads=[uc4, C["hingeb"]], writes=[ht4[j]])
        for j, k in enumerate(ks):
            op = ALU.add if _PWL_DJ[k - 1] >= 0 else ALU.subtract
            s.emit(v, lambda e, j=j, op=op: e.tensor_tensor(
                       ac4.ap, ac4.ap, ht4[j].ap, op),
                   reads=[ht4[j], ac4], writes=[ac4])
    s.emit(v, lambda e: e.tensor_scalar(ti4.ap, ac4.ap, 1.0, None, ALU.mult),
           reads=[ac4], writes=[ti4])
    hw = [C["h1"][g0 + i] for i in range(4)]
    s.emit(v, lambda e, g0=g0: e.tensor_copy(
               C["h1big"][:, g0 * TF:(g0 + 4) * TF], ti4.ap),
           reads=[ti4], writes=hw)


def emit_ffn(s, C, xblk, w2_h):
    v = "vector"
    xn = C["xn"]
    emit_ln(s, C, xblk, xn)
    h1 = C["h1"]
    for g0 in range(0, FBLK, 4):
        for i in range(4):
            m = g0 + i
            ph = C["ps4"][i]
            for kb in range(DBLK):
                s.mm(ph, C["w1b"][kb], C["w1b"][kb][:, m * P:(m + 1) * P],
                     xn[kb], xn[kb].ap, start=(kb == 0), stop=(kb == DBLK - 1))
        emit_pwl4(s, C, g0)
    w2r = w2_h.rearrange("(m p) o -> p m o", p=P)
    for mo in range(DBLK):
        w2b = C["w2b"][mo % 2]
        for h in range(2):
            w2s = C["wst"][h]
            hm = FBLK // 2
            s.dma(w2s.ap, w2r[:, h * hm:(h + 1) * hm, mo * P:(mo + 1) * P], writes=[w2s])
            s.emit(v, lambda e, w2b=w2b, w2s=w2s, h=h, hm=hm: e.tensor_copy(
                       w2b[:, h * hm * P:(h + 1) * hm * P], w2s.ap),
                   reads=[w2s], writes=[w2b])
        po = C["ps_o"][mo % 2]
        for m in range(FBLK):
            s.mm(po, w2b, w2b[:, m * P:(m + 1) * P], h1[m], h1[m].ap,
                 start=(m == 0), stop=(m == FBLK - 1))
        t3 = C["t3"][mo % 2]
        s.emit(v, lambda e, t3=t3, po=po, mo=mo: e.tensor_scalar(
                   t3.ap, po.ap, C["b2c"][:, mo:mo + 1], None, ALU.add),
               reads=[po, C["b2c_buf"]], writes=[t3], psum_src=True)
        s.emit(v, lambda e, t3=t3, mo=mo: e.scalar_tensor_tensor(
                   xblk[mo].ap, t3.ap, 1.0 / 512.0, xblk[mo].ap, ALU.mult, ALU.add),
               reads=[t3, xblk[mo]], writes=[xblk[mo]])


def emit_w1_load(s, C, w1_h):
    v = "vector"
    for kb in range(DBLK):
        for h in range(2):
            st = C["wst"][(kb * 2 + h) % 2]
            sl = slice(h * (FF // 2), (h + 1) * (FF // 2))
            s.dma(st.ap, w1_h[kb * P:(kb + 1) * P, sl], writes=[st])
            s.emit(v, lambda e, kb=kb, st=st, sl=sl: e.tensor_copy(
                       C["w1b"][kb][:, sl], st.ap),
                   reads=[st], writes=[C["w1b"][kb]])



def build_fused(core_ids):
    nc = bass.Bass("TRN2", target_bir_lowering=False, debug=False)
    x_h = nc.dram_tensor("x", [D, TC], F32, kind="ExternalInput")
    w1_h = nc.dram_tensor("w1", [D, FF], F32, kind="ExternalInput")
    b1_h = nc.dram_tensor("b1", [P, FBLK], F32, kind="ExternalInput")
    w2_h = nc.dram_tensor("w2", [FF, D], F32, kind="ExternalInput")
    b2_h = nc.dram_tensor("b2", [P, DBLK], F32, kind="ExternalInput")
    wz_h = nc.dram_tensor("wz", [D, D], F32, kind="ExternalInput")
    wx_h = nc.dram_tensor("wx", [D, D], F32, kind="ExternalInput")
    wdt_h = nc.dram_tensor("wdt", [D, D], F32, kind="ExternalInput")
    bz_h = nc.dram_tensor("bz", [P, DBLK], F32, kind="ExternalInput")
    bx_h = nc.dram_tensor("bx", [P, DBLK], F32, kind="ExternalInput")
    bdt_h = nc.dram_tensor("bdt", [P, DBLK], F32, kind="ExternalInput")
    wb_h = nc.dram_tensor("wb", [D, NSTATE], F32, kind="ExternalInput")
    wc_h = nc.dram_tensor("wc", [D, NSTATE], F32, kind="ExternalInput")
    bb_h = nc.dram_tensor("bb", [NSTATE, 1], F32, kind="ExternalInput")
    bc_h = nc.dram_tensor("bc", [NSTATE, 1], F32, kind="ExternalInput")
    al_h = nc.dram_tensor("alog", [D, NSTATE], F32, kind="ExternalInput")
    wo_h = nc.dram_tensor("wo", [D, D], F32, kind="ExternalInput")
    bo_h = nc.dram_tensor("bo", [P, DBLK], F32, kind="ExternalInput")
    xo_h = nc.dram_tensor("xo", [D, TC], F32, kind="ExternalOutput")
    KI = "ExternalOutput" if DEBUG_TAPS else "Internal"
    dtl_h = nc.dram_tensor("dtl", [D, TC], F32, kind=KI)
    vvl_h = nc.dram_tensor("vvl", [D, TC], F32, kind=KI)
    bml_h = nc.dram_tensor("bml", [NSTATE, TC], F32, kind=KI)
    ccl_h = nc.dram_tensor("ccl", [NSTATE, TC], F32, kind=KI)
    gl_h = nc.dram_tensor("gl", [D, TC], BF16, kind=KI)
    agin_h = nc.dram_tensor("agin", [2 * D, NSTATE], F32, kind="Internal")
    agout_h = nc.dram_tensor("agout", [NCORES * 2 * D, NSTATE], F32, kind="Internal")
    x2_h = nc.dram_tensor("x2i", [D, TC], F32, kind=KI)
    z_h = nc.dram_tensor("zi", [D, TC], BF16, kind=KI)
    selm_h = nc.dram_tensor("selm", [P, NCORES], F32, kind="ExternalInput")

    s = Sch(nc)
    v, a, g, t = "vector", "scalar", "gpsimd", "tensor"

    C = {}
    C["ones_col"] = sbuf(nc, "ones_col", [P, 1], BF16)
    C["ones_row"] = sbuf(nc, "ones_row", [1, P], F32)
    s.emit(v, lambda e: e.memset(C["ones_col"].ap, 1.0), writes=[C["ones_col"]])
    s.emit(v, lambda e: e.memset(C["ones_row"].ap, 1.0), writes=[C["ones_row"]])
    C["eps1"] = sbuf(nc, "eps1", [1, 1], F32)
    s.emit(v, lambda e: e.memset(C["eps1"].ap, EPS), writes=[C["eps1"]])
    C["ps_s1"] = psum(nc, "ps_s1", [1, TF])
    C["ps_s2"] = psum(nc, "ps_s2", [1, TF])
    C["ps_rstd"] = psum(nc, "ps_rstd", [P, TF])
    C["ps_mmr"] = psum(nc, "ps_mmr", [P, TF])
    C["ps_h"] = [psum(nc, f"ps_h{i}", [P, TF]) for i in range(2)]
    C["ps_o"] = [psum(nc, f"ps_o{i}", [P, TF]) for i in range(2)]
    C["st_mean"] = sbuf(nc, "st_mean", [1, TF], F32)
    C["st_rstd"] = sbuf(nc, "st_rstd", [1, TF], F32)
    C["st_mmr"] = sbuf(nc, "st_mmr", [1, TF], F32)
    C["tmp_sq"] = sbuf(nc, "tmp_sq", [P, TF], BF16)
    C["tmp_sq2"] = sbuf(nc, "tmp_sq2", [P, TF], BF16)
    C["tmp32"] = sbuf(nc, "tmp32", [P, TF], F32)
    C["xn"] = [sbuf(nc, f"xn{kb}", [P, TF], BF16) for kb in range(DBLK)]
    w1big = nc.alloc_sbuf_tensor("w1big", [P, DBLK * FF], BF16)
    h1big = nc.alloc_sbuf_tensor("h1big", [P, FBLK * TF], BF16)
    C["w1b"] = [Buf(w1big[:, kb * FF:(kb + 1) * FF]) for kb in range(DBLK)]
    C["h1"] = [Buf(h1big[:, m * TF:(m + 1) * TF]) for m in range(FBLK)]
    C["wst"] = [sbuf(nc, f"wst{i}", [P, FF // 2], F32) for i in range(2)]
    C["w2b"] = [sbuf(nc, f"w2b{i}", [P, FF], BF16) for i in range(2)]
    C["t3"] = [sbuf(nc, f"t3_{i}", [P, TF], I32) for i in range(2)]
    C["ti4"] = sbuf(nc, "pw_ti4", [P, 4 * TF], I32)
    C["uc4"] = sbuf(nc, "pw_uc4", [P, 4 * TF], F32)
    C["ac4"] = sbuf(nc, "pw_ac4", [P, 4 * TF], F32)
    C["ht4"] = [Buf(C["w2b"][i].t[:, 0:FF].bitcast(F32)) for i in range(2)]
    C["ps4"] = [C["ps_h"][0], C["ps_h"][1], C["ps_o"][0], C["ps_o"][1]]
    C["h1big"] = h1big
    hb = sbuf(nc, "hingeb", [P, 15], F32)
    for k in range(1, 16):
        ad = abs(_PWL_DJ[k - 1])
        s.emit(v, lambda e, k=k, ad=ad: e.memset(hb[:, k - 1:k], -ad * 64.0 * k),
               writes=[hb])
    C["hingeb"] = hb
    def bias_col(name, h, n, scale=None):
        raw = sbuf(nc, name + "r", [P, n], F32)
        s.dma(raw.ap, h[:], writes=[raw])
        if scale is None:
            return raw
        sc = sbuf(nc, name + "c", [P, n], F32)
        s.emit(v, lambda e: e.tensor_scalar(sc.ap, raw.ap, scale, None, ALU.mult),
               reads=[raw], writes=[sc])
        return sc
    b1c = bias_col("b1", b1_h, FBLK, 256.0)
    b2c = bias_col("b2", b2_h, DBLK, 256.0)
    C["b1c"], C["b1c_buf"] = b1c.t, b1c
    C["b2c"], C["b2c_buf"] = b2c.t, b2c
    bzc = bias_col("bz", bz_h, DBLK)
    bxc = bias_col("bx", bx_h, DBLK)
    bdtc = bias_col("bdt", bdt_h, DBLK)
    boc = bias_col("bo", bo_h, DBLK)
    bb16 = sbuf(nc, "bbc", [NSTATE, 1], F32)
    s.dma(bb16.ap, bb_h[:], writes=[bb16])
    bc16 = sbuf(nc, "bcc", [NSTATE, 1], F32)
    s.dma(bc16.ap, bc_h[:], writes=[bc16])

    xs = [sbuf(nc, f"xs{kb}", [P, TF], F32) for kb in range(DBLK)]
    x2_d, ain_d, aout_d, yin_d, yout_d, z_d = (Buf(None) for _ in range(6))

    emit_w1_load(s, C, w1_h)
    for f in range(NT):
        sl = slice(f * TF, (f + 1) * TF)
        for kb in range(DBLK):
            s.dma(xs[kb].ap, x_h[kb * P:(kb + 1) * P, sl], writes=[xs[kb]])
        emit_ffn(s, C, xs, w2_h)
        for kb in range(DBLK):
            s.dma(x2_h[kb * P:(kb + 1) * P, sl], xs[kb].ap,
                  reads=[xs[kb]], writes=[x2_d])

    for name, h, off in (("wz", wz_h, 0), ("wx", wx_h, D), ("wdt", wdt_h, 2 * D)):
        for kb in range(DBLK):
            st = C["wst"][kb % 2]
            s.dma(st[:, 0:D], h[kb * P:(kb + 1) * P, :], writes=[st])
            s.emit(v, lambda e, kb=kb, st=st, off=off: e.tensor_copy(
                       C["w1b"][kb][:, off:off + D], st[:, 0:D]),
                   reads=[st], writes=[C["w1b"][kb]])
    wsmall = {}
    for name, h in (("wb", wb_h), ("wc", wc_h)):
        st = C["wst"][0]
        s.dma(st[:, 0:DBLK * NSTATE], h.rearrange("(k p) n -> p k n", p=P), writes=[st])
        wbv = sbuf(nc, f"{name}b", [P, DBLK * NSTATE], BF16)
        s.emit(v, lambda e, wbv=wbv, st=st: e.tensor_copy(wbv.ap, st[:, 0:DBLK * NSTATE]),
               reads=[st], writes=[wbv])
        wsmall[name] = wbv

    xssm = [Buf(h1big[:, i * 1024:(i + 1) * 1024].bitcast(F32)) for i in range(2)]
    dtt = [Buf(h1big[:, (2 + i) * 1024:(3 + i) * 1024].bitcast(F32)) for i in range(2)]
    pbt = [Buf(h1big[:, (4 + i) * 1024:(5 + i) * 1024].bitcast(F32)) for i in range(2)]
    sc1 = [Buf(h1big[:, (6 + i) * 1024:(7 + i) * 1024].bitcast(F32)) for i in range(2)]
    zt = sbuf(nc, "zt", [P, TF], BF16)
    sm16 = sbuf(nc, "sm16", [NSTATE, TF], F32)
    ps_bc = C["ps_rstd"]

    dtl_d, vvl_d, bml_d, ccl_d, gl_d, agin_d, agout_d = (Buf(None) for _ in range(7))
    for f in range(NT):
        sl = slice(f * TF, (f + 1) * TF)
        for kb in range(DBLK):
            s.dma(xs[kb].ap, x2_h[kb * P:(kb + 1) * P, sl], reads=[x2_d], writes=[xs[kb]])
        xn = C["xn"]
        emit_ln(s, C, xs, xn)
        for mo in range(DBLK):
            po = C["ps_o"][mo % 2]
            for kb in range(DBLK):
                s.mm(po, C["w1b"][kb], C["w1b"][kb][:, mo * P:(mo + 1) * P],
                     xn[kb], xn[kb].ap, start=(kb == 0), stop=(kb == DBLK - 1))
            s.emit(a, lambda e, po=po, mo=mo: e.activation(
                       zt.ap, po.ap, AF.Sigmoid, bias=bzc[:, mo:mo + 1]),
                   reads=[po, bzc], writes=[zt], psum_src=True)
            s.dma(z_h[mo * P:(mo + 1) * P, sl], zt.ap, reads=[zt], writes=[z_d])
        for mo in range(DBLK):
            i2 = mo % 2
            po = C["ps_o"][i2]
            for kb in range(DBLK):
                s.mm(po, C["w1b"][kb], C["w1b"][kb][:, 2 * D + mo * P:2 * D + (mo + 1) * P],
                     xn[kb], xn[kb].ap, start=(kb == 0), stop=(kb == DBLK - 1))
            s.emit(a, lambda e, po=po, mo=mo, i2=i2: e.activation(
                       pbt[i2].ap, po.ap, AF.Identity, bias=bdtc[:, mo:mo + 1]),
                   reads=[po, bdtc], writes=[pbt[i2]], psum_src=True)
            s.emit(v, lambda e, i2=i2: e.scalar_tensor_tensor(
                       sc1[i2].ap, pbt[i2].ap, -1.0, pbt[i2].ap, ALU.mult, ALU.max),
                   reads=[pbt[i2]], writes=[sc1[i2]])
            s.emit(a, lambda e, i2=i2: e.activation(sc1[i2].ap, sc1[i2].ap, AF.Exp,
                                                    scale=-1.0),
                   reads=[sc1[i2]], writes=[sc1[i2]])
            s.emit(a, lambda e, i2=i2: e.activation(sc1[i2].ap, sc1[i2].ap, AF.Ln,
                                                    bias=1.0),
                   reads=[sc1[i2]], writes=[sc1[i2]])
            s.emit(v, lambda e, i2=i2: e.scalar_tensor_tensor(
                       dtt[i2].ap, pbt[i2].ap, 0.0, sc1[i2].ap, ALU.max, ALU.add),
                   reads=[pbt[i2], sc1[i2]], writes=[dtt[i2]])
            s.dma(dtl_h[mo * P:(mo + 1) * P, sl], dtt[i2].ap, reads=[dtt[i2]], writes=[dtl_d])
            po2 = C["ps_h"][i2]
            for kb in range(DBLK):
                s.mm(po2, C["w1b"][kb], C["w1b"][kb][:, D + mo * P:D + (mo + 1) * P],
                     xn[kb], xn[kb].ap, start=(kb == 0), stop=(kb == DBLK - 1))
            s.emit(a, lambda e, po2=po2, mo=mo, i2=i2: e.activation(
                       xssm[i2].ap, po2.ap, AF.Identity, bias=bxc[:, mo:mo + 1]),
                   reads=[po2, bxc], writes=[xssm[i2]], psum_src=True)
            s.emit(v, lambda e, i2=i2: e.tensor_mul(xssm[i2].ap, xssm[i2].ap, dtt[i2].ap),
                   reads=[dtt[i2], xssm[i2]], writes=[xssm[i2]])
            s.dma(vvl_h[mo * P:(mo + 1) * P, sl], xssm[i2].ap, reads=[xssm[i2]], writes=[vvl_d])
        for name, out_h2, out_d2, bias in (("wb", bml_h, bml_d, bb16),
                                           ("wc", ccl_h, ccl_d, bc16)):
            wbv = wsmall[name]
            for kb in range(DBLK):
                s.mm(ps_bc, wbv, wbv[:, kb * NSTATE:(kb + 1) * NSTATE],
                     xn[kb], xn[kb].ap, start=(kb == 0), stop=(kb == DBLK - 1),
                     out_ap=ps_bc.t[0:NSTATE, :])
            s.emit(a, lambda e, bias=bias: e.activation(
                       sm16.ap, ps_bc.t[0:NSTATE, :], AF.Identity, bias=bias[:, 0:1]),
                   reads=[ps_bc, bias], writes=[sm16], psum_src=True)
            s.dma(out_h2[:, sl], sm16.ap, reads=[sm16], writes=[out_d2])

    rg = [list(core_ids)]
    acol = sbuf(nc, "acol", [P, DBLK * NSTATE], F32)
    s.dma(acol.ap, al_h.rearrange("(k p) n -> p k n", p=P), writes=[acol])
    s.emit(a, lambda e: e.activation(acol.ap, acol.ap, AF.Exp), reads=[acol], writes=[acol])
    s.emit(v, lambda e: e.tensor_scalar(acol.ap, acol.ap, -1.0, None, ALU.mult),
           reads=[acol], writes=[acol])
    yloc = [Buf(w1big[:, dj * 4096:(dj + 1) * 4096].bitcast(F32)) for dj in range(DBLK)]
    at = Buf(h1big[:, 0:4096].bitcast(F32))
    ut = Buf(h1big[:, 4096:8192].bitcast(F32))
    ht = Buf(h1big[:, 8192:12288].bitcast(F32))
    qt = Buf(h1big[:, 12288:16384].bitcast(F32))
    dtb = C["wst"][0]
    vvb = C["wst"][1]
    zerot = Buf(C["w2b"][0][:, 0:4096].bitcast(F32))
    rowB = [Buf(C["w2b"][0][0:1, 0:4096].bitcast(F32)),
            Buf(C["w2b"][0][32:33, 0:4096].bitcast(F32))]
    rowC = [Buf(C["w2b"][1][0:1, 0:4096].bitcast(F32)),
            Buf(C["w2b"][1][32:33, 0:4096].bitcast(F32))]
    rowbase = {id(rowB[0]): 0, id(rowB[1]): 32, id(rowC[0]): 0, id(rowC[1]): 32}
    ones128 = sbuf(nc, "ones128", [P, P], F32)
    s.emit(v, lambda e: e.memset(ones128.ap, 1.0), writes=[ones128])
    stot = sbuf(nc, "stot", [P, 1], F32)
    hendL = sbuf(nc, "hendL", [P, NSTATE], F32)
    wtotL = sbuf(nc, "wtotL", [P, NSTATE], F32)
    hinT = sbuf(nc, "hinT", [P, NSTATE], F32)
    e1T = sbuf(nc, "e1T", [P, NSTATE], F32)
    wjT = sbuf(nc, "wjT", [P, NSTATE], F32)
    hjT = sbuf(nc, "hjT", [P, NSTATE], F32)
    selm = sbuf(nc, "selmT", [P, NCORES], F32)
    s.dma(selm.ap, selm_h[:], writes=[selm])
    NPB = TC // TF

    def scan_block(dj, init_col, data1, y_first):
        a0 = dj * NSTATE
        for n in range(NSTATE):
            i2 = n % 2
            rC = rowC[i2]
            s.dma(rC.ap, ccl_h[n:n + 1, :], reads=[ccl_d], writes=[rC])
            if data1 is ut:
                rB = rowB[i2]
                s.dma(rB.ap, bml_h[n:n + 1, :], reads=[bml_d], writes=[rB])
            s.emit(a, lambda e, n=n, a0=a0: e.activation(
                       at.ap, dtb.ap, AF.Exp, scale=acol[:, a0 + n:a0 + n + 1]),
                   reads=[dtb, acol], writes=[at])
            if data1 is ut:
                rbb = rowbase[id(rB)]
                for p in range(NPB):
                    pb_ = C["ps_h"][p % 2]
                    s.mm(pb_, ones128, ones128[rbb:rbb + 1, :], rB,
                         rB[0:1, p * TF:(p + 1) * TF], start=True, stop=True)
                    s.emit(v, lambda e, p=p, pb_=pb_: e.tensor_tensor(
                               ut[:, p * TF:(p + 1) * TF],
                               vvb[:, p * TF:(p + 1) * TF], pb_.ap, ALU.mult),
                           reads=[vvb, pb_], writes=[ut], psum_src=True)
            s.emit(v, lambda e, n=n, ic=init_col(n): e.tensor_tensor_scan(
                       ht.ap, at.ap, data1.ap, ic, ALU.mult, ALU.add),
                   reads=[at, data1] + ([hinT] if init_col(n) is not None and data1 is zerot else []),
                   writes=[ht])
            if data1 is ut:
                s.emit(v, lambda e, n=n: e.tensor_copy(
                           hendL[:, n:n + 1], ht[:, TC - 1:TC]),
                       reads=[ht], writes=[hendL])
            rcb = rowbase[id(rC)]
            for p in range(NPB):
                pc_ = C["ps_o"][p % 2]
                s.mm(pc_, ones128, ones128[rcb:rcb + 1, :], rC,
                     rC[0:1, p * TF:(p + 1) * TF], start=True, stop=True)
                s.emit(v, lambda e, p=p, pc_=pc_: e.tensor_tensor(
                           qt[:, p * TF:(p + 1) * TF],
                           ht[:, p * TF:(p + 1) * TF], pc_.ap, ALU.mult),
                       reads=[ht, pc_], writes=[qt], psum_src=True)
            if y_first and n == 0:
                s.emit(v, lambda e, dj=dj: e.tensor_copy(yloc[dj].ap, qt.ap),
                       reads=[qt], writes=[yloc[dj]])
            else:
                s.emit(v, lambda e, dj=dj: e.tensor_add(yloc[dj].ap, yloc[dj].ap, qt.ap),
                       reads=[qt, yloc[dj]], writes=[yloc[dj]])

    for dj in range(DBLK):
        s.dma(dtb.ap, dtl_h[dj * P:(dj + 1) * P, :], reads=[dtl_d], writes=[dtb])
        s.dma(vvb.ap, vvl_h[dj * P:(dj + 1) * P, :], reads=[vvl_d], writes=[vvb])
        scan_block(dj, lambda n: 0.0, ut, y_first=True)
        s.emit(v, lambda e: e.tensor_reduce(stot.ap, dtb.ap, AX.X, ALU.add),
               reads=[dtb], writes=[stot])
        for n in range(NSTATE):
            s.emit(a, lambda e, n=n, dj=dj: e.activation(
                       wtotL[:, n:n + 1], stot.ap, AF.Exp,
                       scale=acol[:, dj * NSTATE + n:dj * NSTATE + n + 1]),
                   reads=[stot, acol], writes=[wtotL])
        s.dma(agin_h[dj * P:(dj + 1) * P, :], hendL.ap, reads=[hendL], writes=[agin_d])
        s.dma(agin_h[D + dj * P:D + (dj + 1) * P, :], wtotL.ap,
              reads=[wtotL], writes=[agin_d])

    s.emit(g, lambda e: e.collective_compute(
               "AllGather", ALU.bypass, replica_groups=rg,
               ins=[agin_h[:]], outs=[agout_h[:]]),
           reads=[agin_d], writes=[agout_d], dma=False)

    s.emit(v, lambda e: e.memset(zerot.ap, 0.0), writes=[zerot])
    for dj in range(DBLK):
        s.dma(dtb.ap, dtl_h[dj * P:(dj + 1) * P, :], reads=[dtl_d], writes=[dtb])
        s.emit(v, lambda e: e.memset(hinT.ap, 0.0), writes=[hinT])
        for j in range(NCORES):
            base = j * 2 * D
            s.dma(hjT.ap, agout_h[base + dj * P:base + (dj + 1) * P, :],
                  reads=[agout_d], writes=[hjT])
            s.dma(wjT.ap, agout_h[base + D + dj * P:base + D + (dj + 1) * P, :],
                  reads=[agout_d], writes=[wjT])
            s.emit(v, lambda e: e.tensor_mul(e1T.ap, wjT.ap, hinT.ap),
                   reads=[wjT, hinT], writes=[e1T])
            s.emit(v, lambda e: e.tensor_add(e1T.ap, e1T.ap, hjT.ap),
                   reads=[e1T, hjT], writes=[e1T])
            s.emit(v, lambda e: e.tensor_sub(e1T.ap, e1T.ap, hinT.ap),
                   reads=[e1T, hinT], writes=[e1T])
            s.emit(v, lambda e, j=j: e.scalar_tensor_tensor(
                       hinT.ap, e1T.ap, selm[:, j:j + 1], hinT.ap, ALU.mult, ALU.add),
                   reads=[e1T, selm, hinT], writes=[hinT])
        scan_block(dj, lambda n: hinT[:, n:n + 1], zerot, y_first=False)
        zview = Buf(C["w2b"][1][0:P, 0:2048])
        gview = Buf(C["w2b"][1][0:P, 2048:4096])
        s.dma(zview.ap, z_h[dj * P:(dj + 1) * P, :], reads=[z_d], writes=[zview])
        s.emit(v, lambda e, dj=dj: e.tensor_tensor(
                   gview.ap, yloc[dj].ap, zview.ap, ALU.mult),
               reads=[yloc[dj], zview], writes=[gview])
        s.dma(gl_h[dj * P:(dj + 1) * P, :], gview.ap, reads=[gview], writes=[gl_d])

    emit_w1_load(s, C, w1_h)
    gt = C["h1"][:DBLK]
    xo_d = Buf(None)
    for f in range(NT):
        sl = slice(f * TF, (f + 1) * TF)
        for kb in range(DBLK):
            s.dma(xs[kb].ap, x2_h[kb * P:(kb + 1) * P, sl], reads=[x2_d], writes=[xs[kb]])
        for kb in range(DBLK):
            st = C["wst"][kb % 2]
            s.dma(st[:, 0:D], wo_h[kb * P:(kb + 1) * P, :], writes=[st])
            s.emit(v, lambda e, kb=kb, st=st: e.tensor_copy(
                       C["w2b"][0][:, kb * 512:kb * 512 + 512], st[:, 0:512]),
                   reads=[st], writes=[C["w2b"][0]])
            s.emit(v, lambda e, kb=kb, st=st: e.tensor_copy(
                       C["w2b"][1][:, kb * 512:kb * 512 + 512], st[:, 512:D]),
                   reads=[st], writes=[C["w2b"][1]])
        for kb in range(DBLK):
            s.dma(gt[kb].ap, gl_h[kb * P:(kb + 1) * P, sl], reads=[gl_d], writes=[gt[kb]])
        for mo in range(DBLK):
            po = C["ps_o"][mo % 2]
            half, off = divmod(mo * P, 512)
            wob = C["w2b"][half]
            for kb in range(DBLK):
                s.mm(po, wob, wob[:, kb * 512 + off:kb * 512 + off + P],
                     gt[kb], gt[kb].ap, start=(kb == 0), stop=(kb == DBLK - 1))
            s.emit(v, lambda e, po=po, mo=mo: e.scalar_tensor_tensor(
                       xs[mo].ap, po.ap, boc[:, mo:mo + 1], xs[mo].ap, ALU.add, ALU.add),
                   reads=[po, boc, xs[mo]], writes=[xs[mo]], psum_src=True)
        emit_ffn(s, C, xs, w2_h)
        for kb in range(DBLK):
            s.dma(xo_h[kb * P:(kb + 1) * P, sl], xs[kb].ap, reads=[xs[kb]], writes=[xo_d])

    s.emit("sync", lambda e: e.nop(), deps=[("sync", s.val["sync"])])
    s.finalize()
    return nc


def _col(b):
    n = b.shape[0] // P
    return np.ascontiguousarray(b.reshape(n, P).T)


def kernel(**inputs):
    xf = np.asarray(inputs["L0_attn_x"], np.float32).reshape(B * L, D)
    shared = dict(
        w1=np.ascontiguousarray(np.asarray(inputs["L0_ffn_w1"], np.float32)),
        b1=_col(np.asarray(inputs["L0_ffn_b1"], np.float32)),
        w2=np.ascontiguousarray(np.asarray(inputs["L0_ffn_w2"], np.float32)),
        b2=_col(np.asarray(inputs["L0_ffn_b2"], np.float32)),
        wz=np.ascontiguousarray(np.asarray(inputs["L0_attn_Wz"], np.float32)),
        wx=np.ascontiguousarray(np.asarray(inputs["L0_attn_Wx"], np.float32)),
        wdt=np.ascontiguousarray(np.asarray(inputs["L0_attn_Wdt"], np.float32)),
        bz=_col(np.asarray(inputs["L0_attn_bz"], np.float32)),
        bx=_col(np.asarray(inputs["L0_attn_bx"], np.float32)),
        bdt=_col(np.asarray(inputs["L0_attn_bdt"], np.float32)),
        wb=np.ascontiguousarray(np.asarray(inputs["L0_attn_WB"], np.float32)),
        wc=np.ascontiguousarray(np.asarray(inputs["L0_attn_WC"], np.float32)),
        bb=np.ascontiguousarray(np.asarray(inputs["L0_attn_bB"], np.float32).reshape(NSTATE, 1)),
        bc=np.ascontiguousarray(np.asarray(inputs["L0_attn_bC"], np.float32).reshape(NSTATE, 1)),
        wo=np.ascontiguousarray(np.asarray(inputs["L0_attn_Wout"], np.float32)),
        bo=_col(np.asarray(inputs["L0_attn_bout"], np.float32)),
    )
    alog = np.asarray(inputs["L0_attn_A_log"], np.float32)
    core_ids = list(range(NCORES))
    nc = build_fused(core_ids)
    maps = []
    for i in range(NCORES):
        m = dict(shared)
        m["x"] = np.ascontiguousarray(xf[i * TC:(i + 1) * TC, :].T)
        m["alog"] = np.ascontiguousarray(alog)
        sel = np.zeros((P, NCORES), np.float32)
        for j in range(NCORES):
            if j // 4 == i // 4 and j % 4 < i % 4:
                sel[:, j] = 1.0
        m["selm"] = sel
        maps.append(m)
    res = run_bass_kernel_spmd(nc, maps, core_ids=core_ids)
    out = np.empty((B * L, D), np.float32)
    for i in range(NCORES):
        out[i * TC:(i + 1) * TC, :] = np.asarray(res.results[i]["xo"], np.float32).T
    return out.reshape(B, L, D)


# version 07
# speedup vs baseline: 1.3484x; 1.0499x over previous
import math
from contextlib import ExitStack

import numpy as np
import ml_dtypes

import concourse.bass as bass
import concourse.mybir as mybir
from concourse.bass_utils import run_bass_kernel_spmd

F32 = mybir.dt.float32
I32 = mybir.dt.int32
BF16 = mybir.dt.bfloat16
AF = mybir.ActivationFunctionType
ALU = mybir.AluOpType
AX = mybir.AxisListType

P = 128
D = 1024
FF = 4096
NSTATE = 16
B = 2
L = 8192
NCORES = 8
TC = B * L // NCORES
DBLK = D // P
FBLK = FF // P
TF = 512
NT = TC // TF
TP2 = 2048
EPS = 1e-5

SERIALIZE = True
CHAIN_EXEMPT = set()
ISSUE_CHAIN = True
DEBUG_TAPS = False


def _gelu64(x):
    x = np.asarray(x, np.float64)
    return 0.5 * x * (1.0 + np.vectorize(math.erf)(x / math.sqrt(2.0)))

def _build_pwl():
    X = np.linspace(-4.0, 4.0, 33)
    Y = _gelu64(X)
    j = np.arange(17)
    qv = 256.0 * Y[16 - j]
    g = (qv[1:] - qv[:-1]) / 64.0
    g0 = g[0]
    dj = g[1:] - g[:-1]
    t = np.arange(-40000, 40001, dtype=np.float64)
    uc = np.minimum(np.abs(t), 1024.0)
    q = g0 * uc
    for k in range(1, 16):
        q = q + dj[k - 1] * np.maximum(uc - 64.0 * k, 0.0)
    mine = np.maximum(t, 0.0) + np.round(q)
    xq = t / 256.0
    y = np.interp(xq, X, Y)
    y = np.where(xq > 4.0, xq, np.where(xq < -4.0, 0.0, y))
    ref = np.round(256.0 * y)
    assert np.array_equal(mine, ref), "PWL hinge table does not match reference"
    return float(g0), [float(v) for v in dj]

_PWL_G0, _PWL_DJ = _build_pwl()


class Buf:
    __slots__ = ("t", "last_write", "readers")
    def __init__(self, t):
        self.t = t
        self.last_write = None
        self.readers = []
    def __getitem__(self, sl):
        return self.t[sl]
    @property
    def ap(self):
        return self.t[:]


class Sch:
    ENGS = ("sync", "vector", "scalar", "gpsimd", "tensor")
    SEMS = ENGS + ("syncI",)

    def __init__(self, nc):
        self.nc = nc
        self.ops = {e: [] for e in self.ENGS}
        self.val = {e: 0 for e in self.SEMS}
        self.es = ExitStack()
        self.sem = {e: self.es.enter_context(nc.semaphore(f"s_{e}")) for e in self.SEMS}
        self.prev = None

    def emit(self, eng, fn, reads=(), writes=(), deps=(), dma=False, psum_src=False,
             ccomp=False):
        dl = list(deps)
        if SERIALIZE and eng not in CHAIN_EXEMPT \
                and self.prev is not None and self.prev[0] != eng:
            dl.append(self.prev)
        for b in reads:
            if b.last_write is not None:
                dl.append(b.last_write)
        for b in writes:
            if b.last_write is not None:
                dl.append(b.last_write)
            dl.extend(b.readers)
        inc = 16 if dma else 1
        self.val[eng] += inc
        tok = (eng, self.val[eng])
        self.ops[eng].append((fn, dl, inc, psum_src, None))
        if dma and ISSUE_CHAIN and not ccomp:
            self.val["syncI"] += 1
            itok = ("syncI", self.val["syncI"])
            self.ops[eng].append(
                (lambda e: e.nop(cycle_cnt=0, nofuse=True), [], 1, False, "syncI"))
            if eng not in CHAIN_EXEMPT:
                self.prev = itok
        elif eng not in CHAIN_EXEMPT:
            self.prev = tok
        for b in reads:
            b.readers.append(tok)
        for b in writes:
            b.last_write = tok
            b.readers = []
        return tok

    def mm(self, ps, lhsT_buf, lhsT_ap, rhs_buf, rhs_ap, start, stop, out_ap=None):
        dl = []
        if SERIALIZE and "tensor" not in CHAIN_EXEMPT \
                and self.prev is not None and self.prev[0] != "tensor":
            dl.append(self.prev)
        if lhsT_buf is not None and lhsT_buf.last_write is not None:
            dl.append(lhsT_buf.last_write)
        if rhs_buf is not None and rhs_buf.last_write is not None:
            dl.append(rhs_buf.last_write)
        if start:
            if ps.last_write is not None:
                dl.append(ps.last_write)
            dl.extend(ps.readers)
        self.val["tensor"] += 1
        tok = ("tensor", self.val["tensor"])
        if "tensor" not in CHAIN_EXEMPT:
            self.prev = tok
        oap = ps.ap if out_ap is None else out_ap
        self.ops["tensor"].append(
            (lambda e, oap=oap, la=lhsT_ap, ra=rhs_ap, st=start, sp=stop:
             e.matmul(oap, la, ra, start=st, stop=sp), dl, 1, False, None))
        if lhsT_buf is not None:
            lhsT_buf.readers.append(tok)
        if rhs_buf is not None:
            rhs_buf.readers.append(tok)
        if stop:
            ps.last_write = tok
            ps.readers = []
        return tok

    def dma(self, dst_ap, src_ap, reads=(), writes=(), deps=(), eng="sync",
            ccomp=False):
        import numpy as _np
        assert _np.prod(dst_ap.shape) == _np.prod(src_ap.shape), (
            f"dma size mismatch {dst_ap.shape} <- {src_ap.shape}")
        return self.emit(
            eng, lambda e: e.dma_start(out=dst_ap, in_=src_ap),
            reads=reads, writes=writes, deps=deps, dma=True, ccomp=ccomp)

    def finalize(self):
        nc = self.nc
        with nc.Block() as block:
            for eng in self.ENGS:
                if not self.ops[eng]:
                    continue
                def make(eng):
                    def body(e):
                        water = {s2: 0 for s2 in self.SEMS}
                        for fn, deps, inc, psrc, semov in self.ops[eng]:
                            need = {}
                            for (de, dv) in deps:
                                if dv > water[de]:
                                    need[de] = max(need.get(de, 0), dv)
                            waited_pe = False
                            for de, dv in need.items():
                                e.wait_ge(self.sem[de], dv)
                                water[de] = dv
                                if de == "tensor":
                                    waited_pe = True
                            if psrc and waited_pe:
                                e.nop(cycle_cnt=192, nofuse=True)
                            ins = fn(e)
                            ins.then_inc(self.sem[semov or eng], inc)
                    return body
                getattr(block, eng)(make(eng))
        self.es.close()


def sbuf(nc, name, shape, dtype):
    return Buf(nc.alloc_sbuf_tensor(name, shape, dtype))

def psum(nc, name, shape, dtype=F32):
    return Buf(nc.alloc_psum_tensor(name, shape, dtype))



def emit_ln(s, C, xblk, xn_out):
    v, a = "vector", "scalar"
    ones_col, ones_row = C["ones_col"], C["ones_row"]
    ps_s1, ps_s2, ps_r, ps_m = C["ps_s1"], C["ps_s2"], C["ps_rstd"], C["ps_mmr"]
    st_mean, st_rstd, st_mmr = C["st_mean"], C["st_rstd"], C["st_mmr"]
    tmp_sq, tmp_sq2, tmp32 = C["tmp_sq"], C["tmp_sq2"], C["tmp32"]
    for kb in range(DBLK):
        s.emit(v, lambda e, kb=kb: e.tensor_copy(tmp_sq.ap, xblk[kb].ap),
               reads=[xblk[kb]], writes=[tmp_sq])
        s.mm(ps_s1, ones_col, ones_col.ap, tmp_sq, tmp_sq.ap,
             start=(kb == 0), stop=(kb == DBLK - 1))
        s.emit(a, lambda e, kb=kb: e.activation(tmp_sq2.ap, xblk[kb].ap, AF.Square),
               reads=[xblk[kb]], writes=[tmp_sq2])
        s.mm(ps_s2, ones_col, ones_col.ap, tmp_sq2, tmp_sq2.ap,
             start=(kb == 0), stop=(kb == DBLK - 1))
    s.emit(v, lambda e: e.tensor_scalar(st_mean.ap, ps_s1.ap, 1.0 / D, None, ALU.mult),
           reads=[ps_s1], writes=[st_mean], psum_src=True)
    s.emit(v, lambda e: e.tensor_mul(st_rstd.ap, st_mean.ap, st_mean.ap),
           reads=[st_mean], writes=[st_rstd])
    s.emit(v, lambda e: e.scalar_tensor_tensor(st_rstd.ap, ps_s2.ap, 1.0 / D,
                                               st_rstd.ap, ALU.mult, ALU.subtract),
           reads=[ps_s2, st_rstd], writes=[st_rstd], psum_src=True)
    s.emit(a, lambda e: e.activation(st_rstd.ap, st_rstd.ap, AF.Ln,
                                     bias=C["eps1"][0:1, 0:1]),
           reads=[st_rstd, C["eps1"]], writes=[st_rstd])
    s.emit(a, lambda e: e.activation(st_rstd.ap, st_rstd.ap, AF.Exp, scale=-0.5),
           reads=[st_rstd], writes=[st_rstd])
    s.emit(v, lambda e: e.tensor_mul(st_mmr.ap, st_mean.ap, st_rstd.ap),
           reads=[st_mean, st_rstd], writes=[st_mmr])
    s.mm(ps_r, ones_row, ones_row.ap, st_rstd, st_rstd.ap, start=True, stop=True)
    s.mm(ps_m, ones_row, ones_row.ap, st_mmr, st_mmr.ap, start=True, stop=True)
    for kb in range(DBLK):
        s.emit(v, lambda e, kb=kb: e.tensor_mul(tmp32.ap, xblk[kb].ap, ps_r.ap),
               reads=[xblk[kb], ps_r], writes=[tmp32], psum_src=True)
        s.emit(v, lambda e, kb=kb: e.tensor_sub(xn_out[kb].ap, tmp32.ap, ps_m.ap),
               reads=[tmp32, ps_m], writes=[xn_out[kb]], psum_src=True)


def emit_pwl4(s, C, g0):
    v, a = "vector", "scalar"
    ti4, uc4, ac4, ht4 = C["ti4"], C["uc4"], C["ac4"], C["ht4"]
    for i in range(4):
        m = g0 + i
        ph = C["ps4"][i]
        s.emit(v, lambda e, ph=ph, m=m, i=i: e.tensor_scalar(
                   ti4[:, i * TF:(i + 1) * TF], ph.ap, 256.0, C["b1c"][:, m:m + 1],
                   ALU.mult, ALU.add),
               reads=[ph, C["b1c_buf"]], writes=[ti4], psum_src=True)
    s.emit(v, lambda e: e.scalar_tensor_tensor(uc4.ap, ti4.ap, -1.0, ti4.ap,
                                               ALU.mult, ALU.max),
           reads=[ti4], writes=[uc4])
    s.emit(v, lambda e: e.tensor_scalar(uc4.ap, uc4.ap, 1024.0, None, ALU.min),
           reads=[uc4], writes=[uc4])
    s.emit(v, lambda e: e.tensor_scalar(ac4.ap, uc4.ap, _PWL_G0, None, ALU.mult),
           reads=[uc4], writes=[ac4])
    s.emit(v, lambda e: e.scalar_tensor_tensor(ac4.ap, ti4.ap, 0.0, ac4.ap,
                                               ALU.max, ALU.add),
           reads=[ti4, ac4], writes=[ac4])
    for k0 in range(1, 16, 2):
        ks = [k for k in (k0, k0 + 1) if k <= 15]
        for j, k in enumerate(ks):
            ad = abs(_PWL_DJ[k - 1])
            s.emit(a, lambda e, ad=ad, k=k, j=j: e.activation(
                       ht4[j].ap, uc4.ap, AF.Relu, bias=C["hingeb"][:, k - 1:k],
                       scale=ad),
                   reads=[uc4, C["hingeb"]], writes=[ht4[j]])
        for j, k in enumerate(ks):
            op = ALU.add if _PWL_DJ[k - 1] >= 0 else ALU.subtract
            s.emit(v, lambda e, j=j, op=op: e.tensor_tensor(
                       ac4.ap, ac4.ap, ht4[j].ap, op),
                   reads=[ht4[j], ac4], writes=[ac4])
    s.emit(v, lambda e: e.tensor_scalar(ti4.ap, ac4.ap, 1.0, None, ALU.mult),
           reads=[ac4], writes=[ti4])
    hw = [C["h1"][g0 + i] for i in range(4)]
    s.emit(v, lambda e, g0=g0: e.tensor_copy(
               C["h1big"][:, g0 * TF:(g0 + 4) * TF], ti4.ap),
           reads=[ti4], writes=hw)


def emit_ffn(s, C, xblk, w2_h):
    v = "vector"
    xn = C["xn"]
    emit_ln(s, C, xblk, xn)
    h1 = C["h1"]
    for g0 in range(0, FBLK, 4):
        for i in range(4):
            m = g0 + i
            ph = C["ps4"][i]
            for kb in range(DBLK):
                s.mm(ph, C["w1b"][kb], C["w1b"][kb][:, m * P:(m + 1) * P],
                     xn[kb], xn[kb].ap, start=(kb == 0), stop=(kb == DBLK - 1))
        emit_pwl4(s, C, g0)
    w2r = w2_h.rearrange("(m p) o -> p m o", p=P)
    for mo in range(DBLK):
        w2b = C["w2b"][mo % 2]
        for h in range(2):
            w2s = C["wst"][h]
            hm = FBLK // 2
            s.dma(w2s.ap, w2r[:, h * hm:(h + 1) * hm, mo * P:(mo + 1) * P], writes=[w2s])
            s.emit(v, lambda e, w2b=w2b, w2s=w2s, h=h, hm=hm: e.tensor_copy(
                       w2b[:, h * hm * P:(h + 1) * hm * P], w2s.ap),
                   reads=[w2s], writes=[w2b])
        po = C["ps_o"][mo % 2]
        for m in range(FBLK):
            s.mm(po, w2b, w2b[:, m * P:(m + 1) * P], h1[m], h1[m].ap,
                 start=(m == 0), stop=(m == FBLK - 1))
        t3 = C["t3"][mo % 2]
        s.emit(v, lambda e, t3=t3, po=po, mo=mo: e.tensor_scalar(
                   t3.ap, po.ap, C["b2c"][:, mo:mo + 1], None, ALU.add),
               reads=[po, C["b2c_buf"]], writes=[t3], psum_src=True)
        s.emit(v, lambda e, t3=t3, mo=mo: e.scalar_tensor_tensor(
                   xblk[mo].ap, t3.ap, 1.0 / 512.0, xblk[mo].ap, ALU.mult, ALU.add),
               reads=[t3, xblk[mo]], writes=[xblk[mo]])


def emit_w1_load(s, C, w1_h):
    v = "vector"
    for kb in range(DBLK):
        for h in range(2):
            st = C["wst"][(kb * 2 + h) % 2]
            sl = slice(h * (FF // 2), (h + 1) * (FF // 2))
            s.dma(st.ap, w1_h[kb * P:(kb + 1) * P, sl], writes=[st])
            s.emit(v, lambda e, kb=kb, st=st, sl=sl: e.tensor_copy(
                       C["w1b"][kb][:, sl], st.ap),
                   reads=[st], writes=[C["w1b"][kb]])



def build_fused(core_ids):
    nc = bass.Bass("TRN2", target_bir_lowering=False, debug=False)
    x_h = nc.dram_tensor("x", [D, TC], F32, kind="ExternalInput")
    w1_h = nc.dram_tensor("w1", [D, FF], F32, kind="ExternalInput")
    b1_h = nc.dram_tensor("b1", [P, FBLK], F32, kind="ExternalInput")
    w2_h = nc.dram_tensor("w2", [FF, D], F32, kind="ExternalInput")
    b2_h = nc.dram_tensor("b2", [P, DBLK], F32, kind="ExternalInput")
    wz_h = nc.dram_tensor("wz", [D, D], F32, kind="ExternalInput")
    wx_h = nc.dram_tensor("wx", [D, D], F32, kind="ExternalInput")
    wdt_h = nc.dram_tensor("wdt", [D, D], F32, kind="ExternalInput")
    bz_h = nc.dram_tensor("bz", [P, DBLK], F32, kind="ExternalInput")
    bx_h = nc.dram_tensor("bx", [P, DBLK], F32, kind="ExternalInput")
    bdt_h = nc.dram_tensor("bdt", [P, DBLK], F32, kind="ExternalInput")
    wb_h = nc.dram_tensor("wb", [D, NSTATE], F32, kind="ExternalInput")
    wc_h = nc.dram_tensor("wc", [D, NSTATE], F32, kind="ExternalInput")
    bb_h = nc.dram_tensor("bb", [NSTATE, 1], F32, kind="ExternalInput")
    bc_h = nc.dram_tensor("bc", [NSTATE, 1], F32, kind="ExternalInput")
    al_h = nc.dram_tensor("alog", [D, NSTATE], F32, kind="ExternalInput")
    wo_h = nc.dram_tensor("wo", [D, D], F32, kind="ExternalInput")
    bo_h = nc.dram_tensor("bo", [P, DBLK], F32, kind="ExternalInput")
    xo_h = nc.dram_tensor("xo", [D, TC], F32, kind="ExternalOutput")
    KI = "ExternalOutput" if DEBUG_TAPS else "Internal"
    dtl_h = nc.dram_tensor("dtl", [D, TC], F32, kind=KI)
    vvl_h = nc.dram_tensor("vvl", [D, TC], F32, kind=KI)
    bml_h = nc.dram_tensor("bml", [NSTATE, TC], F32, kind=KI)
    ccl_h = nc.dram_tensor("ccl", [NSTATE, TC], F32, kind=KI)
    gl_h = nc.dram_tensor("gl", [D, TC], BF16, kind=KI)
    agin_h = nc.dram_tensor("agin", [2 * D, NSTATE], F32, kind="Internal")
    agout_h = nc.dram_tensor("agout", [NCORES * 2 * D, NSTATE], F32, kind="Internal")
    x2_h = nc.dram_tensor("x2i", [D, TC], F32, kind=KI)
    z_h = nc.dram_tensor("zi", [D, TC], BF16, kind=KI)
    selm_h = nc.dram_tensor("selm", [P, NCORES], F32, kind="ExternalInput")

    s = Sch(nc)
    v, a, g, t = "vector", "scalar", "gpsimd", "tensor"

    C = {}
    C["ones_col"] = sbuf(nc, "ones_col", [P, 1], BF16)
    C["ones_row"] = sbuf(nc, "ones_row", [1, P], F32)
    s.emit(v, lambda e: e.memset(C["ones_col"].ap, 1.0), writes=[C["ones_col"]])
    s.emit(v, lambda e: e.memset(C["ones_row"].ap, 1.0), writes=[C["ones_row"]])
    C["eps1"] = sbuf(nc, "eps1", [1, 1], F32)
    s.emit(v, lambda e: e.memset(C["eps1"].ap, EPS), writes=[C["eps1"]])
    psA = nc.alloc_psum_tensor("psA", [P, 4 * TF], F32)
    psB = nc.alloc_psum_tensor("psB", [P, 4 * TF], F32)
    C["ps_s1"] = Buf(psA[0:1, 0:TF])
    C["ps_s2"] = Buf(psA[0:1, TF:2 * TF])
    C["ps_rstd"] = Buf(psA[:, 2 * TF:3 * TF])
    C["ps_mmr"] = Buf(psA[:, 3 * TF:4 * TF])
    C["ps_h"] = [Buf(psB[:, 0:TF]), Buf(psB[:, TF:2 * TF])]
    C["ps_o"] = [Buf(psB[:, 2 * TF:3 * TF]), Buf(psB[:, 3 * TF:4 * TF])]
    C["psA"], C["psB"] = psA, psB
    C["st_mean"] = sbuf(nc, "st_mean", [1, TF], F32)
    C["st_rstd"] = sbuf(nc, "st_rstd", [1, TF], F32)
    C["st_mmr"] = sbuf(nc, "st_mmr", [1, TF], F32)
    C["tmp_sq"] = sbuf(nc, "tmp_sq", [P, TF], BF16)
    C["tmp_sq2"] = sbuf(nc, "tmp_sq2", [P, TF], BF16)
    C["tmp32"] = sbuf(nc, "tmp32", [P, TF], F32)
    C["xn"] = [sbuf(nc, f"xn{kb}", [P, TF], BF16) for kb in range(DBLK)]
    w1big = nc.alloc_sbuf_tensor("w1big", [P, DBLK * FF], BF16)
    h1big = nc.alloc_sbuf_tensor("h1big", [P, FBLK * TF], BF16)
    C["w1b"] = [Buf(w1big[:, kb * FF:(kb + 1) * FF]) for kb in range(DBLK)]
    C["h1"] = [Buf(h1big[:, m * TF:(m + 1) * TF]) for m in range(FBLK)]
    C["wst"] = [sbuf(nc, f"wst{i}", [P, FF // 2], F32) for i in range(2)]
    C["w2b"] = [sbuf(nc, f"w2b{i}", [P, FF], BF16) for i in range(2)]
    C["t3"] = [sbuf(nc, f"t3_{i}", [P, TF], I32) for i in range(2)]
    C["ti4"] = sbuf(nc, "pw_ti4", [P, 4 * TF], I32)
    C["uc4"] = sbuf(nc, "pw_uc4", [P, 4 * TF], F32)
    C["ac4"] = sbuf(nc, "pw_ac4", [P, 4 * TF], F32)
    C["ht4"] = [Buf(C["w2b"][i].t[:, 0:FF].bitcast(F32)) for i in range(2)]
    C["ps4"] = [C["ps_h"][0], C["ps_h"][1], C["ps_o"][0], C["ps_o"][1]]
    C["h1big"] = h1big
    hb = sbuf(nc, "hingeb", [P, 15], F32)
    for k in range(1, 16):
        ad = abs(_PWL_DJ[k - 1])
        s.emit(v, lambda e, k=k, ad=ad: e.memset(hb[:, k - 1:k], -ad * 64.0 * k),
               writes=[hb])
    C["hingeb"] = hb
    def bias_col(name, h, n, scale=None):
        raw = sbuf(nc, name + "r", [P, n], F32)
        s.dma(raw.ap, h[:], writes=[raw])
        if scale is None:
            return raw
        sc = sbuf(nc, name + "c", [P, n], F32)
        s.emit(v, lambda e: e.tensor_scalar(sc.ap, raw.ap, scale, None, ALU.mult),
               reads=[raw], writes=[sc])
        return sc
    b1c = bias_col("b1", b1_h, FBLK, 256.0)
    b2c = bias_col("b2", b2_h, DBLK, 256.0)
    C["b1c"], C["b1c_buf"] = b1c.t, b1c
    C["b2c"], C["b2c_buf"] = b2c.t, b2c
    bzc = bias_col("bz", bz_h, DBLK)
    bxc = bias_col("bx", bx_h, DBLK)
    bdtc = bias_col("bdt", bdt_h, DBLK)
    boc = bias_col("bo", bo_h, DBLK)
    bb16 = sbuf(nc, "bbc", [NSTATE, 1], F32)
    s.dma(bb16.ap, bb_h[:], writes=[bb16])
    bc16 = sbuf(nc, "bcc", [NSTATE, 1], F32)
    s.dma(bc16.ap, bc_h[:], writes=[bc16])

    xs = [sbuf(nc, f"xs{kb}", [P, TF], F32) for kb in range(DBLK)]
    x2_d, ain_d, aout_d, yin_d, yout_d, z_d = (Buf(None) for _ in range(6))

    emit_w1_load(s, C, w1_h)
    for f in range(NT):
        sl = slice(f * TF, (f + 1) * TF)
        for kb in range(DBLK):
            s.dma(xs[kb].ap, x_h[kb * P:(kb + 1) * P, sl], writes=[xs[kb]])
        emit_ffn(s, C, xs, w2_h)
        for kb in range(DBLK):
            s.dma(x2_h[kb * P:(kb + 1) * P, sl], xs[kb].ap,
                  reads=[xs[kb]], writes=[x2_d])

    for name, h, off in (("wz", wz_h, 0), ("wx", wx_h, D), ("wdt", wdt_h, 2 * D)):
        for kb in range(DBLK):
            st = C["wst"][kb % 2]
            s.dma(st[:, 0:D], h[kb * P:(kb + 1) * P, :], writes=[st])
            s.emit(v, lambda e, kb=kb, st=st, off=off: e.tensor_copy(
                       C["w1b"][kb][:, off:off + D], st[:, 0:D]),
                   reads=[st], writes=[C["w1b"][kb]])
    wsmall = {}
    for name, h in (("wb", wb_h), ("wc", wc_h)):
        st = C["wst"][0]
        s.dma(st[:, 0:DBLK * NSTATE], h.rearrange("(k p) n -> p k n", p=P), writes=[st])
        wbv = sbuf(nc, f"{name}b", [P, DBLK * NSTATE], BF16)
        s.emit(v, lambda e, wbv=wbv, st=st: e.tensor_copy(wbv.ap, st[:, 0:DBLK * NSTATE]),
               reads=[st], writes=[wbv])
        wsmall[name] = wbv

    xssm = [Buf(h1big[:, i * 1024:(i + 1) * 1024].bitcast(F32)) for i in range(2)]
    dtt = [Buf(h1big[:, (2 + i) * 1024:(3 + i) * 1024].bitcast(F32)) for i in range(2)]
    pbt = [Buf(h1big[:, (4 + i) * 1024:(5 + i) * 1024].bitcast(F32)) for i in range(2)]
    sc1 = [Buf(h1big[:, (6 + i) * 1024:(7 + i) * 1024].bitcast(F32)) for i in range(2)]
    zt = sbuf(nc, "zt", [P, TF], BF16)
    sm16 = sbuf(nc, "sm16", [NSTATE, TF], F32)
    ps_bc = Buf(psA[0:NSTATE, 2 * TF:3 * TF])

    dtl_d, vvl_d, bml_d, ccl_d, gl_d, agin_d, agout_d = (Buf(None) for _ in range(7))
    for f in range(NT):
        sl = slice(f * TF, (f + 1) * TF)
        for kb in range(DBLK):
            s.dma(xs[kb].ap, x2_h[kb * P:(kb + 1) * P, sl], reads=[x2_d], writes=[xs[kb]])
        xn = C["xn"]
        emit_ln(s, C, xs, xn)
        for mo in range(DBLK):
            po = C["ps_o"][mo % 2]
            for kb in range(DBLK):
                s.mm(po, C["w1b"][kb], C["w1b"][kb][:, mo * P:(mo + 1) * P],
                     xn[kb], xn[kb].ap, start=(kb == 0), stop=(kb == DBLK - 1))
            s.emit(a, lambda e, po=po, mo=mo: e.activation(
                       zt.ap, po.ap, AF.Sigmoid, bias=bzc[:, mo:mo + 1]),
                   reads=[po, bzc], writes=[zt], psum_src=True)
            s.dma(z_h[mo * P:(mo + 1) * P, sl], zt.ap, reads=[zt], writes=[z_d])
        for mo in range(DBLK):
            i2 = mo % 2
            po = C["ps_o"][i2]
            for kb in range(DBLK):
                s.mm(po, C["w1b"][kb], C["w1b"][kb][:, 2 * D + mo * P:2 * D + (mo + 1) * P],
                     xn[kb], xn[kb].ap, start=(kb == 0), stop=(kb == DBLK - 1))
            s.emit(a, lambda e, po=po, mo=mo, i2=i2: e.activation(
                       pbt[i2].ap, po.ap, AF.Identity, bias=bdtc[:, mo:mo + 1]),
                   reads=[po, bdtc], writes=[pbt[i2]], psum_src=True)
            s.emit(v, lambda e, i2=i2: e.scalar_tensor_tensor(
                       sc1[i2].ap, pbt[i2].ap, -1.0, pbt[i2].ap, ALU.mult, ALU.max),
                   reads=[pbt[i2]], writes=[sc1[i2]])
            s.emit(a, lambda e, i2=i2: e.activation(sc1[i2].ap, sc1[i2].ap, AF.Exp,
                                                    scale=-1.0),
                   reads=[sc1[i2]], writes=[sc1[i2]])
            s.emit(a, lambda e, i2=i2: e.activation(sc1[i2].ap, sc1[i2].ap, AF.Ln,
                                                    bias=1.0),
                   reads=[sc1[i2]], writes=[sc1[i2]])
            s.emit(v, lambda e, i2=i2: e.scalar_tensor_tensor(
                       dtt[i2].ap, pbt[i2].ap, 0.0, sc1[i2].ap, ALU.max, ALU.add),
                   reads=[pbt[i2], sc1[i2]], writes=[dtt[i2]])
            s.dma(dtl_h[mo * P:(mo + 1) * P, sl], dtt[i2].ap, reads=[dtt[i2]], writes=[dtl_d])
            po2 = C["ps_h"][i2]
            for kb in range(DBLK):
                s.mm(po2, C["w1b"][kb], C["w1b"][kb][:, D + mo * P:D + (mo + 1) * P],
                     xn[kb], xn[kb].ap, start=(kb == 0), stop=(kb == DBLK - 1))
            s.emit(a, lambda e, po2=po2, mo=mo, i2=i2: e.activation(
                       xssm[i2].ap, po2.ap, AF.Identity, bias=bxc[:, mo:mo + 1]),
                   reads=[po2, bxc], writes=[xssm[i2]], psum_src=True)
            s.emit(v, lambda e, i2=i2: e.tensor_mul(xssm[i2].ap, xssm[i2].ap, dtt[i2].ap),
                   reads=[dtt[i2], xssm[i2]], writes=[xssm[i2]])
            s.dma(vvl_h[mo * P:(mo + 1) * P, sl], xssm[i2].ap, reads=[xssm[i2]], writes=[vvl_d])
        for name, out_h2, out_d2, bias in (("wb", bml_h, bml_d, bb16),
                                           ("wc", ccl_h, ccl_d, bc16)):
            wbv = wsmall[name]
            for kb in range(DBLK):
                s.mm(ps_bc, wbv, wbv[:, kb * NSTATE:(kb + 1) * NSTATE],
                     xn[kb], xn[kb].ap, start=(kb == 0), stop=(kb == DBLK - 1),
                     out_ap=ps_bc.ap)
            s.emit(a, lambda e, bias=bias: e.activation(
                       sm16.ap, ps_bc.ap, AF.Identity, bias=bias[:, 0:1]),
                   reads=[ps_bc, bias], writes=[sm16], psum_src=True)
            s.dma(out_h2[:, sl], sm16.ap, reads=[sm16], writes=[out_d2])

    rg = [list(core_ids)]
    acol = sbuf(nc, "acol", [P, DBLK * NSTATE], F32)
    s.dma(acol.ap, al_h.rearrange("(k p) n -> p k n", p=P), writes=[acol])
    s.emit(a, lambda e: e.activation(acol.ap, acol.ap, AF.Exp), reads=[acol], writes=[acol])
    s.emit(v, lambda e: e.tensor_scalar(acol.ap, acol.ap, -1.0, None, ALU.mult),
           reads=[acol], writes=[acol])
    yloc = [Buf(w1big[:, dj * 4096:(dj + 1) * 4096].bitcast(F32)) for dj in range(DBLK)]
    at = Buf(h1big[:, 0:4096].bitcast(F32))
    ut = Buf(h1big[:, 4096:8192].bitcast(F32))
    ht = Buf(h1big[:, 8192:12288].bitcast(F32))
    qt = Buf(h1big[:, 12288:16384].bitcast(F32))
    dtb = C["wst"][0]
    vvb = C["wst"][1]
    zerot = Buf(C["w2b"][0][:, 0:4096].bitcast(F32))
    rowB = [Buf(C["w2b"][0][0:1, 0:4096].bitcast(F32)),
            Buf(C["w2b"][0][32:33, 0:4096].bitcast(F32))]
    rowC = [Buf(C["w2b"][1][0:1, 0:4096].bitcast(F32)),
            Buf(C["w2b"][1][32:33, 0:4096].bitcast(F32))]
    rowbase = {id(rowB[0]): 0, id(rowB[1]): 32, id(rowC[0]): 0, id(rowC[1]): 32}
    ones128 = sbuf(nc, "ones128", [P, P], F32)
    s.emit(v, lambda e: e.memset(ones128.ap, 1.0), writes=[ones128])
    stot = sbuf(nc, "stot", [P, 1], F32)
    hendL = sbuf(nc, "hendL", [P, NSTATE], F32)
    wtotL = sbuf(nc, "wtotL", [P, NSTATE], F32)
    hinT = sbuf(nc, "hinT", [P, NSTATE], F32)
    e1T = sbuf(nc, "e1T", [P, NSTATE], F32)
    wjT = sbuf(nc, "wjT", [P, NSTATE], F32)
    hjT = sbuf(nc, "hjT", [P, NSTATE], F32)
    selm = sbuf(nc, "selmT", [P, NCORES], F32)
    s.dma(selm.ap, selm_h[:], writes=[selm])
    NPB = TC // TF

    pbS = Buf(psA[:, :])
    pcS = Buf(psB[:, :])

    def scan_block(dj, init_col, data1, y_first):
        a0 = dj * NSTATE
        for n in range(NSTATE):
            i2 = n % 2
            rC = rowC[i2]
            s.dma(rC.ap, ccl_h[n:n + 1, :], reads=[ccl_d], writes=[rC])
            if data1 is ut:
                rB = rowB[i2]
                s.dma(rB.ap, bml_h[n:n + 1, :], reads=[bml_d], writes=[rB])
            s.emit(a, lambda e, n=n, a0=a0: e.activation(
                       at.ap, dtb.ap, AF.Exp, scale=acol[:, a0 + n:a0 + n + 1]),
                   reads=[dtb, acol], writes=[at])
            if data1 is ut:
                rbb = rowbase[id(rB)]
                for p in range(NPB):
                    s.mm(pbS, ones128, ones128[rbb:rbb + 1, :], rB,
                         rB[0:1, p * TF:(p + 1) * TF], start=True, stop=True,
                         out_ap=psA[:, p * TF:(p + 1) * TF])
                s.emit(v, lambda e: e.tensor_tensor(
                           ut.ap, vvb.ap, pbS.ap, ALU.mult),
                       reads=[vvb, pbS], writes=[ut], psum_src=True)
            s.emit(v, lambda e, n=n, ic=init_col(n): e.tensor_tensor_scan(
                       ht.ap, at.ap, data1.ap, ic, ALU.mult, ALU.add),
                   reads=[at, data1] + ([hinT] if init_col(n) is not None and data1 is zerot else []),
                   writes=[ht])
            if data1 is ut:
                s.emit(v, lambda e, n=n: e.tensor_copy(
                           hendL[:, n:n + 1], ht[:, TC - 1:TC]),
                       reads=[ht], writes=[hendL])
            rcb = rowbase[id(rC)]
            for p in range(NPB):
                s.mm(pcS, ones128, ones128[rcb:rcb + 1, :], rC,
                     rC[0:1, p * TF:(p + 1) * TF], start=True, stop=True,
                     out_ap=psB[:, p * TF:(p + 1) * TF])
            s.emit(v, lambda e: e.tensor_tensor(qt.ap, ht.ap, pcS.ap, ALU.mult),
                   reads=[ht, pcS], writes=[qt], psum_src=True)
            if y_first and n == 0:
                s.emit(v, lambda e, dj=dj: e.tensor_copy(yloc[dj].ap, qt.ap),
                       reads=[qt], writes=[yloc[dj]])
            else:
                s.emit(v, lambda e, dj=dj: e.tensor_add(yloc[dj].ap, yloc[dj].ap, qt.ap),
                       reads=[qt, yloc[dj]], writes=[yloc[dj]])

    for dj in range(DBLK):
        s.dma(dtb.ap, dtl_h[dj * P:(dj + 1) * P, :], reads=[dtl_d], writes=[dtb])
        s.dma(vvb.ap, vvl_h[dj * P:(dj + 1) * P, :], reads=[vvl_d], writes=[vvb])
        scan_block(dj, lambda n: 0.0, ut, y_first=True)
        s.emit(v, lambda e: e.tensor_reduce(stot.ap, dtb.ap, AX.X, ALU.add),
               reads=[dtb], writes=[stot])
        for n in range(NSTATE):
            s.emit(a, lambda e, n=n, dj=dj: e.activation(
                       wtotL[:, n:n + 1], stot.ap, AF.Exp,
                       scale=acol[:, dj * NSTATE + n:dj * NSTATE + n + 1]),
                   reads=[stot, acol], writes=[wtotL])
        s.dma(agin_h[dj * P:(dj + 1) * P, :], hendL.ap, reads=[hendL], writes=[agin_d])
        s.dma(agin_h[D + dj * P:D + (dj + 1) * P, :], wtotL.ap,
              reads=[wtotL], writes=[agin_d])

    s.emit(g, lambda e: e.collective_compute(
               "AllGather", ALU.bypass, replica_groups=rg,
               ins=[agin_h[:]], outs=[agout_h[:]]),
           reads=[agin_d], writes=[agout_d], dma=False)

    s.emit(v, lambda e: e.memset(zerot.ap, 0.0), writes=[zerot])
    for dj in range(DBLK):
        s.dma(dtb.ap, dtl_h[dj * P:(dj + 1) * P, :], reads=[dtl_d], writes=[dtb])
        s.emit(v, lambda e: e.memset(hinT.ap, 0.0), writes=[hinT])
        for j in range(NCORES):
            base = j * 2 * D
            s.dma(hjT.ap, agout_h[base + dj * P:base + (dj + 1) * P, :],
                  reads=[agout_d], writes=[hjT])
            s.dma(wjT.ap, agout_h[base + D + dj * P:base + D + (dj + 1) * P, :],
                  reads=[agout_d], writes=[wjT])
            s.emit(v, lambda e: e.tensor_mul(e1T.ap, wjT.ap, hinT.ap),
                   reads=[wjT, hinT], writes=[e1T])
            s.emit(v, lambda e: e.tensor_add(e1T.ap, e1T.ap, hjT.ap),
                   reads=[e1T, hjT], writes=[e1T])
            s.emit(v, lambda e: e.tensor_sub(e1T.ap, e1T.ap, hinT.ap),
                   reads=[e1T, hinT], writes=[e1T])
            s.emit(v, lambda e, j=j: e.scalar_tensor_tensor(
                       hinT.ap, e1T.ap, selm[:, j:j + 1], hinT.ap, ALU.mult, ALU.add),
                   reads=[e1T, selm, hinT], writes=[hinT])
        scan_block(dj, lambda n: hinT[:, n:n + 1], zerot, y_first=False)
        zview = Buf(C["w2b"][1][0:P, 0:2048])
        gview = Buf(C["w2b"][1][0:P, 2048:4096])
        s.dma(zview.ap, z_h[dj * P:(dj + 1) * P, :], reads=[z_d], writes=[zview])
        s.emit(v, lambda e, dj=dj: e.tensor_tensor(
                   gview.ap, yloc[dj].ap, zview.ap, ALU.mult),
               reads=[yloc[dj], zview], writes=[gview])
        s.dma(gl_h[dj * P:(dj + 1) * P, :], gview.ap, reads=[gview], writes=[gl_d])

    emit_w1_load(s, C, w1_h)
    gt = C["h1"][:DBLK]
    xo_d = Buf(None)
    for f in range(NT):
        sl = slice(f * TF, (f + 1) * TF)
        for kb in range(DBLK):
            s.dma(xs[kb].ap, x2_h[kb * P:(kb + 1) * P, sl], reads=[x2_d], writes=[xs[kb]])
        for kb in range(DBLK):
            st = C["wst"][kb % 2]
            s.dma(st[:, 0:D], wo_h[kb * P:(kb + 1) * P, :], writes=[st])
            s.emit(v, lambda e, kb=kb, st=st: e.tensor_copy(
                       C["w2b"][0][:, kb * 512:kb * 512 + 512], st[:, 0:512]),
                   reads=[st], writes=[C["w2b"][0]])
            s.emit(v, lambda e, kb=kb, st=st: e.tensor_copy(
                       C["w2b"][1][:, kb * 512:kb * 512 + 512], st[:, 512:D]),
                   reads=[st], writes=[C["w2b"][1]])
        for kb in range(DBLK):
            s.dma(gt[kb].ap, gl_h[kb * P:(kb + 1) * P, sl], reads=[gl_d], writes=[gt[kb]])
        for mo in range(DBLK):
            po = C["ps_o"][mo % 2]
            half, off = divmod(mo * P, 512)
            wob = C["w2b"][half]
            for kb in range(DBLK):
                s.mm(po, wob, wob[:, kb * 512 + off:kb * 512 + off + P],
                     gt[kb], gt[kb].ap, start=(kb == 0), stop=(kb == DBLK - 1))
            s.emit(v, lambda e, po=po, mo=mo: e.scalar_tensor_tensor(
                       xs[mo].ap, po.ap, boc[:, mo:mo + 1], xs[mo].ap, ALU.add, ALU.add),
                   reads=[po, boc, xs[mo]], writes=[xs[mo]], psum_src=True)
        emit_ffn(s, C, xs, w2_h)
        for kb in range(DBLK):
            s.dma(xo_h[kb * P:(kb + 1) * P, sl], xs[kb].ap, reads=[xs[kb]], writes=[xo_d])

    s.emit("sync", lambda e: e.nop(), deps=[("sync", s.val["sync"])])
    s.finalize()
    return nc


def _col(b):
    n = b.shape[0] // P
    return np.ascontiguousarray(b.reshape(n, P).T)


def kernel(**inputs):
    xf = np.asarray(inputs["L0_attn_x"], np.float32).reshape(B * L, D)
    shared = dict(
        w1=np.ascontiguousarray(np.asarray(inputs["L0_ffn_w1"], np.float32)),
        b1=_col(np.asarray(inputs["L0_ffn_b1"], np.float32)),
        w2=np.ascontiguousarray(np.asarray(inputs["L0_ffn_w2"], np.float32)),
        b2=_col(np.asarray(inputs["L0_ffn_b2"], np.float32)),
        wz=np.ascontiguousarray(np.asarray(inputs["L0_attn_Wz"], np.float32)),
        wx=np.ascontiguousarray(np.asarray(inputs["L0_attn_Wx"], np.float32)),
        wdt=np.ascontiguousarray(np.asarray(inputs["L0_attn_Wdt"], np.float32)),
        bz=_col(np.asarray(inputs["L0_attn_bz"], np.float32)),
        bx=_col(np.asarray(inputs["L0_attn_bx"], np.float32)),
        bdt=_col(np.asarray(inputs["L0_attn_bdt"], np.float32)),
        wb=np.ascontiguousarray(np.asarray(inputs["L0_attn_WB"], np.float32)),
        wc=np.ascontiguousarray(np.asarray(inputs["L0_attn_WC"], np.float32)),
        bb=np.ascontiguousarray(np.asarray(inputs["L0_attn_bB"], np.float32).reshape(NSTATE, 1)),
        bc=np.ascontiguousarray(np.asarray(inputs["L0_attn_bC"], np.float32).reshape(NSTATE, 1)),
        wo=np.ascontiguousarray(np.asarray(inputs["L0_attn_Wout"], np.float32)),
        bo=_col(np.asarray(inputs["L0_attn_bout"], np.float32)),
    )
    alog = np.asarray(inputs["L0_attn_A_log"], np.float32)
    core_ids = list(range(NCORES))
    nc = build_fused(core_ids)
    maps = []
    for i in range(NCORES):
        m = dict(shared)
        m["x"] = np.ascontiguousarray(xf[i * TC:(i + 1) * TC, :].T)
        m["alog"] = np.ascontiguousarray(alog)
        sel = np.zeros((P, NCORES), np.float32)
        for j in range(NCORES):
            if j // 4 == i // 4 and j % 4 < i % 4:
                sel[:, j] = 1.0
        m["selm"] = sel
        maps.append(m)
    res = run_bass_kernel_spmd(nc, maps, core_ids=core_ids)
    out = np.empty((B * L, D), np.float32)
    for i in range(NCORES):
        out[i * TC:(i + 1) * TC, :] = np.asarray(res.results[i]["xo"], np.float32).T
    return out.reshape(B, L, D)


# version 08
# speedup vs baseline: 1.5024x; 1.1142x over previous
import math
from contextlib import ExitStack

import numpy as np
import ml_dtypes

import concourse.bass as bass
import concourse.mybir as mybir
from concourse.bass_utils import run_bass_kernel_spmd

F32 = mybir.dt.float32
I32 = mybir.dt.int32
BF16 = mybir.dt.bfloat16
AF = mybir.ActivationFunctionType
ALU = mybir.AluOpType
AX = mybir.AxisListType

P = 128
D = 1024
FF = 4096
NSTATE = 16
B = 2
L = 8192
NCORES = 8
TC = B * L // NCORES
DBLK = D // P
FBLK = FF // P
TF = 512
NT = TC // TF
TP2 = 2048
EPS = 1e-5

SERIALIZE = True
CHAIN_EXEMPT = set()
ISSUE_CHAIN = True
PE_EXEMPT = False
DEBUG_TAPS = False


def _gelu64(x):
    x = np.asarray(x, np.float64)
    return 0.5 * x * (1.0 + np.vectorize(math.erf)(x / math.sqrt(2.0)))

def _build_pwl():
    X = np.linspace(-4.0, 4.0, 33)
    Y = _gelu64(X)
    j = np.arange(17)
    qv = 256.0 * Y[16 - j]
    g = (qv[1:] - qv[:-1]) / 64.0
    g0 = g[0]
    dj = g[1:] - g[:-1]
    t = np.arange(-40000, 40001, dtype=np.float64)
    uc = np.minimum(np.abs(t), 1024.0)
    q = g0 * uc
    for k in range(1, 16):
        q = q + dj[k - 1] * np.maximum(uc - 64.0 * k, 0.0)
    mine = np.maximum(t, 0.0) + np.round(q)
    xq = t / 256.0
    y = np.interp(xq, X, Y)
    y = np.where(xq > 4.0, xq, np.where(xq < -4.0, 0.0, y))
    ref = np.round(256.0 * y)
    assert np.array_equal(mine, ref), "PWL hinge table does not match reference"
    return float(g0), [float(v) for v in dj]

_PWL_G0, _PWL_DJ = _build_pwl()


class Buf:
    __slots__ = ("t", "last_write", "readers")
    def __init__(self, t):
        self.t = t
        self.last_write = None
        self.readers = []
    def __getitem__(self, sl):
        return self.t[sl]
    @property
    def ap(self):
        return self.t[:]


class Sch:
    ENGS = ("sync", "vector", "scalar", "gpsimd", "tensor")
    SEMS = ENGS + ("syncI",)

    def __init__(self, nc):
        self.nc = nc
        self.ops = {e: [] for e in self.ENGS}
        self.val = {e: 0 for e in self.SEMS}
        self.es = ExitStack()
        self.sem = {e: self.es.enter_context(nc.semaphore(f"s_{e}")) for e in self.SEMS}
        self.prev = None

    def emit(self, eng, fn, reads=(), writes=(), deps=(), dma=False, psum_src=False,
             ccomp=False):
        dl = list(deps)
        if SERIALIZE and eng not in CHAIN_EXEMPT \
                and self.prev is not None and self.prev[0] != eng:
            dl.append(self.prev)
        for b in reads:
            if b.last_write is not None:
                dl.append(b.last_write)
        for b in writes:
            if b.last_write is not None:
                dl.append(b.last_write)
            dl.extend(b.readers)
        inc = 16 if dma else 1
        self.val[eng] += inc
        tok = (eng, self.val[eng])
        self.ops[eng].append((fn, dl, inc, psum_src, None))
        if dma and ISSUE_CHAIN and not ccomp:
            self.val["syncI"] += 1
            itok = ("syncI", self.val["syncI"])
            self.ops[eng].append(
                (lambda e: e.nop(cycle_cnt=0, nofuse=True), [], 1, False, "syncI"))
            if eng not in CHAIN_EXEMPT:
                self.prev = itok
        elif eng not in CHAIN_EXEMPT:
            self.prev = tok
        for b in reads:
            b.readers.append(tok)
        for b in writes:
            b.last_write = tok
            b.readers = []
        return tok

    def mm(self, ps, lhsT_buf, lhsT_ap, rhs_buf, rhs_ap, start, stop, out_ap=None):
        dl = []
        if SERIALIZE and not PE_EXEMPT \
                and self.prev is not None and self.prev[0] != "tensor":
            dl.append(self.prev)
        if lhsT_buf is not None and lhsT_buf.last_write is not None:
            dl.append(lhsT_buf.last_write)
        if rhs_buf is not None and rhs_buf.last_write is not None:
            dl.append(rhs_buf.last_write)
        if start:
            if ps.last_write is not None:
                dl.append(ps.last_write)
            dl.extend(ps.readers)
        self.val["tensor"] += 1
        tok = ("tensor", self.val["tensor"])
        if not PE_EXEMPT:
            self.prev = tok
        oap = ps.ap if out_ap is None else out_ap
        self.ops["tensor"].append(
            (lambda e, oap=oap, la=lhsT_ap, ra=rhs_ap, st=start, sp=stop:
             e.matmul(oap, la, ra, start=st, stop=sp), dl, 1, False, None))
        if lhsT_buf is not None:
            lhsT_buf.readers.append(tok)
        if rhs_buf is not None:
            rhs_buf.readers.append(tok)
        if stop:
            ps.last_write = tok
            ps.readers = []
        return tok

    def dma(self, dst_ap, src_ap, reads=(), writes=(), deps=(), eng="sync",
            ccomp=False):
        import numpy as _np
        assert _np.prod(dst_ap.shape) == _np.prod(src_ap.shape), (
            f"dma size mismatch {dst_ap.shape} <- {src_ap.shape}")
        return self.emit(
            eng, lambda e: e.dma_start(out=dst_ap, in_=src_ap),
            reads=reads, writes=writes, deps=deps, dma=True, ccomp=ccomp)

    def finalize(self):
        nc = self.nc
        with nc.Block() as block:
            for eng in self.ENGS:
                if not self.ops[eng]:
                    continue
                def make(eng):
                    def body(e):
                        water = {s2: 0 for s2 in self.SEMS}
                        for fn, deps, inc, psrc, semov in self.ops[eng]:
                            need = {}
                            for (de, dv) in deps:
                                if dv > water[de]:
                                    need[de] = max(need.get(de, 0), dv)
                            waited_pe = False
                            for de, dv in need.items():
                                e.wait_ge(self.sem[de], dv)
                                water[de] = dv
                                if de == "tensor":
                                    waited_pe = True
                            if psrc and waited_pe:
                                e.nop(cycle_cnt=192, nofuse=True)
                            ins = fn(e)
                            ins.then_inc(self.sem[semov or eng], inc)
                    return body
                getattr(block, eng)(make(eng))
        self.es.close()


def sbuf(nc, name, shape, dtype):
    return Buf(nc.alloc_sbuf_tensor(name, shape, dtype))

def psum(nc, name, shape, dtype=F32):
    return Buf(nc.alloc_psum_tensor(name, shape, dtype))



def emit_ln(s, C, xblk, xn_out):
    v, a = "vector", "scalar"
    ones_col, ones_row = C["ones_col"], C["ones_row"]
    ps_s1, ps_s2, ps_r, ps_m = C["ps_s1"], C["ps_s2"], C["ps_rstd"], C["ps_mmr"]
    st_mean, st_rstd, st_mmr = C["st_mean"], C["st_rstd"], C["st_mmr"]
    tmp_sq, tmp_sq2, tmp32 = C["tmp_sq"], C["tmp_sq2"], C["tmp32"]
    for kb in range(DBLK):
        s.emit(v, lambda e, kb=kb: e.tensor_copy(tmp_sq.ap, xblk[kb].ap),
               reads=[xblk[kb]], writes=[tmp_sq])
        s.mm(ps_s1, ones_col, ones_col.ap, tmp_sq, tmp_sq.ap,
             start=(kb == 0), stop=(kb == DBLK - 1))
        s.emit(a, lambda e, kb=kb: e.activation(tmp_sq2.ap, xblk[kb].ap, AF.Square),
               reads=[xblk[kb]], writes=[tmp_sq2])
        s.mm(ps_s2, ones_col, ones_col.ap, tmp_sq2, tmp_sq2.ap,
             start=(kb == 0), stop=(kb == DBLK - 1))
    s.emit(v, lambda e: e.tensor_scalar(st_mean.ap, ps_s1.ap, 1.0 / D, None, ALU.mult),
           reads=[ps_s1], writes=[st_mean], psum_src=True)
    s.emit(v, lambda e: e.tensor_mul(st_rstd.ap, st_mean.ap, st_mean.ap),
           reads=[st_mean], writes=[st_rstd])
    s.emit(v, lambda e: e.scalar_tensor_tensor(st_rstd.ap, ps_s2.ap, 1.0 / D,
                                               st_rstd.ap, ALU.mult, ALU.subtract),
           reads=[ps_s2, st_rstd], writes=[st_rstd], psum_src=True)
    s.emit(a, lambda e: e.activation(st_rstd.ap, st_rstd.ap, AF.Ln,
                                     bias=C["eps1"][0:1, 0:1]),
           reads=[st_rstd, C["eps1"]], writes=[st_rstd])
    s.emit(a, lambda e: e.activation(st_rstd.ap, st_rstd.ap, AF.Exp, scale=-0.5),
           reads=[st_rstd], writes=[st_rstd])
    s.emit(v, lambda e: e.tensor_mul(st_mmr.ap, st_mean.ap, st_rstd.ap),
           reads=[st_mean, st_rstd], writes=[st_mmr])
    s.mm(ps_r, ones_row, ones_row.ap, st_rstd, st_rstd.ap, start=True, stop=True)
    s.mm(ps_m, ones_row, ones_row.ap, st_mmr, st_mmr.ap, start=True, stop=True)
    for kb in range(DBLK):
        s.emit(v, lambda e, kb=kb: e.tensor_mul(tmp32.ap, xblk[kb].ap, ps_r.ap),
               reads=[xblk[kb], ps_r], writes=[tmp32], psum_src=True)
        s.emit(v, lambda e, kb=kb: e.tensor_sub(xn_out[kb].ap, tmp32.ap, ps_m.ap),
               reads=[tmp32, ps_m], writes=[xn_out[kb]], psum_src=True)


def emit_pwl4(s, C, g0):
    v, a = "vector", "scalar"
    ti4, uc4, ac4, ht4 = C["ti4"], C["uc4"], C["ac4"], C["ht4"]
    for i in range(4):
        m = g0 + i
        ph = C["ps4"][i]
        s.emit(v, lambda e, ph=ph, m=m, i=i: e.tensor_scalar(
                   ti4[:, i * TF:(i + 1) * TF], ph.ap, 256.0, C["b1c"][:, m:m + 1],
                   ALU.mult, ALU.add),
               reads=[ph, C["b1c_buf"]], writes=[ti4], psum_src=True)
    s.emit(v, lambda e: e.scalar_tensor_tensor(uc4.ap, ti4.ap, -1.0, ti4.ap,
                                               ALU.mult, ALU.max),
           reads=[ti4], writes=[uc4])
    s.emit(v, lambda e: e.tensor_scalar(uc4.ap, uc4.ap, 1024.0, None, ALU.min),
           reads=[uc4], writes=[uc4])
    s.emit(v, lambda e: e.tensor_scalar(ac4.ap, uc4.ap, _PWL_G0, None, ALU.mult),
           reads=[uc4], writes=[ac4])
    s.emit(v, lambda e: e.scalar_tensor_tensor(ac4.ap, ti4.ap, 0.0, ac4.ap,
                                               ALU.max, ALU.add),
           reads=[ti4, ac4], writes=[ac4])
    for k0 in range(1, 16, 2):
        ks = [k for k in (k0, k0 + 1) if k <= 15]
        for j, k in enumerate(ks):
            ad = abs(_PWL_DJ[k - 1])
            s.emit(a, lambda e, ad=ad, k=k, j=j: e.activation(
                       ht4[j].ap, uc4.ap, AF.Relu, bias=C["hingeb"][:, k - 1:k],
                       scale=ad),
                   reads=[uc4, C["hingeb"]], writes=[ht4[j]])
        for j, k in enumerate(ks):
            op = ALU.add if _PWL_DJ[k - 1] >= 0 else ALU.subtract
            s.emit(v, lambda e, j=j, op=op: e.tensor_tensor(
                       ac4.ap, ac4.ap, ht4[j].ap, op),
                   reads=[ht4[j], ac4], writes=[ac4])
    s.emit(v, lambda e: e.tensor_scalar(ti4.ap, ac4.ap, 1.0, None, ALU.mult),
           reads=[ac4], writes=[ti4])
    hw = [C["h1"][g0 + i] for i in range(4)]
    s.emit(v, lambda e, g0=g0: e.tensor_copy(
               C["h1big"][:, g0 * TF:(g0 + 4) * TF], ti4.ap),
           reads=[ti4], writes=hw)


def emit_ffn(s, C, xblk, w2_h):
    v = "vector"
    xn = C["xn"]
    emit_ln(s, C, xblk, xn)
    h1 = C["h1"]
    for g0 in range(0, FBLK, 4):
        for i in range(4):
            m = g0 + i
            ph = C["ps4"][i]
            for kb in range(DBLK):
                s.mm(ph, C["w1b"][kb], C["w1b"][kb][:, m * P:(m + 1) * P],
                     xn[kb], xn[kb].ap, start=(kb == 0), stop=(kb == DBLK - 1))
        emit_pwl4(s, C, g0)
    w2r = w2_h.rearrange("(m p) o -> p m o", p=P)
    for mo in range(DBLK):
        w2b = C["w2b"][mo % 2]
        for h in range(2):
            w2s = C["wst"][h]
            hm = FBLK // 2
            s.dma(w2s.ap, w2r[:, h * hm:(h + 1) * hm, mo * P:(mo + 1) * P], writes=[w2s])
            s.emit(v, lambda e, w2b=w2b, w2s=w2s, h=h, hm=hm: e.tensor_copy(
                       w2b[:, h * hm * P:(h + 1) * hm * P], w2s.ap),
                   reads=[w2s], writes=[w2b])
        po = C["ps_o"][mo % 2]
        for m in range(FBLK):
            s.mm(po, w2b, w2b[:, m * P:(m + 1) * P], h1[m], h1[m].ap,
                 start=(m == 0), stop=(m == FBLK - 1))
        t3 = C["t3"][mo % 2]
        s.emit(v, lambda e, t3=t3, po=po, mo=mo: e.tensor_scalar(
                   t3.ap, po.ap, C["b2c"][:, mo:mo + 1], None, ALU.add),
               reads=[po, C["b2c_buf"]], writes=[t3], psum_src=True)
        s.emit(v, lambda e, t3=t3, mo=mo: e.scalar_tensor_tensor(
                   xblk[mo].ap, t3.ap, 1.0 / 512.0, xblk[mo].ap, ALU.mult, ALU.add),
               reads=[t3, xblk[mo]], writes=[xblk[mo]])


def emit_w1_load(s, C, w1_h):
    v = "vector"
    for kb in range(DBLK):
        for h in range(2):
            st = C["wst"][(kb * 2 + h) % 2]
            sl = slice(h * (FF // 2), (h + 1) * (FF // 2))
            s.dma(st.ap, w1_h[kb * P:(kb + 1) * P, sl], writes=[st])
            s.emit(v, lambda e, kb=kb, st=st, sl=sl: e.tensor_copy(
                       C["w1b"][kb][:, sl], st.ap),
                   reads=[st], writes=[C["w1b"][kb]])



def build_fused(core_ids):
    nc = bass.Bass("TRN2", target_bir_lowering=False, debug=False)
    x_h = nc.dram_tensor("x", [D, TC], F32, kind="ExternalInput")
    w1_h = nc.dram_tensor("w1", [D, FF], F32, kind="ExternalInput")
    b1_h = nc.dram_tensor("b1", [P, FBLK], F32, kind="ExternalInput")
    w2_h = nc.dram_tensor("w2", [FF, D], F32, kind="ExternalInput")
    b2_h = nc.dram_tensor("b2", [P, DBLK], F32, kind="ExternalInput")
    wz_h = nc.dram_tensor("wz", [D, D], F32, kind="ExternalInput")
    wx_h = nc.dram_tensor("wx", [D, D], F32, kind="ExternalInput")
    wdt_h = nc.dram_tensor("wdt", [D, D], F32, kind="ExternalInput")
    bz_h = nc.dram_tensor("bz", [P, DBLK], F32, kind="ExternalInput")
    bx_h = nc.dram_tensor("bx", [P, DBLK], F32, kind="ExternalInput")
    bdt_h = nc.dram_tensor("bdt", [P, DBLK], F32, kind="ExternalInput")
    wb_h = nc.dram_tensor("wb", [D, NSTATE], F32, kind="ExternalInput")
    wc_h = nc.dram_tensor("wc", [D, NSTATE], F32, kind="ExternalInput")
    bb_h = nc.dram_tensor("bb", [NSTATE, 1], F32, kind="ExternalInput")
    bc_h = nc.dram_tensor("bc", [NSTATE, 1], F32, kind="ExternalInput")
    al_h = nc.dram_tensor("alog", [D, NSTATE], F32, kind="ExternalInput")
    wo_h = nc.dram_tensor("wo", [D, D], F32, kind="ExternalInput")
    bo_h = nc.dram_tensor("bo", [P, DBLK], F32, kind="ExternalInput")
    xo_h = nc.dram_tensor("xo", [D, TC], F32, kind="ExternalOutput")
    KI = "ExternalOutput" if DEBUG_TAPS else "Internal"
    dtl_h = nc.dram_tensor("dtl", [D, TC], F32, kind=KI)
    vvl_h = nc.dram_tensor("vvl", [D, TC], F32, kind=KI)
    bml_h = nc.dram_tensor("bml", [NSTATE, TC], F32, kind=KI)
    ccl_h = nc.dram_tensor("ccl", [NSTATE, TC], F32, kind=KI)
    gl_h = nc.dram_tensor("gl", [D, TC], BF16, kind=KI)
    agin_h = nc.dram_tensor("agin", [2 * D, NSTATE], F32, kind="Internal")
    agout_h = nc.dram_tensor("agout", [NCORES * 2 * D, NSTATE], F32, kind="Internal")
    x2_h = nc.dram_tensor("x2i", [D, TC], F32, kind=KI)
    z_h = nc.dram_tensor("zi", [D, TC], BF16, kind=KI)
    selm_h = nc.dram_tensor("selm", [P, NCORES], F32, kind="ExternalInput")

    s = Sch(nc)
    v, a, g, t = "vector", "scalar", "gpsimd", "tensor"

    C = {}
    C["ones_col"] = sbuf(nc, "ones_col", [P, 1], BF16)
    C["ones_row"] = sbuf(nc, "ones_row", [1, P], F32)
    s.emit(v, lambda e: e.memset(C["ones_col"].ap, 1.0), writes=[C["ones_col"]])
    s.emit(v, lambda e: e.memset(C["ones_row"].ap, 1.0), writes=[C["ones_row"]])
    C["eps1"] = sbuf(nc, "eps1", [1, 1], F32)
    s.emit(v, lambda e: e.memset(C["eps1"].ap, EPS), writes=[C["eps1"]])
    psA = nc.alloc_psum_tensor("psA", [P, 4 * TF], F32)
    psB = nc.alloc_psum_tensor("psB", [P, 4 * TF], F32)
    C["ps_s1"] = Buf(psA[0:1, 0:TF])
    C["ps_s2"] = Buf(psA[0:1, TF:2 * TF])
    C["ps_rstd"] = Buf(psA[:, 2 * TF:3 * TF])
    C["ps_mmr"] = Buf(psA[:, 3 * TF:4 * TF])
    C["ps_h"] = [Buf(psB[:, 0:TF]), Buf(psB[:, TF:2 * TF])]
    C["ps_o"] = [Buf(psB[:, 2 * TF:3 * TF]), Buf(psB[:, 3 * TF:4 * TF])]
    C["psA"], C["psB"] = psA, psB
    C["st_mean"] = sbuf(nc, "st_mean", [1, TF], F32)
    C["st_rstd"] = sbuf(nc, "st_rstd", [1, TF], F32)
    C["st_mmr"] = sbuf(nc, "st_mmr", [1, TF], F32)
    C["tmp_sq"] = sbuf(nc, "tmp_sq", [P, TF], BF16)
    C["tmp_sq2"] = sbuf(nc, "tmp_sq2", [P, TF], BF16)
    C["tmp32"] = sbuf(nc, "tmp32", [P, TF], F32)
    C["xn"] = [sbuf(nc, f"xn{kb}", [P, TF], BF16) for kb in range(DBLK)]
    w1big = nc.alloc_sbuf_tensor("w1big", [P, DBLK * FF], BF16)
    h1big = nc.alloc_sbuf_tensor("h1big", [P, FBLK * TF], BF16)
    C["w1b"] = [Buf(w1big[:, kb * FF:(kb + 1) * FF]) for kb in range(DBLK)]
    C["h1"] = [Buf(h1big[:, m * TF:(m + 1) * TF]) for m in range(FBLK)]
    C["wst"] = [sbuf(nc, f"wst{i}", [P, FF // 2], F32) for i in range(2)]
    C["w2b"] = [sbuf(nc, f"w2b{i}", [P, FF], BF16) for i in range(2)]
    C["t3"] = [sbuf(nc, f"t3_{i}", [P, TF], I32) for i in range(2)]
    C["ti4"] = sbuf(nc, "pw_ti4", [P, 4 * TF], I32)
    C["uc4"] = sbuf(nc, "pw_uc4", [P, 4 * TF], F32)
    C["ac4"] = sbuf(nc, "pw_ac4", [P, 4 * TF], F32)
    C["ht4"] = [Buf(C["w2b"][i].t[:, 0:FF].bitcast(F32)) for i in range(2)]
    C["ps4"] = [C["ps_h"][0], C["ps_h"][1], C["ps_o"][0], C["ps_o"][1]]
    C["h1big"] = h1big
    hb = sbuf(nc, "hingeb", [P, 15], F32)
    for k in range(1, 16):
        ad = abs(_PWL_DJ[k - 1])
        s.emit(v, lambda e, k=k, ad=ad: e.memset(hb[:, k - 1:k], -ad * 64.0 * k),
               writes=[hb])
    C["hingeb"] = hb
    def bias_col(name, h, n, scale=None):
        raw = sbuf(nc, name + "r", [P, n], F32)
        s.dma(raw.ap, h[:], writes=[raw])
        if scale is None:
            return raw
        sc = sbuf(nc, name + "c", [P, n], F32)
        s.emit(v, lambda e: e.tensor_scalar(sc.ap, raw.ap, scale, None, ALU.mult),
               reads=[raw], writes=[sc])
        return sc
    b1c = bias_col("b1", b1_h, FBLK, 256.0)
    b2c = bias_col("b2", b2_h, DBLK, 256.0)
    C["b1c"], C["b1c_buf"] = b1c.t, b1c
    C["b2c"], C["b2c_buf"] = b2c.t, b2c
    bzc = bias_col("bz", bz_h, DBLK)
    bxc = bias_col("bx", bx_h, DBLK)
    bdtc = bias_col("bdt", bdt_h, DBLK)
    boc = bias_col("bo", bo_h, DBLK)
    bb16 = sbuf(nc, "bbc", [NSTATE, 1], F32)
    s.dma(bb16.ap, bb_h[:], writes=[bb16])
    bc16 = sbuf(nc, "bcc", [NSTATE, 1], F32)
    s.dma(bc16.ap, bc_h[:], writes=[bc16])

    xs = [sbuf(nc, f"xs{kb}", [P, TF], F32) for kb in range(DBLK)]
    x2_d, ain_d, aout_d, yin_d, yout_d, z_d = (Buf(None) for _ in range(6))

    emit_w1_load(s, C, w1_h)
    for f in range(NT):
        sl = slice(f * TF, (f + 1) * TF)
        for kb in range(DBLK):
            s.dma(xs[kb].ap, x_h[kb * P:(kb + 1) * P, sl], writes=[xs[kb]])
        emit_ffn(s, C, xs, w2_h)
        for kb in range(DBLK):
            s.dma(x2_h[kb * P:(kb + 1) * P, sl], xs[kb].ap,
                  reads=[xs[kb]], writes=[x2_d])

    for name, h, off in (("wz", wz_h, 0), ("wx", wx_h, D), ("wdt", wdt_h, 2 * D)):
        for kb in range(DBLK):
            st = C["wst"][kb % 2]
            s.dma(st[:, 0:D], h[kb * P:(kb + 1) * P, :], writes=[st])
            s.emit(v, lambda e, kb=kb, st=st, off=off: e.tensor_copy(
                       C["w1b"][kb][:, off:off + D], st[:, 0:D]),
                   reads=[st], writes=[C["w1b"][kb]])
    wsmall = {}
    for name, h in (("wb", wb_h), ("wc", wc_h)):
        st = C["wst"][0]
        s.dma(st[:, 0:DBLK * NSTATE], h.rearrange("(k p) n -> p k n", p=P), writes=[st])
        wbv = sbuf(nc, f"{name}b", [P, DBLK * NSTATE], BF16)
        s.emit(v, lambda e, wbv=wbv, st=st: e.tensor_copy(wbv.ap, st[:, 0:DBLK * NSTATE]),
               reads=[st], writes=[wbv])
        wsmall[name] = wbv

    xssm = [Buf(h1big[:, i * 1024:(i + 1) * 1024].bitcast(F32)) for i in range(2)]
    dtt = [Buf(h1big[:, (2 + i) * 1024:(3 + i) * 1024].bitcast(F32)) for i in range(2)]
    pbt = [Buf(h1big[:, (4 + i) * 1024:(5 + i) * 1024].bitcast(F32)) for i in range(2)]
    sc1 = [Buf(h1big[:, (6 + i) * 1024:(7 + i) * 1024].bitcast(F32)) for i in range(2)]
    zt = sbuf(nc, "zt", [P, TF], BF16)
    sm16 = sbuf(nc, "sm16", [NSTATE, TF], F32)
    ps_bc = Buf(psA[0:NSTATE, 2 * TF:3 * TF])

    dtl_d, vvl_d, bml_d, ccl_d, gl_d, agin_d, agout_d = (Buf(None) for _ in range(7))
    for f in range(NT):
        sl = slice(f * TF, (f + 1) * TF)
        for kb in range(DBLK):
            s.dma(xs[kb].ap, x2_h[kb * P:(kb + 1) * P, sl], reads=[x2_d], writes=[xs[kb]])
        xn = C["xn"]
        emit_ln(s, C, xs, xn)
        for mo in range(DBLK):
            po = C["ps_o"][mo % 2]
            for kb in range(DBLK):
                s.mm(po, C["w1b"][kb], C["w1b"][kb][:, mo * P:(mo + 1) * P],
                     xn[kb], xn[kb].ap, start=(kb == 0), stop=(kb == DBLK - 1))
            s.emit(a, lambda e, po=po, mo=mo: e.activation(
                       zt.ap, po.ap, AF.Sigmoid, bias=bzc[:, mo:mo + 1]),
                   reads=[po, bzc], writes=[zt], psum_src=True)
            s.dma(z_h[mo * P:(mo + 1) * P, sl], zt.ap, reads=[zt], writes=[z_d])
        for mo in range(DBLK):
            i2 = mo % 2
            po = C["ps_o"][i2]
            for kb in range(DBLK):
                s.mm(po, C["w1b"][kb], C["w1b"][kb][:, 2 * D + mo * P:2 * D + (mo + 1) * P],
                     xn[kb], xn[kb].ap, start=(kb == 0), stop=(kb == DBLK - 1))
            s.emit(a, lambda e, po=po, mo=mo, i2=i2: e.activation(
                       pbt[i2].ap, po.ap, AF.Identity, bias=bdtc[:, mo:mo + 1]),
                   reads=[po, bdtc], writes=[pbt[i2]], psum_src=True)
            s.emit(v, lambda e, i2=i2: e.scalar_tensor_tensor(
                       sc1[i2].ap, pbt[i2].ap, -1.0, pbt[i2].ap, ALU.mult, ALU.max),
                   reads=[pbt[i2]], writes=[sc1[i2]])
            s.emit(a, lambda e, i2=i2: e.activation(sc1[i2].ap, sc1[i2].ap, AF.Exp,
                                                    scale=-1.0),
                   reads=[sc1[i2]], writes=[sc1[i2]])
            s.emit(a, lambda e, i2=i2: e.activation(sc1[i2].ap, sc1[i2].ap, AF.Ln,
                                                    bias=1.0),
                   reads=[sc1[i2]], writes=[sc1[i2]])
            s.emit(v, lambda e, i2=i2: e.scalar_tensor_tensor(
                       dtt[i2].ap, pbt[i2].ap, 0.0, sc1[i2].ap, ALU.max, ALU.add),
                   reads=[pbt[i2], sc1[i2]], writes=[dtt[i2]])
            s.dma(dtl_h[mo * P:(mo + 1) * P, sl], dtt[i2].ap, reads=[dtt[i2]], writes=[dtl_d])
            po2 = C["ps_h"][i2]
            for kb in range(DBLK):
                s.mm(po2, C["w1b"][kb], C["w1b"][kb][:, D + mo * P:D + (mo + 1) * P],
                     xn[kb], xn[kb].ap, start=(kb == 0), stop=(kb == DBLK - 1))
            s.emit(a, lambda e, po2=po2, mo=mo, i2=i2: e.activation(
                       xssm[i2].ap, po2.ap, AF.Identity, bias=bxc[:, mo:mo + 1]),
                   reads=[po2, bxc], writes=[xssm[i2]], psum_src=True)
            s.emit(v, lambda e, i2=i2: e.tensor_mul(xssm[i2].ap, xssm[i2].ap, dtt[i2].ap),
                   reads=[dtt[i2], xssm[i2]], writes=[xssm[i2]])
            s.dma(vvl_h[mo * P:(mo + 1) * P, sl], xssm[i2].ap, reads=[xssm[i2]], writes=[vvl_d])
        for name, out_h2, out_d2, bias in (("wb", bml_h, bml_d, bb16),
                                           ("wc", ccl_h, ccl_d, bc16)):
            wbv = wsmall[name]
            for kb in range(DBLK):
                s.mm(ps_bc, wbv, wbv[:, kb * NSTATE:(kb + 1) * NSTATE],
                     xn[kb], xn[kb].ap, start=(kb == 0), stop=(kb == DBLK - 1),
                     out_ap=ps_bc.ap)
            s.emit(a, lambda e, bias=bias: e.activation(
                       sm16.ap, ps_bc.ap, AF.Identity, bias=bias[:, 0:1]),
                   reads=[ps_bc, bias], writes=[sm16], psum_src=True)
            s.dma(out_h2[:, sl], sm16.ap, reads=[sm16], writes=[out_d2])

    rg = [list(core_ids)]
    acol = sbuf(nc, "acol", [P, DBLK * NSTATE], F32)
    s.dma(acol.ap, al_h.rearrange("(k p) n -> p k n", p=P), writes=[acol])
    s.emit(a, lambda e: e.activation(acol.ap, acol.ap, AF.Exp), reads=[acol], writes=[acol])
    s.emit(v, lambda e: e.tensor_scalar(acol.ap, acol.ap, -1.0, None, ALU.mult),
           reads=[acol], writes=[acol])
    yloc = [Buf(w1big[:, dj * 4096:(dj + 1) * 4096].bitcast(F32)) for dj in range(DBLK)]
    at = Buf(h1big[:, 0:4096].bitcast(F32))
    ut = Buf(h1big[:, 4096:8192].bitcast(F32))
    ht = Buf(h1big[:, 8192:12288].bitcast(F32))
    qt = Buf(h1big[:, 12288:16384].bitcast(F32))
    dtb = C["wst"][0]
    vvb = C["wst"][1]
    zerot = Buf(C["w2b"][0][:, 0:4096].bitcast(F32))
    rowB = [Buf(C["w2b"][0][0:1, 0:4096].bitcast(F32)),
            Buf(C["w2b"][0][32:33, 0:4096].bitcast(F32))]
    rowC = [Buf(C["w2b"][1][0:1, 0:4096].bitcast(F32)),
            Buf(C["w2b"][1][32:33, 0:4096].bitcast(F32))]
    rowbase = {id(rowB[0]): 0, id(rowB[1]): 32, id(rowC[0]): 0, id(rowC[1]): 32}
    ones128 = sbuf(nc, "ones128", [P, P], F32)
    s.emit(v, lambda e: e.memset(ones128.ap, 1.0), writes=[ones128])
    stot = sbuf(nc, "stot", [P, 1], F32)
    hendL = sbuf(nc, "hendL", [P, NSTATE], F32)
    wtotL = sbuf(nc, "wtotL", [P, NSTATE], F32)
    hinT = sbuf(nc, "hinT", [P, NSTATE], F32)
    e1T = sbuf(nc, "e1T", [P, NSTATE], F32)
    wjT = sbuf(nc, "wjT", [P, NSTATE], F32)
    hjT = sbuf(nc, "hjT", [P, NSTATE], F32)
    selm = sbuf(nc, "selmT", [P, NCORES], F32)
    s.dma(selm.ap, selm_h[:], writes=[selm])
    NPB = TC // TF

    pbS = Buf(psA[:, :])
    pcS = Buf(psB[:, :])

    dtres = [Buf(C["ti4"].t[:, :].bitcast(F32)), Buf(C["uc4"].t[:, :])]
    vvres = [Buf(C["ac4"].t[:, :]), C["wst"][0]]
    hendL2 = sbuf(nc, "hendL2", [P, NSTATE], F32)
    hinT2 = sbuf(nc, "hinT2", [P, NSTATE], F32)
    hends = [hendL, hendL2]
    hinTs = [hinT, hinT2]
    CH = 2

    def bcast_row(row, strip, arena):
        rb = rowbase[id(row)]
        for p in range(NPB):
            s.mm(strip, ones128, ones128[rb:rb + 1, :], row,
                 row[0:1, p * TF:(p + 1) * TF], start=True, stop=True,
                 out_ap=arena[:, p * TF:(p + 1) * TF])

    for ch in range(DBLK // CH):
        djs = [ch * CH + i for i in range(CH)]
        for i, dj in enumerate(djs):
            s.dma(dtres[i].ap, dtl_h[dj * P:(dj + 1) * P, :],
                  reads=[dtl_d], writes=[dtres[i]])
            s.dma(vvres[i].ap, vvl_h[dj * P:(dj + 1) * P, :],
                  reads=[vvl_d], writes=[vvres[i]])
        for n in range(NSTATE):
            i2 = n % 2
            rC, rB = rowC[i2], rowB[i2]
            s.dma(rC.ap, ccl_h[n:n + 1, :], reads=[ccl_d], writes=[rC])
            s.dma(rB.ap, bml_h[n:n + 1, :], reads=[bml_d], writes=[rB])
            bcast_row(rB, pbS, psA)
            bcast_row(rC, pcS, psB)
            for i, dj in enumerate(djs):
                s.emit(a, lambda e, i=i, n=n, dj=dj: e.activation(
                           at.ap, dtres[i].ap, AF.Exp,
                           scale=acol[:, dj * NSTATE + n:dj * NSTATE + n + 1]),
                       reads=[dtres[i], acol], writes=[at])
                s.emit(v, lambda e, i=i: e.tensor_tensor(
                           ut.ap, vvres[i].ap, pbS.ap, ALU.mult),
                       reads=[vvres[i], pbS], writes=[ut], psum_src=True)
                s.emit(v, lambda e: e.tensor_tensor_scan(
                           ht.ap, at.ap, ut.ap, 0.0, ALU.mult, ALU.add),
                       reads=[at, ut], writes=[ht])
                s.emit(v, lambda e, i=i, n=n: e.tensor_copy(
                           hends[i][:, n:n + 1], ht[:, TC - 1:TC]),
                       reads=[ht], writes=[hends[i]])
                s.emit(v, lambda e: e.tensor_tensor(qt.ap, ht.ap, pcS.ap, ALU.mult),
                       reads=[ht, pcS], writes=[qt], psum_src=True)
                if n == 0:
                    s.emit(v, lambda e, dj=dj: e.tensor_copy(yloc[dj].ap, qt.ap),
                           reads=[qt], writes=[yloc[dj]])
                else:
                    s.emit(v, lambda e, dj=dj: e.tensor_add(
                               yloc[dj].ap, yloc[dj].ap, qt.ap),
                           reads=[qt, yloc[dj]], writes=[yloc[dj]])
        for i, dj in enumerate(djs):
            s.emit(v, lambda e, i=i: e.tensor_reduce(
                       stot.ap, dtres[i].ap, AX.X, ALU.add),
                   reads=[dtres[i]], writes=[stot])
            for n in range(NSTATE):
                s.emit(a, lambda e, n=n, dj=dj: e.activation(
                           wtotL[:, n:n + 1], stot.ap, AF.Exp,
                           scale=acol[:, dj * NSTATE + n:dj * NSTATE + n + 1]),
                       reads=[stot, acol], writes=[wtotL])
            s.dma(agin_h[dj * P:(dj + 1) * P, :], hends[i].ap,
                  reads=[hends[i]], writes=[agin_d])
            s.dma(agin_h[D + dj * P:D + (dj + 1) * P, :], wtotL.ap,
                  reads=[wtotL], writes=[agin_d])

    s.emit(g, lambda e: e.collective_compute(
               "AllGather", ALU.bypass, replica_groups=rg,
               ins=[agin_h[:]], outs=[agout_h[:]]),
           reads=[agin_d], writes=[agout_d], dma=False)

    s.emit(v, lambda e: e.memset(zerot.ap, 0.0), writes=[zerot])
    for ch in range(DBLK // CH):
        djs = [ch * CH + i for i in range(CH)]
        for i, dj in enumerate(djs):
            s.dma(dtres[i].ap, dtl_h[dj * P:(dj + 1) * P, :],
                  reads=[dtl_d], writes=[dtres[i]])
            s.emit(v, lambda e, i=i: e.memset(hinTs[i].ap, 0.0), writes=[hinTs[i]])
            for j in range(NCORES):
                base = j * 2 * D
                s.dma(hjT.ap, agout_h[base + dj * P:base + (dj + 1) * P, :],
                      reads=[agout_d], writes=[hjT])
                s.dma(wjT.ap, agout_h[base + D + dj * P:base + D + (dj + 1) * P, :],
                      reads=[agout_d], writes=[wjT])
                s.emit(v, lambda e, i=i: e.tensor_mul(e1T.ap, wjT.ap, hinTs[i].ap),
                       reads=[wjT, hinTs[i]], writes=[e1T])
                s.emit(v, lambda e: e.tensor_add(e1T.ap, e1T.ap, hjT.ap),
                       reads=[e1T, hjT], writes=[e1T])
                s.emit(v, lambda e, i=i: e.tensor_sub(e1T.ap, e1T.ap, hinTs[i].ap),
                       reads=[e1T, hinTs[i]], writes=[e1T])
                s.emit(v, lambda e, i=i, j=j: e.scalar_tensor_tensor(
                           hinTs[i].ap, e1T.ap, selm[:, j:j + 1], hinTs[i].ap,
                           ALU.mult, ALU.add),
                       reads=[e1T, selm, hinTs[i]], writes=[hinTs[i]])
        for n in range(NSTATE):
            rC = rowC[n % 2]
            s.dma(rC.ap, ccl_h[n:n + 1, :], reads=[ccl_d], writes=[rC])
            bcast_row(rC, pcS, psB)
            for i, dj in enumerate(djs):
                s.emit(a, lambda e, i=i, n=n, dj=dj: e.activation(
                           at.ap, dtres[i].ap, AF.Exp,
                           scale=acol[:, dj * NSTATE + n:dj * NSTATE + n + 1]),
                       reads=[dtres[i], acol], writes=[at])
                s.emit(v, lambda e, i=i, n=n: e.tensor_tensor_scan(
                           ht.ap, at.ap, zerot.ap, hinTs[i][:, n:n + 1],
                           ALU.mult, ALU.add),
                       reads=[at, zerot, hinTs[i]], writes=[ht])
                s.emit(v, lambda e: e.tensor_tensor(qt.ap, ht.ap, pcS.ap, ALU.mult),
                       reads=[ht, pcS], writes=[qt], psum_src=True)
                s.emit(v, lambda e, dj=dj: e.tensor_add(
                           yloc[dj].ap, yloc[dj].ap, qt.ap),
                       reads=[qt, yloc[dj]], writes=[yloc[dj]])
        for i, dj in enumerate(djs):
            zview = Buf(C["w2b"][1][0:P, 0:2048])
            gview = Buf(C["w2b"][1][0:P, 2048:4096])
            s.dma(zview.ap, z_h[dj * P:(dj + 1) * P, :], reads=[z_d], writes=[zview])
            s.emit(v, lambda e, dj=dj, zview=zview, gview=gview: e.tensor_tensor(
                       gview.ap, yloc[dj].ap, zview.ap, ALU.mult),
                   reads=[yloc[dj], zview], writes=[gview])
            s.dma(gl_h[dj * P:(dj + 1) * P, :], gview.ap, reads=[gview], writes=[gl_d])

    emit_w1_load(s, C, w1_h)
    gt = C["h1"][:DBLK]
    xo_d = Buf(None)
    for f in range(NT):
        sl = slice(f * TF, (f + 1) * TF)
        for kb in range(DBLK):
            s.dma(xs[kb].ap, x2_h[kb * P:(kb + 1) * P, sl], reads=[x2_d], writes=[xs[kb]])
        for kb in range(DBLK):
            st = C["wst"][kb % 2]
            s.dma(st[:, 0:D], wo_h[kb * P:(kb + 1) * P, :], writes=[st])
            s.emit(v, lambda e, kb=kb, st=st: e.tensor_copy(
                       C["w2b"][0][:, kb * 512:kb * 512 + 512], st[:, 0:512]),
                   reads=[st], writes=[C["w2b"][0]])
            s.emit(v, lambda e, kb=kb, st=st: e.tensor_copy(
                       C["w2b"][1][:, kb * 512:kb * 512 + 512], st[:, 512:D]),
                   reads=[st], writes=[C["w2b"][1]])
        for kb in range(DBLK):
            s.dma(gt[kb].ap, gl_h[kb * P:(kb + 1) * P, sl], reads=[gl_d], writes=[gt[kb]])
        for mo in range(DBLK):
            po = C["ps_o"][mo % 2]
            half, off = divmod(mo * P, 512)
            wob = C["w2b"][half]
            for kb in range(DBLK):
                s.mm(po, wob, wob[:, kb * 512 + off:kb * 512 + off + P],
                     gt[kb], gt[kb].ap, start=(kb == 0), stop=(kb == DBLK - 1))
            s.emit(v, lambda e, po=po, mo=mo: e.scalar_tensor_tensor(
                       xs[mo].ap, po.ap, boc[:, mo:mo + 1], xs[mo].ap, ALU.add, ALU.add),
                   reads=[po, boc, xs[mo]], writes=[xs[mo]], psum_src=True)
        emit_ffn(s, C, xs, w2_h)
        for kb in range(DBLK):
            s.dma(xo_h[kb * P:(kb + 1) * P, sl], xs[kb].ap, reads=[xs[kb]], writes=[xo_d])

    s.emit("sync", lambda e: e.nop(), deps=[("sync", s.val["sync"])])
    s.finalize()
    return nc


def _col(b):
    n = b.shape[0] // P
    return np.ascontiguousarray(b.reshape(n, P).T)


def kernel(**inputs):
    xf = np.asarray(inputs["L0_attn_x"], np.float32).reshape(B * L, D)
    shared = dict(
        w1=np.ascontiguousarray(np.asarray(inputs["L0_ffn_w1"], np.float32)),
        b1=_col(np.asarray(inputs["L0_ffn_b1"], np.float32)),
        w2=np.ascontiguousarray(np.asarray(inputs["L0_ffn_w2"], np.float32)),
        b2=_col(np.asarray(inputs["L0_ffn_b2"], np.float32)),
        wz=np.ascontiguousarray(np.asarray(inputs["L0_attn_Wz"], np.float32)),
        wx=np.ascontiguousarray(np.asarray(inputs["L0_attn_Wx"], np.float32)),
        wdt=np.ascontiguousarray(np.asarray(inputs["L0_attn_Wdt"], np.float32)),
        bz=_col(np.asarray(inputs["L0_attn_bz"], np.float32)),
        bx=_col(np.asarray(inputs["L0_attn_bx"], np.float32)),
        bdt=_col(np.asarray(inputs["L0_attn_bdt"], np.float32)),
        wb=np.ascontiguousarray(np.asarray(inputs["L0_attn_WB"], np.float32)),
        wc=np.ascontiguousarray(np.asarray(inputs["L0_attn_WC"], np.float32)),
        bb=np.ascontiguousarray(np.asarray(inputs["L0_attn_bB"], np.float32).reshape(NSTATE, 1)),
        bc=np.ascontiguousarray(np.asarray(inputs["L0_attn_bC"], np.float32).reshape(NSTATE, 1)),
        wo=np.ascontiguousarray(np.asarray(inputs["L0_attn_Wout"], np.float32)),
        bo=_col(np.asarray(inputs["L0_attn_bout"], np.float32)),
    )
    alog = np.asarray(inputs["L0_attn_A_log"], np.float32)
    core_ids = list(range(NCORES))
    nc = build_fused(core_ids)
    maps = []
    for i in range(NCORES):
        m = dict(shared)
        m["x"] = np.ascontiguousarray(xf[i * TC:(i + 1) * TC, :].T)
        m["alog"] = np.ascontiguousarray(alog)
        sel = np.zeros((P, NCORES), np.float32)
        for j in range(NCORES):
            if j // 4 == i // 4 and j % 4 < i % 4:
                sel[:, j] = 1.0
        m["selm"] = sel
        maps.append(m)
    res = run_bass_kernel_spmd(nc, maps, core_ids=core_ids)
    out = np.empty((B * L, D), np.float32)
    for i in range(NCORES):
        out[i * TC:(i + 1) * TC, :] = np.asarray(res.results[i]["xo"], np.float32).T
    return out.reshape(B, L, D)


# version 09
# speedup vs baseline: 1.5756x; 1.0487x over previous
import math
from contextlib import ExitStack

import numpy as np
import ml_dtypes

import concourse.bass as bass
import concourse.mybir as mybir
from concourse.bass_utils import run_bass_kernel_spmd

F32 = mybir.dt.float32
I32 = mybir.dt.int32
BF16 = mybir.dt.bfloat16
AF = mybir.ActivationFunctionType
ALU = mybir.AluOpType
AX = mybir.AxisListType

P = 128
D = 1024
FF = 4096
NSTATE = 16
B = 2
L = 8192
NCORES = 8
TC = B * L // NCORES
DBLK = D // P
FBLK = FF // P
TF = 512
NT = TC // TF
TP2 = 2048
EPS = 1e-5

SERIALIZE = True
CHAIN_EXEMPT = set()
ISSUE_CHAIN = True
PE_EXEMPT = False
DEBUG_TAPS = False


def _gelu64(x):
    x = np.asarray(x, np.float64)
    return 0.5 * x * (1.0 + np.vectorize(math.erf)(x / math.sqrt(2.0)))

def _build_pwl():
    X = np.linspace(-4.0, 4.0, 33)
    Y = _gelu64(X)
    j = np.arange(17)
    qv = 256.0 * Y[16 - j]
    g = (qv[1:] - qv[:-1]) / 64.0
    g0 = g[0]
    dj = g[1:] - g[:-1]
    t = np.arange(-40000, 40001, dtype=np.float64)
    uc = np.minimum(np.abs(t), 1024.0)
    q = g0 * uc
    for k in range(1, 16):
        q = q + dj[k - 1] * np.maximum(uc - 64.0 * k, 0.0)
    mine = np.maximum(t, 0.0) + np.round(q)
    xq = t / 256.0
    y = np.interp(xq, X, Y)
    y = np.where(xq > 4.0, xq, np.where(xq < -4.0, 0.0, y))
    ref = np.round(256.0 * y)
    assert np.array_equal(mine, ref), "PWL hinge table does not match reference"
    return float(g0), [float(v) for v in dj]

_PWL_G0, _PWL_DJ = _build_pwl()


class Buf:
    __slots__ = ("t", "last_write", "readers")
    def __init__(self, t):
        self.t = t
        self.last_write = None
        self.readers = []
    def __getitem__(self, sl):
        return self.t[sl]
    @property
    def ap(self):
        return self.t[:]


class Sch:
    ENGS = ("sync", "vector", "scalar", "gpsimd", "tensor")
    SEMS = ENGS + ("syncI", "p0", "p1", "p2", "p3")

    def __init__(self, nc):
        self.nc = nc
        self.ops = {e: [] for e in self.ENGS}
        self.val = {e: 0 for e in self.SEMS}
        self.es = ExitStack()
        self.sem = {e: self.es.enter_context(nc.semaphore(f"s_{e}")) for e in self.SEMS}
        self.prev = None

    def emit(self, eng, fn, reads=(), writes=(), deps=(), dma=False, psum_src=False,
             ccomp=False, sem=None):
        dl = list(deps)
        if SERIALIZE and eng not in CHAIN_EXEMPT \
                and self.prev is not None and self.prev[0] != eng:
            dl.append(self.prev)
        for b in reads:
            if b.last_write is not None:
                dl.append(b.last_write)
        for b in writes:
            if b.last_write is not None:
                dl.append(b.last_write)
            dl.extend(b.readers)
        inc = 16 if dma else 1
        cname = sem or eng
        self.val[cname] += inc
        tok = (cname, self.val[cname])
        self.ops[eng].append((fn, dl, inc, psum_src, sem))
        if dma and ISSUE_CHAIN and not ccomp:
            self.val["syncI"] += 1
            itok = ("syncI", self.val["syncI"])
            self.ops[eng].append(
                (lambda e: e.nop(cycle_cnt=0, nofuse=True), [], 1, False, "syncI"))
            if eng not in CHAIN_EXEMPT:
                self.prev = itok
        elif eng not in CHAIN_EXEMPT:
            self.prev = tok
        for b in reads:
            b.readers.append(tok)
        for b in writes:
            b.last_write = tok
            b.readers = []
        return tok

    def mm(self, ps, lhsT_buf, lhsT_ap, rhs_buf, rhs_ap, start, stop, out_ap=None):
        dl = []
        if SERIALIZE and not PE_EXEMPT \
                and self.prev is not None and self.prev[0] != "tensor":
            dl.append(self.prev)
        if lhsT_buf is not None and lhsT_buf.last_write is not None:
            dl.append(lhsT_buf.last_write)
        if rhs_buf is not None and rhs_buf.last_write is not None:
            dl.append(rhs_buf.last_write)
        if start:
            if ps.last_write is not None:
                dl.append(ps.last_write)
            dl.extend(ps.readers)
        self.val["tensor"] += 1
        tok = ("tensor", self.val["tensor"])
        if not PE_EXEMPT:
            self.prev = tok
        oap = ps.ap if out_ap is None else out_ap
        self.ops["tensor"].append(
            (lambda e, oap=oap, la=lhsT_ap, ra=rhs_ap, st=start, sp=stop:
             e.matmul(oap, la, ra, start=st, stop=sp), dl, 1, False, None))
        if lhsT_buf is not None:
            lhsT_buf.readers.append(tok)
        if rhs_buf is not None:
            rhs_buf.readers.append(tok)
        if stop:
            ps.last_write = tok
            ps.readers = []
        return tok

    def dma(self, dst_ap, src_ap, reads=(), writes=(), deps=(), eng="sync",
            ccomp=False, sem=None):
        import numpy as _np
        assert _np.prod(dst_ap.shape) == _np.prod(src_ap.shape), (
            f"dma size mismatch {dst_ap.shape} <- {src_ap.shape}")
        return self.emit(
            eng, lambda e: e.dma_start(out=dst_ap, in_=src_ap),
            reads=reads, writes=writes, deps=deps, dma=True, ccomp=ccomp, sem=sem)

    def finalize(self):
        nc = self.nc
        with nc.Block() as block:
            for eng in self.ENGS:
                if not self.ops[eng]:
                    continue
                def make(eng):
                    def body(e):
                        water = {s2: 0 for s2 in self.SEMS}
                        for fn, deps, inc, psrc, semov in self.ops[eng]:
                            need = {}
                            for (de, dv) in deps:
                                if dv > water[de]:
                                    need[de] = max(need.get(de, 0), dv)
                            waited_pe = False
                            for de, dv in need.items():
                                e.wait_ge(self.sem[de], dv)
                                water[de] = dv
                                if de == "tensor":
                                    waited_pe = True
                            if psrc and waited_pe:
                                e.nop(cycle_cnt=192, nofuse=True)
                            ins = fn(e)
                            ins.then_inc(self.sem[semov or eng], inc)
                    return body
                getattr(block, eng)(make(eng))
        self.es.close()


def sbuf(nc, name, shape, dtype):
    return Buf(nc.alloc_sbuf_tensor(name, shape, dtype))

def psum(nc, name, shape, dtype=F32):
    return Buf(nc.alloc_psum_tensor(name, shape, dtype))



def emit_ln(s, C, xblk, xn_out):
    v, a = "vector", "scalar"
    ones_col, ones_row = C["ones_col"], C["ones_row"]
    ps_s1, ps_s2, ps_r, ps_m = C["ps_s1"], C["ps_s2"], C["ps_rstd"], C["ps_mmr"]
    st_mean, st_rstd, st_mmr = C["st_mean"], C["st_rstd"], C["st_mmr"]
    tmp_sq, tmp_sq2, tmp32 = C["tmp_sq"], C["tmp_sq2"], C["tmp32"]
    for kb in range(DBLK):
        s.emit(v, lambda e, kb=kb: e.tensor_copy(tmp_sq.ap, xblk[kb].ap),
               reads=[xblk[kb]], writes=[tmp_sq])
        s.mm(ps_s1, ones_col, ones_col.ap, tmp_sq, tmp_sq.ap,
             start=(kb == 0), stop=(kb == DBLK - 1))
        s.emit(a, lambda e, kb=kb: e.activation(tmp_sq2.ap, xblk[kb].ap, AF.Square),
               reads=[xblk[kb]], writes=[tmp_sq2])
        s.mm(ps_s2, ones_col, ones_col.ap, tmp_sq2, tmp_sq2.ap,
             start=(kb == 0), stop=(kb == DBLK - 1))
    s.emit(v, lambda e: e.tensor_scalar(st_mean.ap, ps_s1.ap, 1.0 / D, None, ALU.mult),
           reads=[ps_s1], writes=[st_mean], psum_src=True)
    s.emit(v, lambda e: e.tensor_mul(st_rstd.ap, st_mean.ap, st_mean.ap),
           reads=[st_mean], writes=[st_rstd])
    s.emit(v, lambda e: e.scalar_tensor_tensor(st_rstd.ap, ps_s2.ap, 1.0 / D,
                                               st_rstd.ap, ALU.mult, ALU.subtract),
           reads=[ps_s2, st_rstd], writes=[st_rstd], psum_src=True)
    s.emit(a, lambda e: e.activation(st_rstd.ap, st_rstd.ap, AF.Ln,
                                     bias=C["eps1"][0:1, 0:1]),
           reads=[st_rstd, C["eps1"]], writes=[st_rstd])
    s.emit(a, lambda e: e.activation(st_rstd.ap, st_rstd.ap, AF.Exp, scale=-0.5),
           reads=[st_rstd], writes=[st_rstd])
    s.emit(v, lambda e: e.tensor_mul(st_mmr.ap, st_mean.ap, st_rstd.ap),
           reads=[st_mean, st_rstd], writes=[st_mmr])
    s.mm(ps_r, ones_row, ones_row.ap, st_rstd, st_rstd.ap, start=True, stop=True)
    s.mm(ps_m, ones_row, ones_row.ap, st_mmr, st_mmr.ap, start=True, stop=True)
    for kb in range(DBLK):
        s.emit(v, lambda e, kb=kb: e.tensor_mul(tmp32.ap, xblk[kb].ap, ps_r.ap),
               reads=[xblk[kb], ps_r], writes=[tmp32], psum_src=True)
        s.emit(v, lambda e, kb=kb: e.tensor_sub(xn_out[kb].ap, tmp32.ap, ps_m.ap),
               reads=[tmp32, ps_m], writes=[xn_out[kb]], psum_src=True)


def emit_pwl4(s, C, g0):
    v, a = "vector", "scalar"
    ti4, uc4, ac4, ht4 = C["ti4"], C["uc4"], C["ac4"], C["ht4"]
    for i in range(4):
        m = g0 + i
        ph = C["ps4"][i]
        s.emit(v, lambda e, ph=ph, m=m, i=i: e.tensor_scalar(
                   ti4[:, i * TF:(i + 1) * TF], ph.ap, 256.0, C["b1c"][:, m:m + 1],
                   ALU.mult, ALU.add),
               reads=[ph, C["b1c_buf"]], writes=[ti4], psum_src=True)
    s.emit(v, lambda e: e.scalar_tensor_tensor(uc4.ap, ti4.ap, -1.0, ti4.ap,
                                               ALU.mult, ALU.max),
           reads=[ti4], writes=[uc4])
    s.emit(v, lambda e: e.tensor_scalar(uc4.ap, uc4.ap, 1024.0, None, ALU.min),
           reads=[uc4], writes=[uc4])
    s.emit(v, lambda e: e.tensor_scalar(ac4.ap, uc4.ap, _PWL_G0, None, ALU.mult),
           reads=[uc4], writes=[ac4])
    s.emit(v, lambda e: e.scalar_tensor_tensor(ac4.ap, ti4.ap, 0.0, ac4.ap,
                                               ALU.max, ALU.add),
           reads=[ti4, ac4], writes=[ac4])
    for k0 in range(1, 16, 2):
        ks = [k for k in (k0, k0 + 1) if k <= 15]
        for j, k in enumerate(ks):
            ad = abs(_PWL_DJ[k - 1])
            s.emit(a, lambda e, ad=ad, k=k, j=j: e.activation(
                       ht4[j].ap, uc4.ap, AF.Relu, bias=C["hingeb"][:, k - 1:k],
                       scale=ad),
                   reads=[uc4, C["hingeb"]], writes=[ht4[j]])
        for j, k in enumerate(ks):
            op = ALU.add if _PWL_DJ[k - 1] >= 0 else ALU.subtract
            s.emit(v, lambda e, j=j, op=op: e.tensor_tensor(
                       ac4.ap, ac4.ap, ht4[j].ap, op),
                   reads=[ht4[j], ac4], writes=[ac4])
    s.emit(v, lambda e: e.tensor_scalar(ti4.ap, ac4.ap, 1.0, None, ALU.mult),
           reads=[ac4], writes=[ti4])
    hw = [C["h1"][g0 + i] for i in range(4)]
    s.emit(v, lambda e, g0=g0: e.tensor_copy(
               C["h1big"][:, g0 * TF:(g0 + 4) * TF], ti4.ap),
           reads=[ti4], writes=hw)


def emit_ffn(s, C, xblk, w2_h):
    v = "vector"
    xn = C["xn"]
    emit_ln(s, C, xblk, xn)
    h1 = C["h1"]
    for g0 in range(0, FBLK, 4):
        for i in range(4):
            m = g0 + i
            ph = C["ps4"][i]
            for kb in range(DBLK):
                s.mm(ph, C["w1b"][kb], C["w1b"][kb][:, m * P:(m + 1) * P],
                     xn[kb], xn[kb].ap, start=(kb == 0), stop=(kb == DBLK - 1))
        emit_pwl4(s, C, g0)
    w2r = w2_h.rearrange("(m p) o -> p m o", p=P)
    hm = FBLK // 2

    def w2dma(mo, h):
        s.dma(C["wst"][h].ap, w2r[:, h * hm:(h + 1) * hm, mo * P:(mo + 1) * P],
              writes=[C["wst"][h]], sem=f"p{h}")
    w2dma(0, 0)
    w2dma(0, 1)
    for mo in range(DBLK):
        w2b = C["w2b"][mo % 2]
        for h in range(2):
            w2s = C["wst"][h]
            s.emit(v, lambda e, w2b=w2b, w2s=w2s, h=h: e.tensor_copy(
                       w2b[:, h * hm * P:(h + 1) * hm * P], w2s.ap),
                   reads=[w2s], writes=[w2b])
        if mo + 1 < DBLK:
            w2dma(mo + 1, 0)
            w2dma(mo + 1, 1)
        po = C["ps_o"][mo % 2]
        for m in range(FBLK):
            s.mm(po, w2b, w2b[:, m * P:(m + 1) * P], h1[m], h1[m].ap,
                 start=(m == 0), stop=(m == FBLK - 1))
        t3 = C["t3"][mo % 2]
        s.emit(v, lambda e, t3=t3, po=po, mo=mo: e.tensor_scalar(
                   t3.ap, po.ap, C["b2c"][:, mo:mo + 1], None, ALU.add),
               reads=[po, C["b2c_buf"]], writes=[t3], psum_src=True)
        s.emit(v, lambda e, t3=t3, mo=mo: e.scalar_tensor_tensor(
                   xblk[mo].ap, t3.ap, 1.0 / 512.0, xblk[mo].ap, ALU.mult, ALU.add),
               reads=[t3, xblk[mo]], writes=[xblk[mo]])


def emit_w1_load(s, C, w1_h):
    v = "vector"
    stages = C["wstage"]
    ns = len(stages)
    steps = [(kb, h) for kb in range(DBLK) for h in range(2)]

    def issue(i):
        kb, h = steps[i]
        sl = slice(h * (FF // 2), (h + 1) * (FF // 2))
        st = stages[i % ns]
        s.dma(st.ap, w1_h[kb * P:(kb + 1) * P, sl], writes=[st], sem=f"p{i % ns}")
    for i in range(min(ns, len(steps))):
        issue(i)
    for i, (kb, h) in enumerate(steps):
        st = stages[i % ns]
        sl = slice(h * (FF // 2), (h + 1) * (FF // 2))
        s.emit(v, lambda e, kb=kb, st=st, sl=sl: e.tensor_copy(
                   C["w1b"][kb][:, sl], st.ap),
               reads=[st], writes=[C["w1b"][kb]])
        if i + ns < len(steps):
            issue(i + ns)



def build_fused(core_ids):
    nc = bass.Bass("TRN2", target_bir_lowering=False, debug=False)
    x_h = nc.dram_tensor("x", [D, TC], F32, kind="ExternalInput")
    w1_h = nc.dram_tensor("w1", [D, FF], F32, kind="ExternalInput")
    b1_h = nc.dram_tensor("b1", [P, FBLK], F32, kind="ExternalInput")
    w2_h = nc.dram_tensor("w2", [FF, D], F32, kind="ExternalInput")
    b2_h = nc.dram_tensor("b2", [P, DBLK], F32, kind="ExternalInput")
    wz_h = nc.dram_tensor("wz", [D, D], F32, kind="ExternalInput")
    wx_h = nc.dram_tensor("wx", [D, D], F32, kind="ExternalInput")
    wdt_h = nc.dram_tensor("wdt", [D, D], F32, kind="ExternalInput")
    bz_h = nc.dram_tensor("bz", [P, DBLK], F32, kind="ExternalInput")
    bx_h = nc.dram_tensor("bx", [P, DBLK], F32, kind="ExternalInput")
    bdt_h = nc.dram_tensor("bdt", [P, DBLK], F32, kind="ExternalInput")
    wb_h = nc.dram_tensor("wb", [D, NSTATE], F32, kind="ExternalInput")
    wc_h = nc.dram_tensor("wc", [D, NSTATE], F32, kind="ExternalInput")
    bb_h = nc.dram_tensor("bb", [NSTATE, 1], F32, kind="ExternalInput")
    bc_h = nc.dram_tensor("bc", [NSTATE, 1], F32, kind="ExternalInput")
    al_h = nc.dram_tensor("alog", [D, NSTATE], F32, kind="ExternalInput")
    wo_h = nc.dram_tensor("wo", [D, D], F32, kind="ExternalInput")
    bo_h = nc.dram_tensor("bo", [P, DBLK], F32, kind="ExternalInput")
    xo_h = nc.dram_tensor("xo", [D, TC], F32, kind="ExternalOutput")
    KI = "ExternalOutput" if DEBUG_TAPS else "Internal"
    dtl_h = nc.dram_tensor("dtl", [D, TC], F32, kind=KI)
    vvl_h = nc.dram_tensor("vvl", [D, TC], F32, kind=KI)
    bml_h = nc.dram_tensor("bml", [NSTATE, TC], F32, kind=KI)
    ccl_h = nc.dram_tensor("ccl", [NSTATE, TC], F32, kind=KI)
    gl_h = nc.dram_tensor("gl", [D, TC], BF16, kind=KI)
    agin_h = nc.dram_tensor("agin", [2 * D, NSTATE], F32, kind="Internal")
    agout_h = nc.dram_tensor("agout", [NCORES * 2 * D, NSTATE], F32, kind="Internal")
    x2_h = nc.dram_tensor("x2i", [D, TC], F32, kind=KI)
    z_h = nc.dram_tensor("zi", [D, TC], BF16, kind=KI)
    selm_h = nc.dram_tensor("selm", [P, NCORES], F32, kind="ExternalInput")

    s = Sch(nc)
    v, a, g, t = "vector", "scalar", "gpsimd", "tensor"

    C = {}
    C["ones_col"] = sbuf(nc, "ones_col", [P, 1], BF16)
    C["ones_row"] = sbuf(nc, "ones_row", [1, P], F32)
    s.emit(v, lambda e: e.memset(C["ones_col"].ap, 1.0), writes=[C["ones_col"]])
    s.emit(v, lambda e: e.memset(C["ones_row"].ap, 1.0), writes=[C["ones_row"]])
    C["eps1"] = sbuf(nc, "eps1", [1, 1], F32)
    s.emit(v, lambda e: e.memset(C["eps1"].ap, EPS), writes=[C["eps1"]])
    psA = nc.alloc_psum_tensor("psA", [P, 4 * TF], F32)
    psB = nc.alloc_psum_tensor("psB", [P, 4 * TF], F32)
    C["ps_s1"] = Buf(psA[0:1, 0:TF])
    C["ps_s2"] = Buf(psA[0:1, TF:2 * TF])
    C["ps_rstd"] = Buf(psA[:, 2 * TF:3 * TF])
    C["ps_mmr"] = Buf(psA[:, 3 * TF:4 * TF])
    C["ps_h"] = [Buf(psB[:, 0:TF]), Buf(psB[:, TF:2 * TF])]
    C["ps_o"] = [Buf(psB[:, 2 * TF:3 * TF]), Buf(psB[:, 3 * TF:4 * TF])]
    C["psA"], C["psB"] = psA, psB
    C["st_mean"] = sbuf(nc, "st_mean", [1, TF], F32)
    C["st_rstd"] = sbuf(nc, "st_rstd", [1, TF], F32)
    C["st_mmr"] = sbuf(nc, "st_mmr", [1, TF], F32)
    C["tmp_sq"] = sbuf(nc, "tmp_sq", [P, TF], BF16)
    C["tmp_sq2"] = sbuf(nc, "tmp_sq2", [P, TF], BF16)
    C["tmp32"] = sbuf(nc, "tmp32", [P, TF], F32)
    C["xn"] = [sbuf(nc, f"xn{kb}", [P, TF], BF16) for kb in range(DBLK)]
    w1big = nc.alloc_sbuf_tensor("w1big", [P, DBLK * FF], BF16)
    h1big = nc.alloc_sbuf_tensor("h1big", [P, FBLK * TF], BF16)
    C["w1b"] = [Buf(w1big[:, kb * FF:(kb + 1) * FF]) for kb in range(DBLK)]
    C["h1"] = [Buf(h1big[:, m * TF:(m + 1) * TF]) for m in range(FBLK)]
    C["wst"] = [sbuf(nc, f"wst{i}", [P, FF // 2], F32) for i in range(2)]
    C["w2b"] = [sbuf(nc, f"w2b{i}", [P, FF], BF16) for i in range(2)]
    C["t3"] = [sbuf(nc, f"t3_{i}", [P, TF], I32) for i in range(2)]
    C["ti4"] = sbuf(nc, "pw_ti4", [P, 4 * TF], I32)
    C["uc4"] = sbuf(nc, "pw_uc4", [P, 4 * TF], F32)
    C["ac4"] = sbuf(nc, "pw_ac4", [P, 4 * TF], F32)
    C["ht4"] = [Buf(C["w2b"][i].t[:, 0:FF].bitcast(F32)) for i in range(2)]
    C["wstage"] = [C["wst"][0], C["wst"][1],
                   Buf(C["ti4"].t[:, :].bitcast(F32)), Buf(C["uc4"].t[:, :])]
    C["ps4"] = [C["ps_h"][0], C["ps_h"][1], C["ps_o"][0], C["ps_o"][1]]
    C["h1big"] = h1big
    hb = sbuf(nc, "hingeb", [P, 15], F32)
    for k in range(1, 16):
        ad = abs(_PWL_DJ[k - 1])
        s.emit(v, lambda e, k=k, ad=ad: e.memset(hb[:, k - 1:k], -ad * 64.0 * k),
               writes=[hb])
    C["hingeb"] = hb
    def bias_col(name, h, n, scale=None):
        raw = sbuf(nc, name + "r", [P, n], F32)
        s.dma(raw.ap, h[:], writes=[raw])
        if scale is None:
            return raw
        sc = sbuf(nc, name + "c", [P, n], F32)
        s.emit(v, lambda e: e.tensor_scalar(sc.ap, raw.ap, scale, None, ALU.mult),
               reads=[raw], writes=[sc])
        return sc
    b1c = bias_col("b1", b1_h, FBLK, 256.0)
    b2c = bias_col("b2", b2_h, DBLK, 256.0)
    C["b1c"], C["b1c_buf"] = b1c.t, b1c
    C["b2c"], C["b2c_buf"] = b2c.t, b2c
    bzc = bias_col("bz", bz_h, DBLK)
    bxc = bias_col("bx", bx_h, DBLK)
    bdtc = bias_col("bdt", bdt_h, DBLK)
    boc = bias_col("bo", bo_h, DBLK)
    bb16 = sbuf(nc, "bbc", [NSTATE, 1], F32)
    s.dma(bb16.ap, bb_h[:], writes=[bb16])
    bc16 = sbuf(nc, "bcc", [NSTATE, 1], F32)
    s.dma(bc16.ap, bc_h[:], writes=[bc16])

    xs = [sbuf(nc, f"xs{kb}", [P, TF], F32) for kb in range(DBLK)]
    x2_d, ain_d, aout_d, yin_d, yout_d, z_d = (Buf(None) for _ in range(6))

    emit_w1_load(s, C, w1_h)
    for f in range(NT):
        sl = slice(f * TF, (f + 1) * TF)
        for kb in range(DBLK):
            s.dma(xs[kb].ap, x_h[kb * P:(kb + 1) * P, sl], writes=[xs[kb]])
        emit_ffn(s, C, xs, w2_h)
        for kb in range(DBLK):
            s.dma(x2_h[kb * P:(kb + 1) * P, sl], xs[kb].ap,
                  reads=[xs[kb]], writes=[x2_d])

    pst = [(h, kb, off) for h, off in ((wz_h, 0), (wx_h, D), (wdt_h, 2 * D))
           for kb in range(DBLK)]
    nsp = len(C["wstage"])

    def pissue(i):
        h, kb, off = pst[i]
        st = C["wstage"][i % nsp]
        s.dma(st[:, 0:D], h[kb * P:(kb + 1) * P, :], writes=[st], sem=f"p{i % nsp}")
    for i in range(min(nsp, len(pst))):
        pissue(i)
    for i, (h, kb, off) in enumerate(pst):
        st = C["wstage"][i % nsp]
        s.emit(v, lambda e, kb=kb, st=st, off=off: e.tensor_copy(
                   C["w1b"][kb][:, off:off + D], st[:, 0:D]),
               reads=[st], writes=[C["w1b"][kb]])
        if i + nsp < len(pst):
            pissue(i + nsp)
    wsmall = {}
    for name, h in (("wb", wb_h), ("wc", wc_h)):
        st = C["wst"][0]
        s.dma(st[:, 0:DBLK * NSTATE], h.rearrange("(k p) n -> p k n", p=P), writes=[st])
        wbv = sbuf(nc, f"{name}b", [P, DBLK * NSTATE], BF16)
        s.emit(v, lambda e, wbv=wbv, st=st: e.tensor_copy(wbv.ap, st[:, 0:DBLK * NSTATE]),
               reads=[st], writes=[wbv])
        wsmall[name] = wbv

    xssm = [Buf(h1big[:, i * 1024:(i + 1) * 1024].bitcast(F32)) for i in range(2)]
    dtt = [Buf(h1big[:, (2 + i) * 1024:(3 + i) * 1024].bitcast(F32)) for i in range(2)]
    pbt = [Buf(h1big[:, (4 + i) * 1024:(5 + i) * 1024].bitcast(F32)) for i in range(2)]
    sc1 = [Buf(h1big[:, (6 + i) * 1024:(7 + i) * 1024].bitcast(F32)) for i in range(2)]
    zt = sbuf(nc, "zt", [P, TF], BF16)
    sm16 = sbuf(nc, "sm16", [NSTATE, TF], F32)
    ps_bc = Buf(psA[0:NSTATE, 2 * TF:3 * TF])

    dtl_d, vvl_d, bml_d, ccl_d, gl_d, agin_d, agout_d = (Buf(None) for _ in range(7))
    for f in range(NT):
        sl = slice(f * TF, (f + 1) * TF)
        for kb in range(DBLK):
            s.dma(xs[kb].ap, x2_h[kb * P:(kb + 1) * P, sl], reads=[x2_d], writes=[xs[kb]])
        xn = C["xn"]
        emit_ln(s, C, xs, xn)
        for mo in range(DBLK):
            po = C["ps_o"][mo % 2]
            for kb in range(DBLK):
                s.mm(po, C["w1b"][kb], C["w1b"][kb][:, mo * P:(mo + 1) * P],
                     xn[kb], xn[kb].ap, start=(kb == 0), stop=(kb == DBLK - 1))
            s.emit(a, lambda e, po=po, mo=mo: e.activation(
                       zt.ap, po.ap, AF.Sigmoid, bias=bzc[:, mo:mo + 1]),
                   reads=[po, bzc], writes=[zt], psum_src=True)
            s.dma(z_h[mo * P:(mo + 1) * P, sl], zt.ap, reads=[zt], writes=[z_d])
        for mo in range(DBLK):
            i2 = mo % 2
            po = C["ps_o"][i2]
            for kb in range(DBLK):
                s.mm(po, C["w1b"][kb], C["w1b"][kb][:, 2 * D + mo * P:2 * D + (mo + 1) * P],
                     xn[kb], xn[kb].ap, start=(kb == 0), stop=(kb == DBLK - 1))
            s.emit(a, lambda e, po=po, mo=mo, i2=i2: e.activation(
                       pbt[i2].ap, po.ap, AF.Identity, bias=bdtc[:, mo:mo + 1]),
                   reads=[po, bdtc], writes=[pbt[i2]], psum_src=True)
            s.emit(v, lambda e, i2=i2: e.scalar_tensor_tensor(
                       sc1[i2].ap, pbt[i2].ap, -1.0, pbt[i2].ap, ALU.mult, ALU.max),
                   reads=[pbt[i2]], writes=[sc1[i2]])
            s.emit(a, lambda e, i2=i2: e.activation(sc1[i2].ap, sc1[i2].ap, AF.Exp,
                                                    scale=-1.0),
                   reads=[sc1[i2]], writes=[sc1[i2]])
            s.emit(a, lambda e, i2=i2: e.activation(sc1[i2].ap, sc1[i2].ap, AF.Ln,
                                                    bias=1.0),
                   reads=[sc1[i2]], writes=[sc1[i2]])
            s.emit(v, lambda e, i2=i2: e.scalar_tensor_tensor(
                       dtt[i2].ap, pbt[i2].ap, 0.0, sc1[i2].ap, ALU.max, ALU.add),
                   reads=[pbt[i2], sc1[i2]], writes=[dtt[i2]])
            s.dma(dtl_h[mo * P:(mo + 1) * P, sl], dtt[i2].ap, reads=[dtt[i2]], writes=[dtl_d])
            po2 = C["ps_h"][i2]
            for kb in range(DBLK):
                s.mm(po2, C["w1b"][kb], C["w1b"][kb][:, D + mo * P:D + (mo + 1) * P],
                     xn[kb], xn[kb].ap, start=(kb == 0), stop=(kb == DBLK - 1))
            s.emit(a, lambda e, po2=po2, mo=mo, i2=i2: e.activation(
                       xssm[i2].ap, po2.ap, AF.Identity, bias=bxc[:, mo:mo + 1]),
                   reads=[po2, bxc], writes=[xssm[i2]], psum_src=True)
            s.emit(v, lambda e, i2=i2: e.tensor_mul(xssm[i2].ap, xssm[i2].ap, dtt[i2].ap),
                   reads=[dtt[i2], xssm[i2]], writes=[xssm[i2]])
            s.dma(vvl_h[mo * P:(mo + 1) * P, sl], xssm[i2].ap, reads=[xssm[i2]], writes=[vvl_d])
        for name, out_h2, out_d2, bias in (("wb", bml_h, bml_d, bb16),
                                           ("wc", ccl_h, ccl_d, bc16)):
            wbv = wsmall[name]
            for kb in range(DBLK):
                s.mm(ps_bc, wbv, wbv[:, kb * NSTATE:(kb + 1) * NSTATE],
                     xn[kb], xn[kb].ap, start=(kb == 0), stop=(kb == DBLK - 1),
                     out_ap=ps_bc.ap)
            s.emit(a, lambda e, bias=bias: e.activation(
                       sm16.ap, ps_bc.ap, AF.Identity, bias=bias[:, 0:1]),
                   reads=[ps_bc, bias], writes=[sm16], psum_src=True)
            s.dma(out_h2[:, sl], sm16.ap, reads=[sm16], writes=[out_d2])

    rg = [list(core_ids)]
    acol = sbuf(nc, "acol", [P, DBLK * NSTATE], F32)
    s.dma(acol.ap, al_h.rearrange("(k p) n -> p k n", p=P), writes=[acol])
    s.emit(a, lambda e: e.activation(acol.ap, acol.ap, AF.Exp), reads=[acol], writes=[acol])
    s.emit(v, lambda e: e.tensor_scalar(acol.ap, acol.ap, -1.0, None, ALU.mult),
           reads=[acol], writes=[acol])
    yloc = [Buf(w1big[:, dj * 4096:(dj + 1) * 4096].bitcast(F32)) for dj in range(DBLK)]
    at = Buf(h1big[:, 0:4096].bitcast(F32))
    ut = Buf(h1big[:, 4096:8192].bitcast(F32))
    ht = Buf(h1big[:, 8192:12288].bitcast(F32))
    qt = Buf(h1big[:, 12288:16384].bitcast(F32))
    dtb = C["wst"][0]
    vvb = C["wst"][1]
    zerot = Buf(C["w2b"][0][:, 0:4096].bitcast(F32))
    rowB = [Buf(C["w2b"][0][0:1, 0:4096].bitcast(F32)),
            Buf(C["w2b"][0][32:33, 0:4096].bitcast(F32))]
    rowC = [Buf(C["w2b"][1][0:1, 0:4096].bitcast(F32)),
            Buf(C["w2b"][1][32:33, 0:4096].bitcast(F32))]
    rowbase = {id(rowB[0]): 0, id(rowB[1]): 32, id(rowC[0]): 0, id(rowC[1]): 32}
    ones128 = sbuf(nc, "ones128", [P, P], F32)
    s.emit(v, lambda e: e.memset(ones128.ap, 1.0), writes=[ones128])
    stot = sbuf(nc, "stot", [P, 1], F32)
    hendL = sbuf(nc, "hendL", [P, NSTATE], F32)
    wtotL = sbuf(nc, "wtotL", [P, NSTATE], F32)
    hinT = sbuf(nc, "hinT", [P, NSTATE], F32)
    e1T = sbuf(nc, "e1T", [P, NSTATE], F32)
    wjT = sbuf(nc, "wjT", [P, NSTATE], F32)
    hjT = sbuf(nc, "hjT", [P, NSTATE], F32)
    selm = sbuf(nc, "selmT", [P, NCORES], F32)
    s.dma(selm.ap, selm_h[:], writes=[selm])
    NPB = TC // TF

    pbS = Buf(psA[:, :])
    pcS = Buf(psB[:, :])

    dtres = [Buf(C["ti4"].t[:, :].bitcast(F32)), Buf(C["uc4"].t[:, :])]
    vvres = [Buf(C["ac4"].t[:, :]), C["wst"][0]]
    hendL2 = sbuf(nc, "hendL2", [P, NSTATE], F32)
    hinT2 = sbuf(nc, "hinT2", [P, NSTATE], F32)
    hends = [hendL, hendL2]
    hinTs = [hinT, hinT2]
    CH = 2

    def bcast_row(row, strip, arena):
        rb = rowbase[id(row)]
        for p in range(NPB):
            s.mm(strip, ones128, ones128[rb:rb + 1, :], row,
                 row[0:1, p * TF:(p + 1) * TF], start=True, stop=True,
                 out_ap=arena[:, p * TF:(p + 1) * TF])

    for ch in range(DBLK // CH):
        djs = [ch * CH + i for i in range(CH)]
        for i, dj in enumerate(djs):
            s.dma(dtres[i].ap, dtl_h[dj * P:(dj + 1) * P, :],
                  reads=[dtl_d], writes=[dtres[i]])
            s.dma(vvres[i].ap, vvl_h[dj * P:(dj + 1) * P, :],
                  reads=[vvl_d], writes=[vvres[i]])
        for n in range(NSTATE):
            i2 = n % 2
            rC, rB = rowC[i2], rowB[i2]
            s.dma(rC.ap, ccl_h[n:n + 1, :], reads=[ccl_d], writes=[rC])
            s.dma(rB.ap, bml_h[n:n + 1, :], reads=[bml_d], writes=[rB])
            bcast_row(rB, pbS, psA)
            bcast_row(rC, pcS, psB)
            for i, dj in enumerate(djs):
                s.emit(a, lambda e, i=i, n=n, dj=dj: e.activation(
                           at.ap, dtres[i].ap, AF.Exp,
                           scale=acol[:, dj * NSTATE + n:dj * NSTATE + n + 1]),
                       reads=[dtres[i], acol], writes=[at])
                s.emit(v, lambda e, i=i: e.tensor_tensor(
                           ut.ap, vvres[i].ap, pbS.ap, ALU.mult),
                       reads=[vvres[i], pbS], writes=[ut], psum_src=True)
                s.emit(v, lambda e: e.tensor_tensor_scan(
                           ht.ap, at.ap, ut.ap, 0.0, ALU.mult, ALU.add),
                       reads=[at, ut], writes=[ht])
                s.emit(v, lambda e, i=i, n=n: e.tensor_copy(
                           hends[i][:, n:n + 1], ht[:, TC - 1:TC]),
                       reads=[ht], writes=[hends[i]])
                s.emit(v, lambda e: e.tensor_tensor(qt.ap, ht.ap, pcS.ap, ALU.mult),
                       reads=[ht, pcS], writes=[qt], psum_src=True)
                if n == 0:
                    s.emit(v, lambda e, dj=dj: e.tensor_copy(yloc[dj].ap, qt.ap),
                           reads=[qt], writes=[yloc[dj]])
                else:
                    s.emit(v, lambda e, dj=dj: e.tensor_add(
                               yloc[dj].ap, yloc[dj].ap, qt.ap),
                           reads=[qt, yloc[dj]], writes=[yloc[dj]])
        for i, dj in enumerate(djs):
            s.emit(v, lambda e, i=i: e.tensor_reduce(
                       stot.ap, dtres[i].ap, AX.X, ALU.add),
                   reads=[dtres[i]], writes=[stot])
            for n in range(NSTATE):
                s.emit(a, lambda e, n=n, dj=dj: e.activation(
                           wtotL[:, n:n + 1], stot.ap, AF.Exp,
                           scale=acol[:, dj * NSTATE + n:dj * NSTATE + n + 1]),
                       reads=[stot, acol], writes=[wtotL])
            s.dma(agin_h[dj * P:(dj + 1) * P, :], hends[i].ap,
                  reads=[hends[i]], writes=[agin_d])
            s.dma(agin_h[D + dj * P:D + (dj + 1) * P, :], wtotL.ap,
                  reads=[wtotL], writes=[agin_d])

    s.emit(g, lambda e: e.collective_compute(
               "AllGather", ALU.bypass, replica_groups=rg,
               ins=[agin_h[:]], outs=[agout_h[:]]),
           reads=[agin_d], writes=[agout_d], dma=False)

    s.emit(v, lambda e: e.memset(zerot.ap, 0.0), writes=[zerot])
    for ch in range(DBLK // CH):
        djs = [ch * CH + i for i in range(CH)]
        for i, dj in enumerate(djs):
            s.dma(dtres[i].ap, dtl_h[dj * P:(dj + 1) * P, :],
                  reads=[dtl_d], writes=[dtres[i]])
            s.emit(v, lambda e, i=i: e.memset(hinTs[i].ap, 0.0), writes=[hinTs[i]])
            for j in range(NCORES):
                base = j * 2 * D
                s.dma(hjT.ap, agout_h[base + dj * P:base + (dj + 1) * P, :],
                      reads=[agout_d], writes=[hjT])
                s.dma(wjT.ap, agout_h[base + D + dj * P:base + D + (dj + 1) * P, :],
                      reads=[agout_d], writes=[wjT])
                s.emit(v, lambda e, i=i: e.tensor_mul(e1T.ap, wjT.ap, hinTs[i].ap),
                       reads=[wjT, hinTs[i]], writes=[e1T])
                s.emit(v, lambda e: e.tensor_add(e1T.ap, e1T.ap, hjT.ap),
                       reads=[e1T, hjT], writes=[e1T])
                s.emit(v, lambda e, i=i: e.tensor_sub(e1T.ap, e1T.ap, hinTs[i].ap),
                       reads=[e1T, hinTs[i]], writes=[e1T])
                s.emit(v, lambda e, i=i, j=j: e.scalar_tensor_tensor(
                           hinTs[i].ap, e1T.ap, selm[:, j:j + 1], hinTs[i].ap,
                           ALU.mult, ALU.add),
                       reads=[e1T, selm, hinTs[i]], writes=[hinTs[i]])
        for n in range(NSTATE):
            rC = rowC[n % 2]
            s.dma(rC.ap, ccl_h[n:n + 1, :], reads=[ccl_d], writes=[rC])
            bcast_row(rC, pcS, psB)
            for i, dj in enumerate(djs):
                s.emit(a, lambda e, i=i, n=n, dj=dj: e.activation(
                           at.ap, dtres[i].ap, AF.Exp,
                           scale=acol[:, dj * NSTATE + n:dj * NSTATE + n + 1]),
                       reads=[dtres[i], acol], writes=[at])
                s.emit(v, lambda e, i=i, n=n: e.tensor_tensor_scan(
                           ht.ap, at.ap, zerot.ap, hinTs[i][:, n:n + 1],
                           ALU.mult, ALU.add),
                       reads=[at, zerot, hinTs[i]], writes=[ht])
                s.emit(v, lambda e: e.tensor_tensor(qt.ap, ht.ap, pcS.ap, ALU.mult),
                       reads=[ht, pcS], writes=[qt], psum_src=True)
                s.emit(v, lambda e, dj=dj: e.tensor_add(
                           yloc[dj].ap, yloc[dj].ap, qt.ap),
                       reads=[qt, yloc[dj]], writes=[yloc[dj]])
        for i, dj in enumerate(djs):
            zview = Buf(C["w2b"][1][0:P, 0:2048])
            gview = Buf(C["w2b"][1][0:P, 2048:4096])
            s.dma(zview.ap, z_h[dj * P:(dj + 1) * P, :], reads=[z_d], writes=[zview])
            s.emit(v, lambda e, dj=dj, zview=zview, gview=gview: e.tensor_tensor(
                       gview.ap, yloc[dj].ap, zview.ap, ALU.mult),
                   reads=[yloc[dj], zview], writes=[gview])
            s.dma(gl_h[dj * P:(dj + 1) * P, :], gview.ap, reads=[gview], writes=[gl_d])

    emit_w1_load(s, C, w1_h)
    gt = C["h1"][:DBLK]
    xo_d = Buf(None)
    for f in range(NT):
        sl = slice(f * TF, (f + 1) * TF)
        for kb in range(DBLK):
            s.dma(xs[kb].ap, x2_h[kb * P:(kb + 1) * P, sl], reads=[x2_d], writes=[xs[kb]])
        for kb in range(DBLK):
            st = C["wst"][kb % 2]
            s.dma(st[:, 0:D], wo_h[kb * P:(kb + 1) * P, :], writes=[st])
            s.emit(v, lambda e, kb=kb, st=st: e.tensor_copy(
                       C["w2b"][0][:, kb * 512:kb * 512 + 512], st[:, 0:512]),
                   reads=[st], writes=[C["w2b"][0]])
            s.emit(v, lambda e, kb=kb, st=st: e.tensor_copy(
                       C["w2b"][1][:, kb * 512:kb * 512 + 512], st[:, 512:D]),
                   reads=[st], writes=[C["w2b"][1]])
        for kb in range(DBLK):
            s.dma(gt[kb].ap, gl_h[kb * P:(kb + 1) * P, sl], reads=[gl_d], writes=[gt[kb]])
        for mo in range(DBLK):
            po = C["ps_o"][mo % 2]
            half, off = divmod(mo * P, 512)
            wob = C["w2b"][half]
            for kb in range(DBLK):
                s.mm(po, wob, wob[:, kb * 512 + off:kb * 512 + off + P],
                     gt[kb], gt[kb].ap, start=(kb == 0), stop=(kb == DBLK - 1))
            s.emit(v, lambda e, po=po, mo=mo: e.scalar_tensor_tensor(
                       xs[mo].ap, po.ap, boc[:, mo:mo + 1], xs[mo].ap, ALU.add, ALU.add),
                   reads=[po, boc, xs[mo]], writes=[xs[mo]], psum_src=True)
        emit_ffn(s, C, xs, w2_h)
        for kb in range(DBLK):
            s.dma(xo_h[kb * P:(kb + 1) * P, sl], xs[kb].ap, reads=[xs[kb]], writes=[xo_d])

    s.emit("sync", lambda e: e.nop(), deps=[("sync", s.val["sync"])])
    s.finalize()
    return nc


def _col(b):
    n = b.shape[0] // P
    return np.ascontiguousarray(b.reshape(n, P).T)


def kernel(**inputs):
    xf = np.asarray(inputs["L0_attn_x"], np.float32).reshape(B * L, D)
    shared = dict(
        w1=np.ascontiguousarray(np.asarray(inputs["L0_ffn_w1"], np.float32)),
        b1=_col(np.asarray(inputs["L0_ffn_b1"], np.float32)),
        w2=np.ascontiguousarray(np.asarray(inputs["L0_ffn_w2"], np.float32)),
        b2=_col(np.asarray(inputs["L0_ffn_b2"], np.float32)),
        wz=np.ascontiguousarray(np.asarray(inputs["L0_attn_Wz"], np.float32)),
        wx=np.ascontiguousarray(np.asarray(inputs["L0_attn_Wx"], np.float32)),
        wdt=np.ascontiguousarray(np.asarray(inputs["L0_attn_Wdt"], np.float32)),
        bz=_col(np.asarray(inputs["L0_attn_bz"], np.float32)),
        bx=_col(np.asarray(inputs["L0_attn_bx"], np.float32)),
        bdt=_col(np.asarray(inputs["L0_attn_bdt"], np.float32)),
        wb=np.ascontiguousarray(np.asarray(inputs["L0_attn_WB"], np.float32)),
        wc=np.ascontiguousarray(np.asarray(inputs["L0_attn_WC"], np.float32)),
        bb=np.ascontiguousarray(np.asarray(inputs["L0_attn_bB"], np.float32).reshape(NSTATE, 1)),
        bc=np.ascontiguousarray(np.asarray(inputs["L0_attn_bC"], np.float32).reshape(NSTATE, 1)),
        wo=np.ascontiguousarray(np.asarray(inputs["L0_attn_Wout"], np.float32)),
        bo=_col(np.asarray(inputs["L0_attn_bout"], np.float32)),
    )
    alog = np.asarray(inputs["L0_attn_A_log"], np.float32)
    core_ids = list(range(NCORES))
    nc = build_fused(core_ids)
    maps = []
    for i in range(NCORES):
        m = dict(shared)
        m["x"] = np.ascontiguousarray(xf[i * TC:(i + 1) * TC, :].T)
        m["alog"] = np.ascontiguousarray(alog)
        sel = np.zeros((P, NCORES), np.float32)
        for j in range(NCORES):
            if j // 4 == i // 4 and j % 4 < i % 4:
                sel[:, j] = 1.0
        m["selm"] = sel
        maps.append(m)
    res = run_bass_kernel_spmd(nc, maps, core_ids=core_ids)
    out = np.empty((B * L, D), np.float32)
    for i in range(NCORES):
        out[i * TC:(i + 1) * TC, :] = np.asarray(res.results[i]["xo"], np.float32).T
    return out.reshape(B, L, D)


# version 10
# speedup vs baseline: 1.6633x; 1.0557x over previous
import math
from contextlib import ExitStack

import numpy as np
import ml_dtypes

import concourse.bass as bass
import concourse.mybir as mybir
from concourse.bass_utils import run_bass_kernel_spmd

F32 = mybir.dt.float32
I32 = mybir.dt.int32
BF16 = mybir.dt.bfloat16
AF = mybir.ActivationFunctionType
ALU = mybir.AluOpType
AX = mybir.AxisListType

P = 128
D = 1024
FF = 4096
NSTATE = 16
B = 2
L = 8192
NCORES = 8
TC = B * L // NCORES
DBLK = D // P
FBLK = FF // P
TF = 512
NT = TC // TF
TP2 = 2048
EPS = 1e-5

SERIALIZE = True
CHAIN_EXEMPT = set()
ISSUE_CHAIN = True
PE_EXEMPT = False
DEBUG_TAPS = False


def _gelu64(x):
    x = np.asarray(x, np.float64)
    return 0.5 * x * (1.0 + np.vectorize(math.erf)(x / math.sqrt(2.0)))

def _build_pwl():
    X = np.linspace(-4.0, 4.0, 33)
    Y = _gelu64(X)
    j = np.arange(17)
    qv = 256.0 * Y[16 - j]
    g = (qv[1:] - qv[:-1]) / 64.0
    g0 = g[0]
    dj = g[1:] - g[:-1]
    t = np.arange(-40000, 40001, dtype=np.float64)
    uc = np.minimum(np.abs(t), 1024.0)
    q = g0 * uc
    for k in range(1, 16):
        q = q + dj[k - 1] * np.maximum(uc - 64.0 * k, 0.0)
    mine = np.maximum(t, 0.0) + np.round(q)
    xq = t / 256.0
    y = np.interp(xq, X, Y)
    y = np.where(xq > 4.0, xq, np.where(xq < -4.0, 0.0, y))
    ref = np.round(256.0 * y)
    assert np.array_equal(mine, ref), "PWL hinge table does not match reference"
    return float(g0), [float(v) for v in dj]

_PWL_G0, _PWL_DJ = _build_pwl()


class Buf:
    __slots__ = ("t", "last_write", "readers")
    def __init__(self, t):
        self.t = t
        self.last_write = None
        self.readers = []
    def __getitem__(self, sl):
        return self.t[sl]
    @property
    def ap(self):
        return self.t[:]


class Sch:
    ENGS = ("sync", "vector", "scalar", "gpsimd", "tensor")
    SEMS = ENGS + ("syncI", "p0", "p1", "p2", "p3")

    def __init__(self, nc):
        self.nc = nc
        self.ops = {e: [] for e in self.ENGS}
        self.val = {e: 0 for e in self.SEMS}
        self.es = ExitStack()
        self.sem = {e: self.es.enter_context(nc.semaphore(f"s_{e}")) for e in self.SEMS}
        self.prev = None

    def emit(self, eng, fn, reads=(), writes=(), deps=(), dma=False, psum_src=False,
             ccomp=False, sem=None):
        dl = list(deps)
        if SERIALIZE and eng not in CHAIN_EXEMPT \
                and self.prev is not None and self.prev[0] != eng:
            dl.append(self.prev)
        for b in reads:
            if b.last_write is not None:
                dl.append(b.last_write)
        for b in writes:
            if b.last_write is not None:
                dl.append(b.last_write)
            dl.extend(b.readers)
        inc = 16 if dma else 1
        cname = sem or eng
        self.val[cname] += inc
        tok = (cname, self.val[cname])
        self.ops[eng].append((fn, dl, inc, psum_src, sem))
        if dma and ISSUE_CHAIN and not ccomp:
            self.val["syncI"] += 1
            itok = ("syncI", self.val["syncI"])
            self.ops[eng].append(
                (lambda e: e.nop(cycle_cnt=0, nofuse=True), [], 1, False, "syncI"))
            if eng not in CHAIN_EXEMPT:
                self.prev = itok
        elif eng not in CHAIN_EXEMPT:
            self.prev = tok
        for b in reads:
            b.readers.append(tok)
        for b in writes:
            b.last_write = tok
            b.readers = []
        return tok

    def mm(self, ps, lhsT_buf, lhsT_ap, rhs_buf, rhs_ap, start, stop, out_ap=None):
        dl = []
        if SERIALIZE and not PE_EXEMPT \
                and self.prev is not None and self.prev[0] != "tensor":
            dl.append(self.prev)
        if lhsT_buf is not None and lhsT_buf.last_write is not None:
            dl.append(lhsT_buf.last_write)
        if rhs_buf is not None and rhs_buf.last_write is not None:
            dl.append(rhs_buf.last_write)
        if start:
            if ps.last_write is not None:
                dl.append(ps.last_write)
            dl.extend(ps.readers)
        self.val["tensor"] += 1
        tok = ("tensor", self.val["tensor"])
        if not PE_EXEMPT:
            self.prev = tok
        oap = ps.ap if out_ap is None else out_ap
        self.ops["tensor"].append(
            (lambda e, oap=oap, la=lhsT_ap, ra=rhs_ap, st=start, sp=stop:
             e.matmul(oap, la, ra, start=st, stop=sp), dl, 1, False, None))
        if lhsT_buf is not None:
            lhsT_buf.readers.append(tok)
        if rhs_buf is not None:
            rhs_buf.readers.append(tok)
        if stop:
            ps.last_write = tok
            ps.readers = []
        return tok

    def dma(self, dst_ap, src_ap, reads=(), writes=(), deps=(), eng="sync",
            ccomp=False, sem=None):
        import numpy as _np
        assert _np.prod(dst_ap.shape) == _np.prod(src_ap.shape), (
            f"dma size mismatch {dst_ap.shape} <- {src_ap.shape}")
        return self.emit(
            eng, lambda e: e.dma_start(out=dst_ap, in_=src_ap),
            reads=reads, writes=writes, deps=deps, dma=True, ccomp=ccomp, sem=sem)

    def finalize(self):
        nc = self.nc
        with nc.Block() as block:
            for eng in self.ENGS:
                if not self.ops[eng]:
                    continue
                def make(eng):
                    def body(e):
                        water = {s2: 0 for s2 in self.SEMS}
                        for fn, deps, inc, psrc, semov in self.ops[eng]:
                            need = {}
                            for (de, dv) in deps:
                                if dv > water[de]:
                                    need[de] = max(need.get(de, 0), dv)
                            waited_pe = False
                            for de, dv in need.items():
                                e.wait_ge(self.sem[de], dv)
                                water[de] = dv
                                if de == "tensor":
                                    waited_pe = True
                            if psrc and waited_pe:
                                e.nop(cycle_cnt=192, nofuse=True)
                            ins = fn(e)
                            ins.then_inc(self.sem[semov or eng], inc)
                    return body
                getattr(block, eng)(make(eng))
        self.es.close()


def sbuf(nc, name, shape, dtype):
    return Buf(nc.alloc_sbuf_tensor(name, shape, dtype))

def psum(nc, name, shape, dtype=F32):
    return Buf(nc.alloc_psum_tensor(name, shape, dtype))



def emit_ln(s, C, xblk, xn_out):
    v, a = "vector", "scalar"
    ones_col, ones_row = C["ones_col"], C["ones_row"]
    ps_s1, ps_s2, ps_r, ps_m = C["ps_s1"], C["ps_s2"], C["ps_rstd"], C["ps_mmr"]
    st_mean, st_rstd, st_mmr = C["st_mean"], C["st_rstd"], C["st_mmr"]
    tmp_sq, tmp_sq2, tmp32 = C["tmp_sq"], C["tmp_sq2"], C["tmp32"]
    for kb in range(DBLK):
        s.emit(v, lambda e, kb=kb: e.tensor_copy(tmp_sq.ap, xblk[kb].ap),
               reads=[xblk[kb]], writes=[tmp_sq])
        s.mm(ps_s1, ones_col, ones_col.ap, tmp_sq, tmp_sq.ap,
             start=(kb == 0), stop=(kb == DBLK - 1))
        s.emit(a, lambda e, kb=kb: e.activation(tmp_sq2.ap, xblk[kb].ap, AF.Square),
               reads=[xblk[kb]], writes=[tmp_sq2])
        s.mm(ps_s2, ones_col, ones_col.ap, tmp_sq2, tmp_sq2.ap,
             start=(kb == 0), stop=(kb == DBLK - 1))
    s.emit(v, lambda e: e.tensor_scalar(st_mean.ap, ps_s1.ap, 1.0 / D, None, ALU.mult),
           reads=[ps_s1], writes=[st_mean], psum_src=True)
    s.emit(v, lambda e: e.tensor_mul(st_rstd.ap, st_mean.ap, st_mean.ap),
           reads=[st_mean], writes=[st_rstd])
    s.emit(v, lambda e: e.scalar_tensor_tensor(st_rstd.ap, ps_s2.ap, 1.0 / D,
                                               st_rstd.ap, ALU.mult, ALU.subtract),
           reads=[ps_s2, st_rstd], writes=[st_rstd], psum_src=True)
    s.emit(a, lambda e: e.activation(st_rstd.ap, st_rstd.ap, AF.Ln,
                                     bias=C["eps1"][0:1, 0:1]),
           reads=[st_rstd, C["eps1"]], writes=[st_rstd])
    s.emit(a, lambda e: e.activation(st_rstd.ap, st_rstd.ap, AF.Exp, scale=-0.5),
           reads=[st_rstd], writes=[st_rstd])
    s.emit(v, lambda e: e.tensor_mul(st_mmr.ap, st_mean.ap, st_rstd.ap),
           reads=[st_mean, st_rstd], writes=[st_mmr])
    s.mm(ps_r, ones_row, ones_row.ap, st_rstd, st_rstd.ap, start=True, stop=True)
    s.mm(ps_m, ones_row, ones_row.ap, st_mmr, st_mmr.ap, start=True, stop=True)
    for kb in range(DBLK):
        s.emit(v, lambda e, kb=kb: e.tensor_mul(tmp32.ap, xblk[kb].ap, ps_r.ap),
               reads=[xblk[kb], ps_r], writes=[tmp32], psum_src=True)
        s.emit(v, lambda e, kb=kb: e.tensor_sub(xn_out[kb].ap, tmp32.ap, ps_m.ap),
               reads=[tmp32, ps_m], writes=[xn_out[kb]], psum_src=True)


def emit_pwl4(s, C, g0):
    v, a = "vector", "scalar"
    ti4, uc4, ac4, ht4 = C["ti4"], C["uc4"], C["ac4"], C["ht4"]
    for i in range(4):
        m = g0 + i
        ph = C["ps4"][i]
        s.emit(v, lambda e, ph=ph, m=m, i=i: e.tensor_scalar(
                   ti4[:, i * TF:(i + 1) * TF], ph.ap, 256.0, C["b1c"][:, m:m + 1],
                   ALU.mult, ALU.add),
               reads=[ph, C["b1c_buf"]], writes=[ti4], psum_src=True)
    s.emit(v, lambda e: e.scalar_tensor_tensor(uc4.ap, ti4.ap, -1.0, ti4.ap,
                                               ALU.mult, ALU.max),
           reads=[ti4], writes=[uc4])
    s.emit(v, lambda e: e.tensor_scalar(uc4.ap, uc4.ap, 1024.0, None, ALU.min),
           reads=[uc4], writes=[uc4])
    s.emit(v, lambda e: e.tensor_scalar(ac4.ap, uc4.ap, _PWL_G0, None, ALU.mult),
           reads=[uc4], writes=[ac4])
    s.emit(v, lambda e: e.scalar_tensor_tensor(ac4.ap, ti4.ap, 0.0, ac4.ap,
                                               ALU.max, ALU.add),
           reads=[ti4, ac4], writes=[ac4])
    hb_ = ht4[0]
    for k in range(1, 16):
        dk = _PWL_DJ[k - 1]
        s.emit(v, lambda e, k=k: e.tensor_scalar(
                   hb_.ap, uc4.ap, -64.0 * k, 0.0, ALU.add, ALU.max),
               reads=[uc4], writes=[hb_])
        s.emit(v, lambda e, dk=dk: e.scalar_tensor_tensor(
                   ac4.ap, hb_.ap, dk, ac4.ap, ALU.mult, ALU.add),
               reads=[hb_, ac4], writes=[ac4])
    s.emit(v, lambda e: e.tensor_scalar(ti4.ap, ac4.ap, 1.0, None, ALU.mult),
           reads=[ac4], writes=[ti4])
    hw = [C["h1"][g0 + i] for i in range(4)]
    s.emit(v, lambda e, g0=g0: e.tensor_copy(
               C["h1big"][:, g0 * TF:(g0 + 4) * TF], ti4.ap),
           reads=[ti4], writes=hw)


def emit_ffn(s, C, xblk, w2_h):
    v = "vector"
    xn = C["xn"]
    emit_ln(s, C, xblk, xn)
    h1 = C["h1"]
    for g0 in range(0, FBLK, 4):
        for i in range(4):
            m = g0 + i
            ph = C["ps4"][i]
            for kb in range(DBLK):
                s.mm(ph, C["w1b"][kb], C["w1b"][kb][:, m * P:(m + 1) * P],
                     xn[kb], xn[kb].ap, start=(kb == 0), stop=(kb == DBLK - 1))
        emit_pwl4(s, C, g0)
    w2r = w2_h.rearrange("(m p) o -> p m o", p=P)
    hm = FBLK // 2

    def w2dma(mo, h):
        s.dma(C["wst"][h].ap, w2r[:, h * hm:(h + 1) * hm, mo * P:(mo + 1) * P],
              writes=[C["wst"][h]], sem=f"p{h}")
    w2dma(0, 0)
    w2dma(0, 1)
    for mo in range(DBLK):
        w2b = C["w2b"][mo % 2]
        for h in range(2):
            w2s = C["wst"][h]
            s.emit(v, lambda e, w2b=w2b, w2s=w2s, h=h: e.tensor_copy(
                       w2b[:, h * hm * P:(h + 1) * hm * P], w2s.ap),
                   reads=[w2s], writes=[w2b])
        if mo + 1 < DBLK:
            w2dma(mo + 1, 0)
            w2dma(mo + 1, 1)
        po = C["ps_o"][mo % 2]
        for m in range(FBLK):
            s.mm(po, w2b, w2b[:, m * P:(m + 1) * P], h1[m], h1[m].ap,
                 start=(m == 0), stop=(m == FBLK - 1))
        t3 = C["t3"][mo % 2]
        s.emit(v, lambda e, t3=t3, po=po, mo=mo: e.tensor_scalar(
                   t3.ap, po.ap, C["b2c"][:, mo:mo + 1], None, ALU.add),
               reads=[po, C["b2c_buf"]], writes=[t3], psum_src=True)
        s.emit(v, lambda e, t3=t3, mo=mo: e.scalar_tensor_tensor(
                   xblk[mo].ap, t3.ap, 1.0 / 512.0, xblk[mo].ap, ALU.mult, ALU.add),
               reads=[t3, xblk[mo]], writes=[xblk[mo]])


def emit_w1_load(s, C, w1_h):
    v = "vector"
    stages = C["wstage"]
    ns = len(stages)
    steps = [(kb, h) for kb in range(DBLK) for h in range(2)]

    def issue(i):
        kb, h = steps[i]
        sl = slice(h * (FF // 2), (h + 1) * (FF // 2))
        st = stages[i % ns]
        s.dma(st.ap, w1_h[kb * P:(kb + 1) * P, sl], writes=[st], sem=f"p{i % ns}")
    for i in range(min(ns, len(steps))):
        issue(i)
    for i, (kb, h) in enumerate(steps):
        st = stages[i % ns]
        sl = slice(h * (FF // 2), (h + 1) * (FF // 2))
        s.emit(v, lambda e, kb=kb, st=st, sl=sl: e.tensor_copy(
                   C["w1b"][kb][:, sl], st.ap),
               reads=[st], writes=[C["w1b"][kb]])
        if i + ns < len(steps):
            issue(i + ns)



def build_fused(core_ids):
    nc = bass.Bass("TRN2", target_bir_lowering=False, debug=False)
    x_h = nc.dram_tensor("x", [D, TC], F32, kind="ExternalInput")
    w1_h = nc.dram_tensor("w1", [D, FF], F32, kind="ExternalInput")
    b1_h = nc.dram_tensor("b1", [P, FBLK], F32, kind="ExternalInput")
    w2_h = nc.dram_tensor("w2", [FF, D], F32, kind="ExternalInput")
    b2_h = nc.dram_tensor("b2", [P, DBLK], F32, kind="ExternalInput")
    wz_h = nc.dram_tensor("wz", [D, D], F32, kind="ExternalInput")
    wx_h = nc.dram_tensor("wx", [D, D], F32, kind="ExternalInput")
    wdt_h = nc.dram_tensor("wdt", [D, D], F32, kind="ExternalInput")
    bz_h = nc.dram_tensor("bz", [P, DBLK], F32, kind="ExternalInput")
    bx_h = nc.dram_tensor("bx", [P, DBLK], F32, kind="ExternalInput")
    bdt_h = nc.dram_tensor("bdt", [P, DBLK], F32, kind="ExternalInput")
    wb_h = nc.dram_tensor("wb", [D, NSTATE], F32, kind="ExternalInput")
    wc_h = nc.dram_tensor("wc", [D, NSTATE], F32, kind="ExternalInput")
    bb_h = nc.dram_tensor("bb", [NSTATE, 1], F32, kind="ExternalInput")
    bc_h = nc.dram_tensor("bc", [NSTATE, 1], F32, kind="ExternalInput")
    al_h = nc.dram_tensor("alog", [D, NSTATE], F32, kind="ExternalInput")
    wo_h = nc.dram_tensor("wo", [D, D], F32, kind="ExternalInput")
    bo_h = nc.dram_tensor("bo", [P, DBLK], F32, kind="ExternalInput")
    xo_h = nc.dram_tensor("xo", [D, TC], F32, kind="ExternalOutput")
    KI = "ExternalOutput" if DEBUG_TAPS else "Internal"
    dtl_h = nc.dram_tensor("dtl", [D, TC], F32, kind=KI)
    vvl_h = nc.dram_tensor("vvl", [D, TC], F32, kind=KI)
    bml_h = nc.dram_tensor("bml", [NSTATE, TC], F32, kind=KI)
    ccl_h = nc.dram_tensor("ccl", [NSTATE, TC], F32, kind=KI)
    gl_h = nc.dram_tensor("gl", [D, TC], BF16, kind=KI)
    agin_h = nc.dram_tensor("agin", [2 * D, NSTATE], F32, kind="Internal")
    agout_h = nc.dram_tensor("agout", [NCORES * 2 * D, NSTATE], F32, kind="Internal")
    x2_h = nc.dram_tensor("x2i", [D, TC], F32, kind=KI)
    z_h = nc.dram_tensor("zi", [D, TC], BF16, kind=KI)
    selm_h = nc.dram_tensor("selm", [P, NCORES], F32, kind="ExternalInput")

    s = Sch(nc)
    v, a, g, t = "vector", "scalar", "gpsimd", "tensor"

    C = {}
    C["ones_col"] = sbuf(nc, "ones_col", [P, 1], BF16)
    C["ones_row"] = sbuf(nc, "ones_row", [1, P], F32)
    s.emit(v, lambda e: e.memset(C["ones_col"].ap, 1.0), writes=[C["ones_col"]])
    s.emit(v, lambda e: e.memset(C["ones_row"].ap, 1.0), writes=[C["ones_row"]])
    C["eps1"] = sbuf(nc, "eps1", [1, 1], F32)
    s.emit(v, lambda e: e.memset(C["eps1"].ap, EPS), writes=[C["eps1"]])
    psA = nc.alloc_psum_tensor("psA", [P, 4 * TF], F32)
    psB = nc.alloc_psum_tensor("psB", [P, 4 * TF], F32)
    C["ps_s1"] = Buf(psA[0:1, 0:TF])
    C["ps_s2"] = Buf(psA[0:1, TF:2 * TF])
    C["ps_rstd"] = Buf(psA[:, 2 * TF:3 * TF])
    C["ps_mmr"] = Buf(psA[:, 3 * TF:4 * TF])
    C["ps_h"] = [Buf(psB[:, 0:TF]), Buf(psB[:, TF:2 * TF])]
    C["ps_o"] = [Buf(psB[:, 2 * TF:3 * TF]), Buf(psB[:, 3 * TF:4 * TF])]
    C["psA"], C["psB"] = psA, psB
    C["st_mean"] = sbuf(nc, "st_mean", [1, TF], F32)
    C["st_rstd"] = sbuf(nc, "st_rstd", [1, TF], F32)
    C["st_mmr"] = sbuf(nc, "st_mmr", [1, TF], F32)
    C["tmp_sq"] = sbuf(nc, "tmp_sq", [P, TF], BF16)
    C["tmp_sq2"] = sbuf(nc, "tmp_sq2", [P, TF], BF16)
    C["tmp32"] = sbuf(nc, "tmp32", [P, TF], F32)
    C["xn"] = [sbuf(nc, f"xn{kb}", [P, TF], BF16) for kb in range(DBLK)]
    w1big = nc.alloc_sbuf_tensor("w1big", [P, DBLK * FF], BF16)
    h1big = nc.alloc_sbuf_tensor("h1big", [P, FBLK * TF], BF16)
    C["w1b"] = [Buf(w1big[:, kb * FF:(kb + 1) * FF]) for kb in range(DBLK)]
    C["h1"] = [Buf(h1big[:, m * TF:(m + 1) * TF]) for m in range(FBLK)]
    C["wst"] = [sbuf(nc, f"wst{i}", [P, FF // 2], F32) for i in range(2)]
    C["w2b"] = [sbuf(nc, f"w2b{i}", [P, FF], BF16) for i in range(2)]
    C["t3"] = [sbuf(nc, f"t3_{i}", [P, TF], I32) for i in range(2)]
    C["ti4"] = sbuf(nc, "pw_ti4", [P, 4 * TF], I32)
    C["uc4"] = sbuf(nc, "pw_uc4", [P, 4 * TF], F32)
    C["ac4"] = sbuf(nc, "pw_ac4", [P, 4 * TF], F32)
    C["ht4"] = [Buf(C["w2b"][i].t[:, 0:FF].bitcast(F32)) for i in range(2)]
    C["wstage"] = [C["wst"][0], C["wst"][1],
                   Buf(C["ti4"].t[:, :].bitcast(F32)), Buf(C["uc4"].t[:, :])]
    C["ps4"] = [C["ps_h"][0], C["ps_h"][1], C["ps_o"][0], C["ps_o"][1]]
    C["h1big"] = h1big
    def bias_col(name, h, n, scale=None):
        raw = sbuf(nc, name + "r", [P, n], F32)
        s.dma(raw.ap, h[:], writes=[raw])
        if scale is None:
            return raw
        sc = sbuf(nc, name + "c", [P, n], F32)
        s.emit(v, lambda e: e.tensor_scalar(sc.ap, raw.ap, scale, None, ALU.mult),
               reads=[raw], writes=[sc])
        return sc
    b1c = bias_col("b1", b1_h, FBLK, 256.0)
    b2c = bias_col("b2", b2_h, DBLK, 256.0)
    C["b1c"], C["b1c_buf"] = b1c.t, b1c
    C["b2c"], C["b2c_buf"] = b2c.t, b2c
    bzc = bias_col("bz", bz_h, DBLK)
    bxc = bias_col("bx", bx_h, DBLK)
    bdtc = bias_col("bdt", bdt_h, DBLK)
    boc = bias_col("bo", bo_h, DBLK)
    bb16 = sbuf(nc, "bbc", [NSTATE, 1], F32)
    s.dma(bb16.ap, bb_h[:], writes=[bb16])
    bc16 = sbuf(nc, "bcc", [NSTATE, 1], F32)
    s.dma(bc16.ap, bc_h[:], writes=[bc16])

    xs = [sbuf(nc, f"xs{kb}", [P, TF], F32) for kb in range(DBLK)]
    x2_d, ain_d, aout_d, yin_d, yout_d, z_d = (Buf(None) for _ in range(6))

    emit_w1_load(s, C, w1_h)
    for f in range(NT):
        sl = slice(f * TF, (f + 1) * TF)
        for kb in range(DBLK):
            s.dma(xs[kb].ap, x_h[kb * P:(kb + 1) * P, sl], writes=[xs[kb]])
        emit_ffn(s, C, xs, w2_h)
        for kb in range(DBLK):
            s.dma(x2_h[kb * P:(kb + 1) * P, sl], xs[kb].ap,
                  reads=[xs[kb]], writes=[x2_d])

    pst = [(h, kb, off) for h, off in ((wz_h, 0), (wx_h, D), (wdt_h, 2 * D))
           for kb in range(DBLK)]
    nsp = len(C["wstage"])

    def pissue(i):
        h, kb, off = pst[i]
        st = C["wstage"][i % nsp]
        s.dma(st[:, 0:D], h[kb * P:(kb + 1) * P, :], writes=[st], sem=f"p{i % nsp}")
    for i in range(min(nsp, len(pst))):
        pissue(i)
    for i, (h, kb, off) in enumerate(pst):
        st = C["wstage"][i % nsp]
        s.emit(v, lambda e, kb=kb, st=st, off=off: e.tensor_copy(
                   C["w1b"][kb][:, off:off + D], st[:, 0:D]),
               reads=[st], writes=[C["w1b"][kb]])
        if i + nsp < len(pst):
            pissue(i + nsp)
    wsmall = {}
    for name, h in (("wb", wb_h), ("wc", wc_h)):
        st = C["wst"][0]
        s.dma(st[:, 0:DBLK * NSTATE], h.rearrange("(k p) n -> p k n", p=P), writes=[st])
        wbv = sbuf(nc, f"{name}b", [P, DBLK * NSTATE], BF16)
        s.emit(v, lambda e, wbv=wbv, st=st: e.tensor_copy(wbv.ap, st[:, 0:DBLK * NSTATE]),
               reads=[st], writes=[wbv])
        wsmall[name] = wbv

    xssm = [Buf(h1big[:, i * 1024:(i + 1) * 1024].bitcast(F32)) for i in range(2)]
    dtt = [Buf(h1big[:, (2 + i) * 1024:(3 + i) * 1024].bitcast(F32)) for i in range(2)]
    pbt = [Buf(h1big[:, (4 + i) * 1024:(5 + i) * 1024].bitcast(F32)) for i in range(2)]
    sc1 = [Buf(h1big[:, (6 + i) * 1024:(7 + i) * 1024].bitcast(F32)) for i in range(2)]
    zt = sbuf(nc, "zt", [P, TF], BF16)
    sm16 = sbuf(nc, "sm16", [NSTATE, TF], F32)
    ps_bc = Buf(psA[0:NSTATE, 2 * TF:3 * TF])

    dtl_d, vvl_d, bml_d, ccl_d, gl_d, agin_d, agout_d = (Buf(None) for _ in range(7))
    for f in range(NT):
        sl = slice(f * TF, (f + 1) * TF)
        for kb in range(DBLK):
            s.dma(xs[kb].ap, x2_h[kb * P:(kb + 1) * P, sl], reads=[x2_d], writes=[xs[kb]])
        xn = C["xn"]
        emit_ln(s, C, xs, xn)
        for mo in range(DBLK):
            po = C["ps_o"][mo % 2]
            for kb in range(DBLK):
                s.mm(po, C["w1b"][kb], C["w1b"][kb][:, mo * P:(mo + 1) * P],
                     xn[kb], xn[kb].ap, start=(kb == 0), stop=(kb == DBLK - 1))
            s.emit(a, lambda e, po=po, mo=mo: e.activation(
                       zt.ap, po.ap, AF.Sigmoid, bias=bzc[:, mo:mo + 1]),
                   reads=[po, bzc], writes=[zt], psum_src=True)
            s.dma(z_h[mo * P:(mo + 1) * P, sl], zt.ap, reads=[zt], writes=[z_d])
        for mo in range(DBLK):
            i2 = mo % 2
            po = C["ps_o"][i2]
            for kb in range(DBLK):
                s.mm(po, C["w1b"][kb], C["w1b"][kb][:, 2 * D + mo * P:2 * D + (mo + 1) * P],
                     xn[kb], xn[kb].ap, start=(kb == 0), stop=(kb == DBLK - 1))
            s.emit(a, lambda e, po=po, mo=mo, i2=i2: e.activation(
                       pbt[i2].ap, po.ap, AF.Identity, bias=bdtc[:, mo:mo + 1]),
                   reads=[po, bdtc], writes=[pbt[i2]], psum_src=True)
            s.emit(v, lambda e, i2=i2: e.scalar_tensor_tensor(
                       sc1[i2].ap, pbt[i2].ap, -1.0, pbt[i2].ap, ALU.mult, ALU.max),
                   reads=[pbt[i2]], writes=[sc1[i2]])
            s.emit(a, lambda e, i2=i2: e.activation(sc1[i2].ap, sc1[i2].ap, AF.Exp,
                                                    scale=-1.0),
                   reads=[sc1[i2]], writes=[sc1[i2]])
            s.emit(a, lambda e, i2=i2: e.activation(sc1[i2].ap, sc1[i2].ap, AF.Ln,
                                                    bias=1.0),
                   reads=[sc1[i2]], writes=[sc1[i2]])
            s.emit(v, lambda e, i2=i2: e.scalar_tensor_tensor(
                       dtt[i2].ap, pbt[i2].ap, 0.0, sc1[i2].ap, ALU.max, ALU.add),
                   reads=[pbt[i2], sc1[i2]], writes=[dtt[i2]])
            s.dma(dtl_h[mo * P:(mo + 1) * P, sl], dtt[i2].ap, reads=[dtt[i2]], writes=[dtl_d])
            po2 = C["ps_h"][i2]
            for kb in range(DBLK):
                s.mm(po2, C["w1b"][kb], C["w1b"][kb][:, D + mo * P:D + (mo + 1) * P],
                     xn[kb], xn[kb].ap, start=(kb == 0), stop=(kb == DBLK - 1))
            s.emit(a, lambda e, po2=po2, mo=mo, i2=i2: e.activation(
                       xssm[i2].ap, po2.ap, AF.Identity, bias=bxc[:, mo:mo + 1]),
                   reads=[po2, bxc], writes=[xssm[i2]], psum_src=True)
            s.emit(v, lambda e, i2=i2: e.tensor_mul(xssm[i2].ap, xssm[i2].ap, dtt[i2].ap),
                   reads=[dtt[i2], xssm[i2]], writes=[xssm[i2]])
            s.dma(vvl_h[mo * P:(mo + 1) * P, sl], xssm[i2].ap, reads=[xssm[i2]], writes=[vvl_d])
        for name, out_h2, out_d2, bias in (("wb", bml_h, bml_d, bb16),
                                           ("wc", ccl_h, ccl_d, bc16)):
            wbv = wsmall[name]
            for kb in range(DBLK):
                s.mm(ps_bc, wbv, wbv[:, kb * NSTATE:(kb + 1) * NSTATE],
                     xn[kb], xn[kb].ap, start=(kb == 0), stop=(kb == DBLK - 1),
                     out_ap=ps_bc.ap)
            s.emit(a, lambda e, bias=bias: e.activation(
                       sm16.ap, ps_bc.ap, AF.Identity, bias=bias[:, 0:1]),
                   reads=[ps_bc, bias], writes=[sm16], psum_src=True)
            s.dma(out_h2[:, sl], sm16.ap, reads=[sm16], writes=[out_d2])

    rg = [list(core_ids)]
    acol = sbuf(nc, "acol", [P, DBLK * NSTATE], F32)
    s.dma(acol.ap, al_h.rearrange("(k p) n -> p k n", p=P), writes=[acol])
    s.emit(a, lambda e: e.activation(acol.ap, acol.ap, AF.Exp), reads=[acol], writes=[acol])
    s.emit(v, lambda e: e.tensor_scalar(acol.ap, acol.ap, -1.0, None, ALU.mult),
           reads=[acol], writes=[acol])
    yloc = [Buf(w1big[:, dj * 4096:(dj + 1) * 4096].bitcast(F32)) for dj in range(DBLK)]
    at = Buf(h1big[:, 0:4096].bitcast(F32))
    ut = Buf(h1big[:, 4096:8192].bitcast(F32))
    ht = Buf(h1big[:, 8192:12288].bitcast(F32))
    qt = Buf(h1big[:, 12288:16384].bitcast(F32))
    dtb = C["wst"][0]
    vvb = C["wst"][1]
    zerot = Buf(C["w2b"][0][:, 0:4096].bitcast(F32))
    rowB = [Buf(C["w2b"][0][0:1, 0:4096].bitcast(F32)),
            Buf(C["w2b"][0][32:33, 0:4096].bitcast(F32))]
    rowC = [Buf(C["w2b"][1][0:1, 0:4096].bitcast(F32)),
            Buf(C["w2b"][1][32:33, 0:4096].bitcast(F32))]
    rowbase = {id(rowB[0]): 0, id(rowB[1]): 32, id(rowC[0]): 0, id(rowC[1]): 32}
    ones128 = sbuf(nc, "ones128", [P, P], F32)
    s.emit(v, lambda e: e.memset(ones128.ap, 1.0), writes=[ones128])
    stot = sbuf(nc, "stot", [P, 1], F32)
    hendL = sbuf(nc, "hendL", [P, NSTATE], F32)
    wtotL = sbuf(nc, "wtotL", [P, NSTATE], F32)
    hinT = sbuf(nc, "hinT", [P, NSTATE], F32)
    e1T = sbuf(nc, "e1T", [P, NSTATE], F32)
    wjT = sbuf(nc, "wjT", [P, NSTATE], F32)
    hjT = sbuf(nc, "hjT", [P, NSTATE], F32)
    selm = sbuf(nc, "selmT", [P, NCORES], F32)
    s.dma(selm.ap, selm_h[:], writes=[selm])
    NPB = TC // TF

    pbS = Buf(psA[:, :])
    pcS = Buf(psB[:, :])

    dtres = [Buf(C["ti4"].t[:, :].bitcast(F32)), Buf(C["uc4"].t[:, :])]
    vvres = [Buf(C["ac4"].t[:, :]), C["wst"][0]]
    hendL2 = sbuf(nc, "hendL2", [P, NSTATE], F32)
    hinT2 = sbuf(nc, "hinT2", [P, NSTATE], F32)
    hends = [hendL, hendL2]
    hinTs = [hinT, hinT2]
    CH = 2

    def bcast_row(row, strip, arena):
        rb = rowbase[id(row)]
        for p in range(NPB):
            s.mm(strip, ones128, ones128[rb:rb + 1, :], row,
                 row[0:1, p * TF:(p + 1) * TF], start=True, stop=True,
                 out_ap=arena[:, p * TF:(p + 1) * TF])

    for ch in range(DBLK // CH):
        djs = [ch * CH + i for i in range(CH)]
        for i, dj in enumerate(djs):
            s.dma(dtres[i].ap, dtl_h[dj * P:(dj + 1) * P, :],
                  reads=[dtl_d], writes=[dtres[i]])
            s.dma(vvres[i].ap, vvl_h[dj * P:(dj + 1) * P, :],
                  reads=[vvl_d], writes=[vvres[i]])
        for n in range(NSTATE):
            i2 = n % 2
            rC, rB = rowC[i2], rowB[i2]
            s.dma(rC.ap, ccl_h[n:n + 1, :], reads=[ccl_d], writes=[rC])
            s.dma(rB.ap, bml_h[n:n + 1, :], reads=[bml_d], writes=[rB])
            bcast_row(rB, pbS, psA)
            bcast_row(rC, pcS, psB)
            for i, dj in enumerate(djs):
                s.emit(a, lambda e, i=i, n=n, dj=dj: e.activation(
                           at.ap, dtres[i].ap, AF.Exp,
                           scale=acol[:, dj * NSTATE + n:dj * NSTATE + n + 1]),
                       reads=[dtres[i], acol], writes=[at])
                s.emit(v, lambda e, i=i: e.tensor_tensor(
                           ut.ap, vvres[i].ap, pbS.ap, ALU.mult),
                       reads=[vvres[i], pbS], writes=[ut], psum_src=True)
                s.emit(v, lambda e: e.tensor_tensor_scan(
                           ht.ap, at.ap, ut.ap, 0.0, ALU.mult, ALU.add),
                       reads=[at, ut], writes=[ht])
                s.emit(v, lambda e, i=i, n=n: e.tensor_copy(
                           hends[i][:, n:n + 1], ht[:, TC - 1:TC]),
                       reads=[ht], writes=[hends[i]])
                s.emit(v, lambda e: e.tensor_tensor(qt.ap, ht.ap, pcS.ap, ALU.mult),
                       reads=[ht, pcS], writes=[qt], psum_src=True)
                if n == 0:
                    s.emit(v, lambda e, dj=dj: e.tensor_copy(yloc[dj].ap, qt.ap),
                           reads=[qt], writes=[yloc[dj]])
                else:
                    s.emit(v, lambda e, dj=dj: e.tensor_add(
                               yloc[dj].ap, yloc[dj].ap, qt.ap),
                           reads=[qt, yloc[dj]], writes=[yloc[dj]])
        for i, dj in enumerate(djs):
            s.emit(v, lambda e, i=i: e.tensor_reduce(
                       stot.ap, dtres[i].ap, AX.X, ALU.add),
                   reads=[dtres[i]], writes=[stot])
            for n in range(NSTATE):
                s.emit(a, lambda e, n=n, dj=dj: e.activation(
                           wtotL[:, n:n + 1], stot.ap, AF.Exp,
                           scale=acol[:, dj * NSTATE + n:dj * NSTATE + n + 1]),
                       reads=[stot, acol], writes=[wtotL])
            s.dma(agin_h[dj * P:(dj + 1) * P, :], hends[i].ap,
                  reads=[hends[i]], writes=[agin_d])
            s.dma(agin_h[D + dj * P:D + (dj + 1) * P, :], wtotL.ap,
                  reads=[wtotL], writes=[agin_d])

    s.emit(g, lambda e: e.collective_compute(
               "AllGather", ALU.bypass, replica_groups=rg,
               ins=[agin_h[:]], outs=[agout_h[:]]),
           reads=[agin_d], writes=[agout_d], dma=False)

    s.emit(v, lambda e: e.memset(zerot.ap, 0.0), writes=[zerot])
    for ch in range(DBLK // CH):
        djs = [ch * CH + i for i in range(CH)]
        for i, dj in enumerate(djs):
            s.dma(dtres[i].ap, dtl_h[dj * P:(dj + 1) * P, :],
                  reads=[dtl_d], writes=[dtres[i]])
            s.emit(v, lambda e, i=i: e.memset(hinTs[i].ap, 0.0), writes=[hinTs[i]])
            for j in range(NCORES):
                base = j * 2 * D
                s.dma(hjT.ap, agout_h[base + dj * P:base + (dj + 1) * P, :],
                      reads=[agout_d], writes=[hjT])
                s.dma(wjT.ap, agout_h[base + D + dj * P:base + D + (dj + 1) * P, :],
                      reads=[agout_d], writes=[wjT])
                s.emit(v, lambda e, i=i: e.tensor_mul(e1T.ap, wjT.ap, hinTs[i].ap),
                       reads=[wjT, hinTs[i]], writes=[e1T])
                s.emit(v, lambda e: e.tensor_add(e1T.ap, e1T.ap, hjT.ap),
                       reads=[e1T, hjT], writes=[e1T])
                s.emit(v, lambda e, i=i: e.tensor_sub(e1T.ap, e1T.ap, hinTs[i].ap),
                       reads=[e1T, hinTs[i]], writes=[e1T])
                s.emit(v, lambda e, i=i, j=j: e.scalar_tensor_tensor(
                           hinTs[i].ap, e1T.ap, selm[:, j:j + 1], hinTs[i].ap,
                           ALU.mult, ALU.add),
                       reads=[e1T, selm, hinTs[i]], writes=[hinTs[i]])
        for n in range(NSTATE):
            rC = rowC[n % 2]
            s.dma(rC.ap, ccl_h[n:n + 1, :], reads=[ccl_d], writes=[rC])
            bcast_row(rC, pcS, psB)
            for i, dj in enumerate(djs):
                s.emit(a, lambda e, i=i, n=n, dj=dj: e.activation(
                           at.ap, dtres[i].ap, AF.Exp,
                           scale=acol[:, dj * NSTATE + n:dj * NSTATE + n + 1]),
                       reads=[dtres[i], acol], writes=[at])
                s.emit(v, lambda e, i=i, n=n: e.tensor_tensor_scan(
                           ht.ap, at.ap, zerot.ap, hinTs[i][:, n:n + 1],
                           ALU.mult, ALU.add),
                       reads=[at, zerot, hinTs[i]], writes=[ht])
                s.emit(v, lambda e: e.tensor_tensor(qt.ap, ht.ap, pcS.ap, ALU.mult),
                       reads=[ht, pcS], writes=[qt], psum_src=True)
                s.emit(v, lambda e, dj=dj: e.tensor_add(
                           yloc[dj].ap, yloc[dj].ap, qt.ap),
                       reads=[qt, yloc[dj]], writes=[yloc[dj]])
        for i, dj in enumerate(djs):
            zview = Buf(C["w2b"][1][0:P, 0:2048])
            gview = Buf(C["w2b"][1][0:P, 2048:4096])
            s.dma(zview.ap, z_h[dj * P:(dj + 1) * P, :], reads=[z_d], writes=[zview])
            s.emit(v, lambda e, dj=dj, zview=zview, gview=gview: e.tensor_tensor(
                       gview.ap, yloc[dj].ap, zview.ap, ALU.mult),
                   reads=[yloc[dj], zview], writes=[gview])
            s.dma(gl_h[dj * P:(dj + 1) * P, :], gview.ap, reads=[gview], writes=[gl_d])

    emit_w1_load(s, C, w1_h)
    gt = C["h1"][:DBLK]
    xo_d = Buf(None)
    for f in range(NT):
        sl = slice(f * TF, (f + 1) * TF)
        for kb in range(DBLK):
            s.dma(xs[kb].ap, x2_h[kb * P:(kb + 1) * P, sl], reads=[x2_d], writes=[xs[kb]])
        for kb in range(DBLK):
            st = C["wst"][kb % 2]
            s.dma(st[:, 0:D], wo_h[kb * P:(kb + 1) * P, :], writes=[st])
            s.emit(v, lambda e, kb=kb, st=st: e.tensor_copy(
                       C["w2b"][0][:, kb * 512:kb * 512 + 512], st[:, 0:512]),
                   reads=[st], writes=[C["w2b"][0]])
            s.emit(v, lambda e, kb=kb, st=st: e.tensor_copy(
                       C["w2b"][1][:, kb * 512:kb * 512 + 512], st[:, 512:D]),
                   reads=[st], writes=[C["w2b"][1]])
        for kb in range(DBLK):
            s.dma(gt[kb].ap, gl_h[kb * P:(kb + 1) * P, sl], reads=[gl_d], writes=[gt[kb]])
        for mo in range(DBLK):
            po = C["ps_o"][mo % 2]
            half, off = divmod(mo * P, 512)
            wob = C["w2b"][half]
            for kb in range(DBLK):
                s.mm(po, wob, wob[:, kb * 512 + off:kb * 512 + off + P],
                     gt[kb], gt[kb].ap, start=(kb == 0), stop=(kb == DBLK - 1))
            s.emit(v, lambda e, po=po, mo=mo: e.scalar_tensor_tensor(
                       xs[mo].ap, po.ap, boc[:, mo:mo + 1], xs[mo].ap, ALU.add, ALU.add),
                   reads=[po, boc, xs[mo]], writes=[xs[mo]], psum_src=True)
        emit_ffn(s, C, xs, w2_h)
        for kb in range(DBLK):
            s.dma(xo_h[kb * P:(kb + 1) * P, sl], xs[kb].ap, reads=[xs[kb]], writes=[xo_d])

    s.emit("sync", lambda e: e.nop(), deps=[("sync", s.val["sync"])])
    s.finalize()
    return nc


def _col(b):
    n = b.shape[0] // P
    return np.ascontiguousarray(b.reshape(n, P).T)


def kernel(**inputs):
    xf = np.asarray(inputs["L0_attn_x"], np.float32).reshape(B * L, D)
    shared = dict(
        w1=np.ascontiguousarray(np.asarray(inputs["L0_ffn_w1"], np.float32)),
        b1=_col(np.asarray(inputs["L0_ffn_b1"], np.float32)),
        w2=np.ascontiguousarray(np.asarray(inputs["L0_ffn_w2"], np.float32)),
        b2=_col(np.asarray(inputs["L0_ffn_b2"], np.float32)),
        wz=np.ascontiguousarray(np.asarray(inputs["L0_attn_Wz"], np.float32)),
        wx=np.ascontiguousarray(np.asarray(inputs["L0_attn_Wx"], np.float32)),
        wdt=np.ascontiguousarray(np.asarray(inputs["L0_attn_Wdt"], np.float32)),
        bz=_col(np.asarray(inputs["L0_attn_bz"], np.float32)),
        bx=_col(np.asarray(inputs["L0_attn_bx"], np.float32)),
        bdt=_col(np.asarray(inputs["L0_attn_bdt"], np.float32)),
        wb=np.ascontiguousarray(np.asarray(inputs["L0_attn_WB"], np.float32)),
        wc=np.ascontiguousarray(np.asarray(inputs["L0_attn_WC"], np.float32)),
        bb=np.ascontiguousarray(np.asarray(inputs["L0_attn_bB"], np.float32).reshape(NSTATE, 1)),
        bc=np.ascontiguousarray(np.asarray(inputs["L0_attn_bC"], np.float32).reshape(NSTATE, 1)),
        wo=np.ascontiguousarray(np.asarray(inputs["L0_attn_Wout"], np.float32)),
        bo=_col(np.asarray(inputs["L0_attn_bout"], np.float32)),
    )
    alog = np.asarray(inputs["L0_attn_A_log"], np.float32)
    core_ids = list(range(NCORES))
    nc = build_fused(core_ids)
    maps = []
    for i in range(NCORES):
        m = dict(shared)
        m["x"] = np.ascontiguousarray(xf[i * TC:(i + 1) * TC, :].T)
        m["alog"] = np.ascontiguousarray(alog)
        sel = np.zeros((P, NCORES), np.float32)
        for j in range(NCORES):
            if j // 4 == i // 4 and j % 4 < i % 4:
                sel[:, j] = 1.0
        m["selm"] = sel
        maps.append(m)
    res = run_bass_kernel_spmd(nc, maps, core_ids=core_ids)
    out = np.empty((B * L, D), np.float32)
    for i in range(NCORES):
        out[i * TC:(i + 1) * TC, :] = np.asarray(res.results[i]["xo"], np.float32).T
    return out.reshape(B, L, D)


# version 11
# speedup vs baseline: 1.7019x; 1.0232x over previous
import math
from contextlib import ExitStack

import numpy as np
import ml_dtypes

import concourse.bass as bass
import concourse.mybir as mybir
from concourse.bass_utils import run_bass_kernel_spmd

F32 = mybir.dt.float32
I32 = mybir.dt.int32
BF16 = mybir.dt.bfloat16
AF = mybir.ActivationFunctionType
ALU = mybir.AluOpType
AX = mybir.AxisListType

P = 128
D = 1024
FF = 4096
NSTATE = 16
B = 2
L = 8192
NCORES = 8
TC = B * L // NCORES
DBLK = D // P
FBLK = FF // P
TF = 512
NT = TC // TF
TP2 = 2048
EPS = 1e-5

SERIALIZE = True
CHAIN_EXEMPT = set()
ISSUE_CHAIN = True
PE_EXEMPT = False
DEBUG_TAPS = False


def _gelu64(x):
    x = np.asarray(x, np.float64)
    return 0.5 * x * (1.0 + np.vectorize(math.erf)(x / math.sqrt(2.0)))

def _build_pwl():
    X = np.linspace(-4.0, 4.0, 33)
    Y = _gelu64(X)
    j = np.arange(17)
    qv = 256.0 * Y[16 - j]
    g = (qv[1:] - qv[:-1]) / 64.0
    g0 = g[0]
    dj = g[1:] - g[:-1]
    t = np.arange(-40000, 40001, dtype=np.float64)
    uc = np.minimum(np.abs(t), 1024.0)
    q = g0 * uc
    for k in range(1, 16):
        q = q + dj[k - 1] * np.maximum(uc - 64.0 * k, 0.0)
    mine = np.maximum(t, 0.0) + np.round(q)
    xq = t / 256.0
    y = np.interp(xq, X, Y)
    y = np.where(xq > 4.0, xq, np.where(xq < -4.0, 0.0, y))
    ref = np.round(256.0 * y)
    assert np.array_equal(mine, ref), "PWL hinge table does not match reference"
    return float(g0), [float(v) for v in dj]

_PWL_G0, _PWL_DJ = _build_pwl()


class Buf:
    __slots__ = ("t", "last_write", "readers")
    def __init__(self, t):
        self.t = t
        self.last_write = None
        self.readers = []
    def __getitem__(self, sl):
        return self.t[sl]
    @property
    def ap(self):
        return self.t[:]


class Sch:
    ENGS = ("sync", "vector", "scalar", "gpsimd", "tensor")
    SEMS = ENGS + ("syncI", "p0", "p1", "p2", "p3")

    def __init__(self, nc):
        self.nc = nc
        self.ops = {e: [] for e in self.ENGS}
        self.val = {e: 0 for e in self.SEMS}
        self.es = ExitStack()
        self.sem = {e: self.es.enter_context(nc.semaphore(f"s_{e}")) for e in self.SEMS}
        self.prev = None

    def emit(self, eng, fn, reads=(), writes=(), deps=(), dma=False, psum_src=False,
             ccomp=False, sem=None):
        dl = list(deps)
        if SERIALIZE and eng not in CHAIN_EXEMPT \
                and self.prev is not None and self.prev[0] != eng:
            dl.append(self.prev)
        for b in reads:
            if b.last_write is not None:
                dl.append(b.last_write)
        for b in writes:
            if b.last_write is not None:
                dl.append(b.last_write)
            dl.extend(b.readers)
        inc = 16 if dma else 1
        cname = sem or eng
        self.val[cname] += inc
        tok = (cname, self.val[cname])
        self.ops[eng].append((fn, dl, inc, psum_src, sem))
        if dma and ISSUE_CHAIN and not ccomp:
            self.val["syncI"] += 1
            itok = ("syncI", self.val["syncI"])
            self.ops[eng].append(
                (lambda e: e.nop(cycle_cnt=0, nofuse=True), [], 1, False, "syncI"))
            if eng not in CHAIN_EXEMPT:
                self.prev = itok
        elif eng not in CHAIN_EXEMPT:
            self.prev = tok
        for b in reads:
            b.readers.append(tok)
        for b in writes:
            b.last_write = tok
            b.readers = []
        return tok

    def mm(self, ps, lhsT_buf, lhsT_ap, rhs_buf, rhs_ap, start, stop, out_ap=None):
        dl = []
        if SERIALIZE and not PE_EXEMPT \
                and self.prev is not None and self.prev[0] != "tensor":
            dl.append(self.prev)
        if lhsT_buf is not None and lhsT_buf.last_write is not None:
            dl.append(lhsT_buf.last_write)
        if rhs_buf is not None and rhs_buf.last_write is not None:
            dl.append(rhs_buf.last_write)
        if start:
            if ps.last_write is not None:
                dl.append(ps.last_write)
            dl.extend(ps.readers)
        self.val["tensor"] += 1
        tok = ("tensor", self.val["tensor"])
        if not PE_EXEMPT:
            self.prev = tok
        oap = ps.ap if out_ap is None else out_ap
        self.ops["tensor"].append(
            (lambda e, oap=oap, la=lhsT_ap, ra=rhs_ap, st=start, sp=stop:
             e.matmul(oap, la, ra, start=st, stop=sp), dl, 1, False, None))
        if lhsT_buf is not None:
            lhsT_buf.readers.append(tok)
        if rhs_buf is not None:
            rhs_buf.readers.append(tok)
        if stop:
            ps.last_write = tok
            ps.readers = []
        return tok

    def dma(self, dst_ap, src_ap, reads=(), writes=(), deps=(), eng="sync",
            ccomp=False, sem=None):
        import numpy as _np
        assert _np.prod(dst_ap.shape) == _np.prod(src_ap.shape), (
            f"dma size mismatch {dst_ap.shape} <- {src_ap.shape}")
        return self.emit(
            eng, lambda e: e.dma_start(out=dst_ap, in_=src_ap),
            reads=reads, writes=writes, deps=deps, dma=True, ccomp=ccomp, sem=sem)

    def finalize(self):
        nc = self.nc
        with nc.Block() as block:
            for eng in self.ENGS:
                if not self.ops[eng]:
                    continue
                def make(eng):
                    def body(e):
                        water = {s2: 0 for s2 in self.SEMS}
                        for fn, deps, inc, psrc, semov in self.ops[eng]:
                            need = {}
                            for (de, dv) in deps:
                                if dv > water[de]:
                                    need[de] = max(need.get(de, 0), dv)
                            waited_pe = False
                            for de, dv in need.items():
                                e.wait_ge(self.sem[de], dv)
                                water[de] = dv
                                if de == "tensor":
                                    waited_pe = True
                            if psrc and waited_pe:
                                e.nop(cycle_cnt=192, nofuse=True)
                            ins = fn(e)
                            ins.then_inc(self.sem[semov or eng], inc)
                    return body
                getattr(block, eng)(make(eng))
        self.es.close()


def sbuf(nc, name, shape, dtype):
    return Buf(nc.alloc_sbuf_tensor(name, shape, dtype))

def psum(nc, name, shape, dtype=F32):
    return Buf(nc.alloc_psum_tensor(name, shape, dtype))



def emit_ln(s, C, xblk, xn_out):
    v, a = "vector", "scalar"
    ones_col, ones_row = C["ones_col"], C["ones_row"]
    ps_s1, ps_s2, ps_r, ps_m = C["ps_s1"], C["ps_s2"], C["ps_rstd"], C["ps_mmr"]
    st_mean, st_rstd, st_mmr = C["st_mean"], C["st_rstd"], C["st_mmr"]
    tmp_sq, tmp_sq2, tmp32 = C["tmp_sq"], C["tmp_sq2"], C["tmp32"]
    for kb in range(DBLK):
        s.emit(v, lambda e, kb=kb: e.tensor_copy(tmp_sq.ap, xblk[kb].ap),
               reads=[xblk[kb]], writes=[tmp_sq])
        s.mm(ps_s1, ones_col, ones_col.ap, tmp_sq, tmp_sq.ap,
             start=(kb == 0), stop=(kb == DBLK - 1))
        s.emit(a, lambda e, kb=kb: e.activation(tmp_sq2.ap, xblk[kb].ap, AF.Square),
               reads=[xblk[kb]], writes=[tmp_sq2])
        s.mm(ps_s2, ones_col, ones_col.ap, tmp_sq2, tmp_sq2.ap,
             start=(kb == 0), stop=(kb == DBLK - 1))
    s.emit(v, lambda e: e.tensor_scalar(st_mean.ap, ps_s1.ap, 1.0 / D, None, ALU.mult),
           reads=[ps_s1], writes=[st_mean], psum_src=True)
    s.emit(v, lambda e: e.tensor_mul(st_rstd.ap, st_mean.ap, st_mean.ap),
           reads=[st_mean], writes=[st_rstd])
    s.emit(v, lambda e: e.scalar_tensor_tensor(st_rstd.ap, ps_s2.ap, 1.0 / D,
                                               st_rstd.ap, ALU.mult, ALU.subtract),
           reads=[ps_s2, st_rstd], writes=[st_rstd], psum_src=True)
    s.emit(a, lambda e: e.activation(st_rstd.ap, st_rstd.ap, AF.Ln,
                                     bias=C["eps1"][0:1, 0:1]),
           reads=[st_rstd, C["eps1"]], writes=[st_rstd])
    s.emit(a, lambda e: e.activation(st_rstd.ap, st_rstd.ap, AF.Exp, scale=-0.5),
           reads=[st_rstd], writes=[st_rstd])
    s.emit(v, lambda e: e.tensor_mul(st_mmr.ap, st_mean.ap, st_rstd.ap),
           reads=[st_mean, st_rstd], writes=[st_mmr])
    s.mm(ps_r, ones_row, ones_row.ap, st_rstd, st_rstd.ap, start=True, stop=True)
    s.mm(ps_m, ones_row, ones_row.ap, st_mmr, st_mmr.ap, start=True, stop=True)
    for kb in range(DBLK):
        s.emit(v, lambda e, kb=kb: e.tensor_mul(tmp32.ap, xblk[kb].ap, ps_r.ap),
               reads=[xblk[kb], ps_r], writes=[tmp32], psum_src=True)
        s.emit(v, lambda e, kb=kb: e.tensor_sub(xn_out[kb].ap, tmp32.ap, ps_m.ap),
               reads=[tmp32, ps_m], writes=[xn_out[kb]], psum_src=True)


def emit_pwl4(s, C, g0):
    v, a = "vector", "scalar"
    ti4, uc4, ac4, ht4 = C["ti4"], C["uc4"], C["ac4"], C["ht4"]
    for i in range(4):
        m = g0 + i
        ph = C["ps4"][i]
        s.emit(v, lambda e, ph=ph, m=m, i=i: e.tensor_scalar(
                   ti4[:, i * TF:(i + 1) * TF], ph.ap, 256.0, C["b1c"][:, m:m + 1],
                   ALU.mult, ALU.add),
               reads=[ph, C["b1c_buf"]], writes=[ti4], psum_src=True)
    s.emit(v, lambda e: e.scalar_tensor_tensor(uc4.ap, ti4.ap, -1.0, ti4.ap,
                                               ALU.mult, ALU.max),
           reads=[ti4], writes=[uc4])
    s.emit(v, lambda e: e.tensor_scalar(uc4.ap, uc4.ap, 1024.0, None, ALU.min),
           reads=[uc4], writes=[uc4])
    s.emit(v, lambda e: e.tensor_scalar(ac4.ap, uc4.ap, _PWL_G0, None, ALU.mult),
           reads=[uc4], writes=[ac4])
    s.emit(v, lambda e: e.scalar_tensor_tensor(ac4.ap, ti4.ap, 0.0, ac4.ap,
                                               ALU.max, ALU.add),
           reads=[ti4, ac4], writes=[ac4])
    hb_ = ht4[0]
    for k in range(1, 16):
        dk = _PWL_DJ[k - 1]
        s.emit(v, lambda e, k=k: e.tensor_scalar(
                   hb_.ap, uc4.ap, -64.0 * k, 0.0, ALU.add, ALU.max),
               reads=[uc4], writes=[hb_])
        s.emit(v, lambda e, dk=dk: e.scalar_tensor_tensor(
                   ac4.ap, hb_.ap, dk, ac4.ap, ALU.mult, ALU.add),
               reads=[hb_, ac4], writes=[ac4])
    s.emit(v, lambda e: e.tensor_scalar(ti4.ap, ac4.ap, 1.0, None, ALU.mult),
           reads=[ac4], writes=[ti4])
    hw = [C["h1"][g0 + i] for i in range(4)]
    s.emit(v, lambda e, g0=g0: e.tensor_copy(
               C["h1big"][:, g0 * TF:(g0 + 4) * TF], ti4.ap),
           reads=[ti4], writes=hw)


def emit_ffn(s, C, xblk, w2_h):
    v = "vector"
    xn = C["xn"]
    emit_ln(s, C, xblk, xn)
    h1 = C["h1"]
    for g0 in range(0, FBLK, 4):
        for i in range(4):
            m = g0 + i
            ph = C["ps4"][i]
            for kb in range(DBLK):
                s.mm(ph, C["w1b"][kb], C["w1b"][kb][:, m * P:(m + 1) * P],
                     xn[kb], xn[kb].ap, start=(kb == 0), stop=(kb == DBLK - 1))
        emit_pwl4(s, C, g0)
    w2r = w2_h.rearrange("(m p) o -> p m o", p=P)
    hm = FBLK // 2

    def w2dma(mo, h):
        s.dma(C["wst"][h].ap, w2r[:, h * hm:(h + 1) * hm, mo * P:(mo + 1) * P],
              writes=[C["wst"][h]], sem=f"p{h}")
    w2dma(0, 0)
    w2dma(0, 1)
    for mo in range(DBLK):
        w2b = C["w2b"][mo % 2]
        for h in range(2):
            w2s = C["wst"][h]
            s.emit(v, lambda e, w2b=w2b, w2s=w2s, h=h: e.tensor_copy(
                       w2b[:, h * hm * P:(h + 1) * hm * P], w2s.ap),
                   reads=[w2s], writes=[w2b])
        if mo + 1 < DBLK:
            w2dma(mo + 1, 0)
            w2dma(mo + 1, 1)
        po = C["ps_o"][mo % 2]
        for m in range(FBLK):
            s.mm(po, w2b, w2b[:, m * P:(m + 1) * P], h1[m], h1[m].ap,
                 start=(m == 0), stop=(m == FBLK - 1))
        t3 = C["t3"][mo % 2]
        s.emit(v, lambda e, t3=t3, po=po, mo=mo: e.tensor_scalar(
                   t3.ap, po.ap, C["b2c"][:, mo:mo + 1], None, ALU.add),
               reads=[po, C["b2c_buf"]], writes=[t3], psum_src=True)
        s.emit(v, lambda e, t3=t3, mo=mo: e.scalar_tensor_tensor(
                   xblk[mo].ap, t3.ap, 1.0 / 512.0, xblk[mo].ap, ALU.mult, ALU.add),
               reads=[t3, xblk[mo]], writes=[xblk[mo]])


def emit_w1_load(s, C, w1_h):
    v = "vector"
    stages = C["wstage"]
    ns = len(stages)
    steps = [(kb, h) for kb in range(DBLK) for h in range(2)]

    def issue(i):
        kb, h = steps[i]
        sl = slice(h * (FF // 2), (h + 1) * (FF // 2))
        st = stages[i % ns]
        s.dma(st.ap, w1_h[kb * P:(kb + 1) * P, sl], writes=[st], sem=f"p{i % ns}")
    for i in range(min(ns, len(steps))):
        issue(i)
    for i, (kb, h) in enumerate(steps):
        st = stages[i % ns]
        sl = slice(h * (FF // 2), (h + 1) * (FF // 2))
        s.emit(v, lambda e, kb=kb, st=st, sl=sl: e.tensor_copy(
                   C["w1b"][kb][:, sl], st.ap),
               reads=[st], writes=[C["w1b"][kb]])
        if i + ns < len(steps):
            issue(i + ns)



def build_fused(core_ids):
    nc = bass.Bass("TRN2", target_bir_lowering=False, debug=False)
    x_h = nc.dram_tensor("x", [D, TC], F32, kind="ExternalInput")
    w1_h = nc.dram_tensor("w1", [D, FF], F32, kind="ExternalInput")
    b1_h = nc.dram_tensor("b1", [P, FBLK], F32, kind="ExternalInput")
    w2_h = nc.dram_tensor("w2", [FF, D], F32, kind="ExternalInput")
    b2_h = nc.dram_tensor("b2", [P, DBLK], F32, kind="ExternalInput")
    wz_h = nc.dram_tensor("wz", [D, D], F32, kind="ExternalInput")
    wx_h = nc.dram_tensor("wx", [D, D], F32, kind="ExternalInput")
    wdt_h = nc.dram_tensor("wdt", [D, D], F32, kind="ExternalInput")
    bz_h = nc.dram_tensor("bz", [P, DBLK], F32, kind="ExternalInput")
    bx_h = nc.dram_tensor("bx", [P, DBLK], F32, kind="ExternalInput")
    bdt_h = nc.dram_tensor("bdt", [P, DBLK], F32, kind="ExternalInput")
    wb_h = nc.dram_tensor("wb", [D, NSTATE], F32, kind="ExternalInput")
    wc_h = nc.dram_tensor("wc", [D, NSTATE], F32, kind="ExternalInput")
    bb_h = nc.dram_tensor("bb", [NSTATE, 1], F32, kind="ExternalInput")
    bc_h = nc.dram_tensor("bc", [NSTATE, 1], F32, kind="ExternalInput")
    al_h = nc.dram_tensor("alog", [D, NSTATE], F32, kind="ExternalInput")
    wo_h = nc.dram_tensor("wo", [D, D], F32, kind="ExternalInput")
    bo_h = nc.dram_tensor("bo", [P, DBLK], F32, kind="ExternalInput")
    xo_h = nc.dram_tensor("xo", [D, TC], F32, kind="ExternalOutput")
    KI = "ExternalOutput" if DEBUG_TAPS else "Internal"
    dtl_h = nc.dram_tensor("dtl", [D, TC], F32, kind=KI)
    vvl_h = nc.dram_tensor("vvl", [D, TC], F32, kind=KI)
    bml_h = nc.dram_tensor("bml", [NSTATE, TC], F32, kind=KI)
    ccl_h = nc.dram_tensor("ccl", [NSTATE, TC], F32, kind=KI)
    gl_h = nc.dram_tensor("gl", [D, TC], BF16, kind=KI)
    agin_h = nc.dram_tensor("agin", [2 * D, NSTATE], F32, kind="Internal")
    agout_h = nc.dram_tensor("agout", [NCORES * 2 * D, NSTATE], F32, kind="Internal")
    x2_h = nc.dram_tensor("x2i", [D, TC], F32, kind=KI)
    z_h = nc.dram_tensor("zi", [D, TC], BF16, kind=KI)
    selm_h = nc.dram_tensor("selm", [P, NCORES], F32, kind="ExternalInput")

    s = Sch(nc)
    v, a, g, t = "vector", "scalar", "gpsimd", "tensor"

    C = {}
    C["ones_col"] = sbuf(nc, "ones_col", [P, 1], BF16)
    C["ones_row"] = sbuf(nc, "ones_row", [1, P], F32)
    s.emit(v, lambda e: e.memset(C["ones_col"].ap, 1.0), writes=[C["ones_col"]])
    s.emit(v, lambda e: e.memset(C["ones_row"].ap, 1.0), writes=[C["ones_row"]])
    C["eps1"] = sbuf(nc, "eps1", [1, 1], F32)
    s.emit(v, lambda e: e.memset(C["eps1"].ap, EPS), writes=[C["eps1"]])
    psA = nc.alloc_psum_tensor("psA", [P, 4 * TF], F32)
    psB = nc.alloc_psum_tensor("psB", [P, 4 * TF], F32)
    C["ps_s1"] = Buf(psA[0:1, 0:TF])
    C["ps_s2"] = Buf(psA[0:1, TF:2 * TF])
    C["ps_rstd"] = Buf(psA[:, 2 * TF:3 * TF])
    C["ps_mmr"] = Buf(psA[:, 3 * TF:4 * TF])
    C["ps_h"] = [Buf(psB[:, 0:TF]), Buf(psB[:, TF:2 * TF])]
    C["ps_o"] = [Buf(psB[:, 2 * TF:3 * TF]), Buf(psB[:, 3 * TF:4 * TF])]
    C["psA"], C["psB"] = psA, psB
    C["st_mean"] = sbuf(nc, "st_mean", [1, TF], F32)
    C["st_rstd"] = sbuf(nc, "st_rstd", [1, TF], F32)
    C["st_mmr"] = sbuf(nc, "st_mmr", [1, TF], F32)
    C["tmp_sq"] = sbuf(nc, "tmp_sq", [P, TF], BF16)
    C["tmp_sq2"] = sbuf(nc, "tmp_sq2", [P, TF], BF16)
    C["tmp32"] = sbuf(nc, "tmp32", [P, TF], F32)
    C["xn"] = [sbuf(nc, f"xn{kb}", [P, TF], BF16) for kb in range(DBLK)]
    w1big = nc.alloc_sbuf_tensor("w1big", [P, DBLK * FF], BF16)
    h1big = nc.alloc_sbuf_tensor("h1big", [P, FBLK * TF], BF16)
    C["w1b"] = [Buf(w1big[:, kb * FF:(kb + 1) * FF]) for kb in range(DBLK)]
    C["h1"] = [Buf(h1big[:, m * TF:(m + 1) * TF]) for m in range(FBLK)]
    C["wst"] = [sbuf(nc, f"wst{i}", [P, FF // 2], F32) for i in range(2)]
    C["w2b"] = [sbuf(nc, f"w2b{i}", [P, FF], BF16) for i in range(2)]
    C["t3"] = [sbuf(nc, f"t3_{i}", [P, TF], I32) for i in range(2)]
    C["ti4"] = sbuf(nc, "pw_ti4", [P, 4 * TF], I32)
    C["uc4"] = sbuf(nc, "pw_uc4", [P, 4 * TF], F32)
    C["ac4"] = sbuf(nc, "pw_ac4", [P, 4 * TF], F32)
    C["ht4"] = [Buf(C["w2b"][i].t[:, 0:FF].bitcast(F32)) for i in range(2)]
    C["wstage"] = [C["wst"][0], C["wst"][1],
                   Buf(C["ti4"].t[:, :].bitcast(F32)), Buf(C["uc4"].t[:, :])]
    C["ps4"] = [C["ps_h"][0], C["ps_h"][1], C["ps_o"][0], C["ps_o"][1]]
    C["h1big"] = h1big
    def bias_col(name, h, n, scale=None):
        raw = sbuf(nc, name + "r", [P, n], F32)
        s.dma(raw.ap, h[:], writes=[raw])
        if scale is None:
            return raw
        sc = sbuf(nc, name + "c", [P, n], F32)
        s.emit(v, lambda e: e.tensor_scalar(sc.ap, raw.ap, scale, None, ALU.mult),
               reads=[raw], writes=[sc])
        return sc
    b1c = bias_col("b1", b1_h, FBLK, 256.0)
    b2c = bias_col("b2", b2_h, DBLK, 256.0)
    C["b1c"], C["b1c_buf"] = b1c.t, b1c
    C["b2c"], C["b2c_buf"] = b2c.t, b2c
    bzc = bias_col("bz", bz_h, DBLK)
    bxc = bias_col("bx", bx_h, DBLK)
    bdtc = bias_col("bdt", bdt_h, DBLK)
    boc = bias_col("bo", bo_h, DBLK)
    bb16 = sbuf(nc, "bbc", [NSTATE, 1], F32)
    s.dma(bb16.ap, bb_h[:], writes=[bb16])
    bc16 = sbuf(nc, "bcc", [NSTATE, 1], F32)
    s.dma(bc16.ap, bc_h[:], writes=[bc16])

    xs = [sbuf(nc, f"xs{kb}", [P, TF], F32) for kb in range(DBLK)]
    x2_d, ain_d, aout_d, yin_d, yout_d, z_d = (Buf(None) for _ in range(6))

    emit_w1_load(s, C, w1_h)
    for f in range(NT):
        sl = slice(f * TF, (f + 1) * TF)
        for kb in range(DBLK):
            s.dma(xs[kb].ap, x_h[kb * P:(kb + 1) * P, sl], writes=[xs[kb]])
        emit_ffn(s, C, xs, w2_h)
        for kb in range(DBLK):
            s.dma(x2_h[kb * P:(kb + 1) * P, sl], xs[kb].ap,
                  reads=[xs[kb]], writes=[x2_d])

    pst = [(h, kb, off) for h, off in ((wz_h, 0), (wx_h, D), (wdt_h, 2 * D))
           for kb in range(DBLK)]
    nsp = len(C["wstage"])

    def pissue(i):
        h, kb, off = pst[i]
        st = C["wstage"][i % nsp]
        s.dma(st[:, 0:D], h[kb * P:(kb + 1) * P, :], writes=[st], sem=f"p{i % nsp}")
    for i in range(min(nsp, len(pst))):
        pissue(i)
    for i, (h, kb, off) in enumerate(pst):
        st = C["wstage"][i % nsp]
        s.emit(v, lambda e, kb=kb, st=st, off=off: e.tensor_copy(
                   C["w1b"][kb][:, off:off + D], st[:, 0:D]),
               reads=[st], writes=[C["w1b"][kb]])
        if i + nsp < len(pst):
            pissue(i + nsp)
    wsmall = {}
    for name, h in (("wb", wb_h), ("wc", wc_h)):
        st = C["wst"][0]
        s.dma(st[:, 0:DBLK * NSTATE], h.rearrange("(k p) n -> p k n", p=P), writes=[st])
        wbv = sbuf(nc, f"{name}b", [P, DBLK * NSTATE], BF16)
        s.emit(v, lambda e, wbv=wbv, st=st: e.tensor_copy(wbv.ap, st[:, 0:DBLK * NSTATE]),
               reads=[st], writes=[wbv])
        wsmall[name] = wbv

    xssm = [Buf(h1big[:, i * 1024:(i + 1) * 1024].bitcast(F32)) for i in range(2)]
    dtt = [Buf(h1big[:, (2 + i) * 1024:(3 + i) * 1024].bitcast(F32)) for i in range(2)]
    pbt = [Buf(h1big[:, (4 + i) * 1024:(5 + i) * 1024].bitcast(F32)) for i in range(2)]
    sc1 = [Buf(h1big[:, (6 + i) * 1024:(7 + i) * 1024].bitcast(F32)) for i in range(2)]
    zt = sbuf(nc, "zt", [P, TF], BF16)
    sm16 = sbuf(nc, "sm16", [NSTATE, TF], F32)
    ps_bc = Buf(psA[0:NSTATE, 2 * TF:3 * TF])

    dtl_d, vvl_d, bml_d, ccl_d, gl_d, agin_d, agout_d = (Buf(None) for _ in range(7))
    for f in range(NT):
        sl = slice(f * TF, (f + 1) * TF)
        for kb in range(DBLK):
            s.dma(xs[kb].ap, x2_h[kb * P:(kb + 1) * P, sl], reads=[x2_d], writes=[xs[kb]])
        xn = C["xn"]
        emit_ln(s, C, xs, xn)
        for mo in range(DBLK):
            po = C["ps_o"][mo % 2]
            for kb in range(DBLK):
                s.mm(po, C["w1b"][kb], C["w1b"][kb][:, mo * P:(mo + 1) * P],
                     xn[kb], xn[kb].ap, start=(kb == 0), stop=(kb == DBLK - 1))
            s.emit(a, lambda e, po=po, mo=mo: e.activation(
                       zt.ap, po.ap, AF.Sigmoid, bias=bzc[:, mo:mo + 1]),
                   reads=[po, bzc], writes=[zt], psum_src=True)
            s.dma(z_h[mo * P:(mo + 1) * P, sl], zt.ap, reads=[zt], writes=[z_d])
        for mo in range(DBLK):
            i2 = mo % 2
            po = C["ps_o"][i2]
            for kb in range(DBLK):
                s.mm(po, C["w1b"][kb], C["w1b"][kb][:, 2 * D + mo * P:2 * D + (mo + 1) * P],
                     xn[kb], xn[kb].ap, start=(kb == 0), stop=(kb == DBLK - 1))
            s.emit(a, lambda e, po=po, mo=mo, i2=i2: e.activation(
                       pbt[i2].ap, po.ap, AF.Identity, bias=bdtc[:, mo:mo + 1]),
                   reads=[po, bdtc], writes=[pbt[i2]], psum_src=True)
            s.emit(v, lambda e, i2=i2: e.scalar_tensor_tensor(
                       sc1[i2].ap, pbt[i2].ap, -1.0, pbt[i2].ap, ALU.mult, ALU.max),
                   reads=[pbt[i2]], writes=[sc1[i2]])
            s.emit(a, lambda e, i2=i2: e.activation(sc1[i2].ap, sc1[i2].ap, AF.Exp,
                                                    scale=-1.0),
                   reads=[sc1[i2]], writes=[sc1[i2]])
            s.emit(a, lambda e, i2=i2: e.activation(sc1[i2].ap, sc1[i2].ap, AF.Ln,
                                                    bias=1.0),
                   reads=[sc1[i2]], writes=[sc1[i2]])
            s.emit(v, lambda e, i2=i2: e.scalar_tensor_tensor(
                       dtt[i2].ap, pbt[i2].ap, 0.0, sc1[i2].ap, ALU.max, ALU.add),
                   reads=[pbt[i2], sc1[i2]], writes=[dtt[i2]])
            s.dma(dtl_h[mo * P:(mo + 1) * P, sl], dtt[i2].ap, reads=[dtt[i2]], writes=[dtl_d])
            po2 = C["ps_h"][i2]
            for kb in range(DBLK):
                s.mm(po2, C["w1b"][kb], C["w1b"][kb][:, D + mo * P:D + (mo + 1) * P],
                     xn[kb], xn[kb].ap, start=(kb == 0), stop=(kb == DBLK - 1))
            s.emit(a, lambda e, po2=po2, mo=mo, i2=i2: e.activation(
                       xssm[i2].ap, po2.ap, AF.Identity, bias=bxc[:, mo:mo + 1]),
                   reads=[po2, bxc], writes=[xssm[i2]], psum_src=True)
            s.emit(v, lambda e, i2=i2: e.tensor_mul(xssm[i2].ap, xssm[i2].ap, dtt[i2].ap),
                   reads=[dtt[i2], xssm[i2]], writes=[xssm[i2]])
            s.dma(vvl_h[mo * P:(mo + 1) * P, sl], xssm[i2].ap, reads=[xssm[i2]], writes=[vvl_d])
        for name, out_h2, out_d2, bias in (("wb", bml_h, bml_d, bb16),
                                           ("wc", ccl_h, ccl_d, bc16)):
            wbv = wsmall[name]
            for kb in range(DBLK):
                s.mm(ps_bc, wbv, wbv[:, kb * NSTATE:(kb + 1) * NSTATE],
                     xn[kb], xn[kb].ap, start=(kb == 0), stop=(kb == DBLK - 1),
                     out_ap=ps_bc.ap)
            s.emit(a, lambda e, bias=bias: e.activation(
                       sm16.ap, ps_bc.ap, AF.Identity, bias=bias[:, 0:1]),
                   reads=[ps_bc, bias], writes=[sm16], psum_src=True)
            s.dma(out_h2[:, sl], sm16.ap, reads=[sm16], writes=[out_d2])

    rg = [list(core_ids)]
    acol = sbuf(nc, "acol", [P, DBLK * NSTATE], F32)
    s.dma(acol.ap, al_h.rearrange("(k p) n -> p k n", p=P), writes=[acol])
    s.emit(a, lambda e: e.activation(acol.ap, acol.ap, AF.Exp), reads=[acol], writes=[acol])
    s.emit(v, lambda e: e.tensor_scalar(acol.ap, acol.ap, -1.0, None, ALU.mult),
           reads=[acol], writes=[acol])
    yloc = [Buf(w1big[:, dj * 4096:(dj + 1) * 4096].bitcast(F32)) for dj in range(DBLK)]
    at = Buf(h1big[:, 0:4096].bitcast(F32))
    ut = Buf(h1big[:, 4096:8192].bitcast(F32))
    ht = Buf(h1big[:, 8192:12288].bitcast(F32))
    qt = Buf(h1big[:, 12288:16384].bitcast(F32))
    dtb = C["wst"][0]
    vvb = C["wst"][1]
    zerot = Buf(C["w2b"][0][:, 0:4096].bitcast(F32))
    rowB = [Buf(C["w2b"][0][0:1, 0:4096].bitcast(F32)),
            Buf(C["w2b"][0][32:33, 0:4096].bitcast(F32))]
    rowC = [Buf(C["w2b"][1][0:1, 0:4096].bitcast(F32)),
            Buf(C["w2b"][1][32:33, 0:4096].bitcast(F32))]
    rowbase = {id(rowB[0]): 0, id(rowB[1]): 32, id(rowC[0]): 0, id(rowC[1]): 32}
    ones128 = sbuf(nc, "ones128", [P, P], F32)
    s.emit(v, lambda e: e.memset(ones128.ap, 1.0), writes=[ones128])
    stot = sbuf(nc, "stot", [P, 1], F32)
    hendL = sbuf(nc, "hendL", [P, NSTATE], F32)
    wtotL = sbuf(nc, "wtotL", [P, NSTATE], F32)
    hinT = sbuf(nc, "hinT", [P, NSTATE], F32)
    e1T = sbuf(nc, "e1T", [P, NSTATE], F32)
    wjT = sbuf(nc, "wjT", [P, NSTATE], F32)
    hjT = sbuf(nc, "hjT", [P, NSTATE], F32)
    selm = sbuf(nc, "selmT", [P, NCORES], F32)
    s.dma(selm.ap, selm_h[:], writes=[selm])
    NPB = TC // TF

    pbS = Buf(psA[:, :])
    pcS = Buf(psB[:, :])

    dtres = [Buf(C["ti4"].t[:, :].bitcast(F32)), Buf(C["uc4"].t[:, :])]
    vvres = [Buf(C["ac4"].t[:, :]), C["wst"][0]]
    hendL2 = sbuf(nc, "hendL2", [P, NSTATE], F32)
    hinT2 = sbuf(nc, "hinT2", [P, NSTATE], F32)
    hends = [hendL, hendL2]
    hinTs = [hinT, hinT2]
    CH = 2

    def bcast_row(row, strip, arena):
        rb = rowbase[id(row)]
        for p in range(NPB):
            s.mm(strip, ones128, ones128[rb:rb + 1, :], row,
                 row[0:1, p * TF:(p + 1) * TF], start=True, stop=True,
                 out_ap=arena[:, p * TF:(p + 1) * TF])

    for ch in range(DBLK // CH):
        djs = [ch * CH + i for i in range(CH)]
        for i, dj in enumerate(djs):
            s.dma(dtres[i].ap, dtl_h[dj * P:(dj + 1) * P, :],
                  reads=[dtl_d], writes=[dtres[i]])
            s.dma(vvres[i].ap, vvl_h[dj * P:(dj + 1) * P, :],
                  reads=[vvl_d], writes=[vvres[i]])
        for n in range(NSTATE):
            i2 = n % 2
            rC, rB = rowC[i2], rowB[i2]
            s.dma(rC.ap, ccl_h[n:n + 1, :], reads=[ccl_d], writes=[rC])
            s.dma(rB.ap, bml_h[n:n + 1, :], reads=[bml_d], writes=[rB])
            bcast_row(rB, pbS, psA)
            bcast_row(rC, pcS, psB)
            for i, dj in enumerate(djs):
                s.emit(a, lambda e, i=i, n=n, dj=dj: e.activation(
                           at.ap, dtres[i].ap, AF.Exp,
                           scale=acol[:, dj * NSTATE + n:dj * NSTATE + n + 1]),
                       reads=[dtres[i], acol], writes=[at])
                s.emit(v, lambda e, i=i: e.tensor_tensor(
                           ut.ap, vvres[i].ap, pbS.ap, ALU.mult),
                       reads=[vvres[i], pbS], writes=[ut], psum_src=True)
                s.emit(v, lambda e: e.tensor_tensor_scan(
                           ht.ap, at.ap, ut.ap, 0.0, ALU.mult, ALU.add),
                       reads=[at, ut], writes=[ht])
                s.emit(v, lambda e, i=i, n=n: e.tensor_copy(
                           hends[i][:, n:n + 1], ht[:, TC - 1:TC]),
                       reads=[ht], writes=[hends[i]])
                s.emit(v, lambda e: e.tensor_tensor(qt.ap, ht.ap, pcS.ap, ALU.mult),
                       reads=[ht, pcS], writes=[qt], psum_src=True)
                if n == 0:
                    s.emit(v, lambda e, dj=dj: e.tensor_copy(yloc[dj].ap, qt.ap),
                           reads=[qt], writes=[yloc[dj]])
                else:
                    s.emit(v, lambda e, dj=dj: e.tensor_add(
                               yloc[dj].ap, yloc[dj].ap, qt.ap),
                           reads=[qt, yloc[dj]], writes=[yloc[dj]])
        for i, dj in enumerate(djs):
            s.emit(v, lambda e, i=i: e.tensor_reduce(
                       stot.ap, dtres[i].ap, AX.X, ALU.add),
                   reads=[dtres[i]], writes=[stot])
            for n in range(NSTATE):
                s.emit(a, lambda e, n=n, dj=dj: e.activation(
                           wtotL[:, n:n + 1], stot.ap, AF.Exp,
                           scale=acol[:, dj * NSTATE + n:dj * NSTATE + n + 1]),
                       reads=[stot, acol], writes=[wtotL])
            s.dma(agin_h[dj * P:(dj + 1) * P, :], hends[i].ap,
                  reads=[hends[i]], writes=[agin_d])
            s.dma(agin_h[D + dj * P:D + (dj + 1) * P, :], wtotL.ap,
                  reads=[wtotL], writes=[agin_d])

    s.emit(g, lambda e: e.collective_compute(
               "AllGather", ALU.bypass, replica_groups=rg,
               ins=[agin_h[:]], outs=[agout_h[:]]),
           reads=[agin_d], writes=[agout_d], dma=False)

    for ch in range(DBLK // CH):
        djs = [ch * CH + i for i in range(CH)]
        for i, dj in enumerate(djs):
            s.dma(dtres[i].ap, dtl_h[dj * P:(dj + 1) * P, :],
                  reads=[dtl_d], writes=[dtres[i]])
            s.emit(v, lambda e, i=i: e.memset(hinTs[i].ap, 0.0), writes=[hinTs[i]])
            for j in range(NCORES):
                base = j * 2 * D
                s.dma(hjT.ap, agout_h[base + dj * P:base + (dj + 1) * P, :],
                      reads=[agout_d], writes=[hjT])
                s.dma(wjT.ap, agout_h[base + D + dj * P:base + D + (dj + 1) * P, :],
                      reads=[agout_d], writes=[wjT])
                s.emit(v, lambda e, i=i: e.tensor_mul(e1T.ap, wjT.ap, hinTs[i].ap),
                       reads=[wjT, hinTs[i]], writes=[e1T])
                s.emit(v, lambda e: e.tensor_add(e1T.ap, e1T.ap, hjT.ap),
                       reads=[e1T, hjT], writes=[e1T])
                s.emit(v, lambda e, i=i: e.tensor_sub(e1T.ap, e1T.ap, hinTs[i].ap),
                       reads=[e1T, hinTs[i]], writes=[e1T])
                s.emit(v, lambda e, i=i, j=j: e.scalar_tensor_tensor(
                           hinTs[i].ap, e1T.ap, selm[:, j:j + 1], hinTs[i].ap,
                           ALU.mult, ALU.add),
                       reads=[e1T, selm, hinTs[i]], writes=[hinTs[i]])
        s.emit(v, lambda e: e.memset(at.ap, 1.0), writes=[at])
        for i, dj in enumerate(djs):
            s.emit(v, lambda e, i=i: e.tensor_tensor_scan(
                       vvres[i].ap, at.ap, dtres[i].ap, 0.0, ALU.mult, ALU.add),
                   reads=[at, dtres[i]], writes=[vvres[i]])
        for n in range(NSTATE):
            rC = rowC[n % 2]
            s.dma(rC.ap, ccl_h[n:n + 1, :], reads=[ccl_d], writes=[rC])
            bcast_row(rC, pcS, psB)
            for i, dj in enumerate(djs):
                s.emit(a, lambda e, i=i, n=n, dj=dj: e.activation(
                           at.ap, vvres[i].ap, AF.Exp,
                           scale=acol[:, dj * NSTATE + n:dj * NSTATE + n + 1]),
                       reads=[vvres[i], acol], writes=[at])
                s.emit(v, lambda e, i=i, n=n: e.tensor_scalar(
                           ht.ap, at.ap, 1.0, hinTs[i][:, n:n + 1],
                           ALU.mult, ALU.mult),
                       reads=[at, hinTs[i]], writes=[ht])
                s.emit(v, lambda e: e.tensor_tensor(qt.ap, ht.ap, pcS.ap, ALU.mult),
                       reads=[ht, pcS], writes=[qt], psum_src=True)
                s.emit(v, lambda e, dj=dj: e.tensor_add(
                           yloc[dj].ap, yloc[dj].ap, qt.ap),
                       reads=[qt, yloc[dj]], writes=[yloc[dj]])
        for i, dj in enumerate(djs):
            zview = Buf(C["w2b"][1][0:P, 0:2048])
            gview = Buf(C["w2b"][1][0:P, 2048:4096])
            s.dma(zview.ap, z_h[dj * P:(dj + 1) * P, :], reads=[z_d], writes=[zview])
            s.emit(v, lambda e, dj=dj, zview=zview, gview=gview: e.tensor_tensor(
                       gview.ap, yloc[dj].ap, zview.ap, ALU.mult),
                   reads=[yloc[dj], zview], writes=[gview])
            s.dma(gl_h[dj * P:(dj + 1) * P, :], gview.ap, reads=[gview], writes=[gl_d])

    emit_w1_load(s, C, w1_h)
    gt = C["h1"][:DBLK]
    xo_d = Buf(None)
    for f in range(NT):
        sl = slice(f * TF, (f + 1) * TF)
        for kb in range(DBLK):
            s.dma(xs[kb].ap, x2_h[kb * P:(kb + 1) * P, sl], reads=[x2_d], writes=[xs[kb]])
        for kb in range(DBLK):
            st = C["wst"][kb % 2]
            s.dma(st[:, 0:D], wo_h[kb * P:(kb + 1) * P, :], writes=[st])
            s.emit(v, lambda e, kb=kb, st=st: e.tensor_copy(
                       C["w2b"][0][:, kb * 512:kb * 512 + 512], st[:, 0:512]),
                   reads=[st], writes=[C["w2b"][0]])
            s.emit(v, lambda e, kb=kb, st=st: e.tensor_copy(
                       C["w2b"][1][:, kb * 512:kb * 512 + 512], st[:, 512:D]),
                   reads=[st], writes=[C["w2b"][1]])
        for kb in range(DBLK):
            s.dma(gt[kb].ap, gl_h[kb * P:(kb + 1) * P, sl], reads=[gl_d], writes=[gt[kb]])
        for mo in range(DBLK):
            po = C["ps_o"][mo % 2]
            half, off = divmod(mo * P, 512)
            wob = C["w2b"][half]
            for kb in range(DBLK):
                s.mm(po, wob, wob[:, kb * 512 + off:kb * 512 + off + P],
                     gt[kb], gt[kb].ap, start=(kb == 0), stop=(kb == DBLK - 1))
            s.emit(v, lambda e, po=po, mo=mo: e.scalar_tensor_tensor(
                       xs[mo].ap, po.ap, boc[:, mo:mo + 1], xs[mo].ap, ALU.add, ALU.add),
                   reads=[po, boc, xs[mo]], writes=[xs[mo]], psum_src=True)
        emit_ffn(s, C, xs, w2_h)
        for kb in range(DBLK):
            s.dma(xo_h[kb * P:(kb + 1) * P, sl], xs[kb].ap, reads=[xs[kb]], writes=[xo_d])

    s.emit("sync", lambda e: e.nop(), deps=[("sync", s.val["sync"])])
    s.finalize()
    return nc


def _col(b):
    n = b.shape[0] // P
    return np.ascontiguousarray(b.reshape(n, P).T)


def kernel(**inputs):
    xf = np.asarray(inputs["L0_attn_x"], np.float32).reshape(B * L, D)
    shared = dict(
        w1=np.ascontiguousarray(np.asarray(inputs["L0_ffn_w1"], np.float32)),
        b1=_col(np.asarray(inputs["L0_ffn_b1"], np.float32)),
        w2=np.ascontiguousarray(np.asarray(inputs["L0_ffn_w2"], np.float32)),
        b2=_col(np.asarray(inputs["L0_ffn_b2"], np.float32)),
        wz=np.ascontiguousarray(np.asarray(inputs["L0_attn_Wz"], np.float32)),
        wx=np.ascontiguousarray(np.asarray(inputs["L0_attn_Wx"], np.float32)),
        wdt=np.ascontiguousarray(np.asarray(inputs["L0_attn_Wdt"], np.float32)),
        bz=_col(np.asarray(inputs["L0_attn_bz"], np.float32)),
        bx=_col(np.asarray(inputs["L0_attn_bx"], np.float32)),
        bdt=_col(np.asarray(inputs["L0_attn_bdt"], np.float32)),
        wb=np.ascontiguousarray(np.asarray(inputs["L0_attn_WB"], np.float32)),
        wc=np.ascontiguousarray(np.asarray(inputs["L0_attn_WC"], np.float32)),
        bb=np.ascontiguousarray(np.asarray(inputs["L0_attn_bB"], np.float32).reshape(NSTATE, 1)),
        bc=np.ascontiguousarray(np.asarray(inputs["L0_attn_bC"], np.float32).reshape(NSTATE, 1)),
        wo=np.ascontiguousarray(np.asarray(inputs["L0_attn_Wout"], np.float32)),
        bo=_col(np.asarray(inputs["L0_attn_bout"], np.float32)),
    )
    alog = np.asarray(inputs["L0_attn_A_log"], np.float32)
    core_ids = list(range(NCORES))
    nc = build_fused(core_ids)
    maps = []
    for i in range(NCORES):
        m = dict(shared)
        m["x"] = np.ascontiguousarray(xf[i * TC:(i + 1) * TC, :].T)
        m["alog"] = np.ascontiguousarray(alog)
        sel = np.zeros((P, NCORES), np.float32)
        for j in range(NCORES):
            if j // 4 == i // 4 and j % 4 < i % 4:
                sel[:, j] = 1.0
        m["selm"] = sel
        maps.append(m)
    res = run_bass_kernel_spmd(nc, maps, core_ids=core_ids)
    out = np.empty((B * L, D), np.float32)
    for i in range(NCORES):
        out[i * TC:(i + 1) * TC, :] = np.asarray(res.results[i]["xo"], np.float32).T
    return out.reshape(B, L, D)
